# version 03
# baseline (speedup 1.0000x reference)
import numpy as np
from contextlib import ExitStack

import concourse.bass as bass
import concourse.mybir as mybir
from concourse.bass import broadcast_tensor_aps
from concourse.bass_utils import run_bass_kernel_spmd

F32 = mybir.dt.float32
BF16 = mybir.dt.bfloat16
AF = mybir.ActivationFunctionType
ALU = mybir.AluOpType
AX = mybir.AxisListType.X

ENGS = ("sync", "tensor", "vector", "scalar", "gpsimd")

B, L, D = 4, 4096, 1024
H, HD = 16, 64
NCORE = 8
NOWN = 2048
CW = 512


class SemC:

    def __init__(self, h):
        self.h = h
        self.n = 0


class Builder:
    def __init__(self, nc):
        self.nc = nc
        self.es = ExitStack()
        self.sched = {e: [] for e in ENGS}
        self.seen = {e: {} for e in ENGS}
        self.sems = {}

    def sem(self, name):
        s = SemC(self.es.enter_context(self.nc.semaphore(name)))
        self.sems[name] = s
        return s

    strict = False

    def _emit(self, eng, fn, waits, sem, inc):
        tgt = None
        if sem is not None:
            sem.n += inc
            tgt = sem.n
        wl = []
        for ws, wv in waits:
            if wv is None:
                wv = ws.n
            if self.strict or self.seen[eng].get(ws, -1) < wv:
                wl.append((ws, wv))
                self.seen[eng][ws] = max(self.seen[eng].get(ws, -1), wv)
        if sem is not None:
            self.seen[eng][sem] = max(self.seen[eng].get(sem, -1), tgt)

        def f2(e, fn=fn, wl=tuple(wl), sh=(sem.h if sem else None), inc=inc):
            for ws, wv in wl:
                e.wait_ge(ws.h, wv)
            inst = fn(e)
            if sh is not None and inst is not None:
                inst.then_inc(sh, inc)
            return inst

        self.sched[eng].append(f2)
        return tgt

    def dma(self, eng, out, in_, sem, waits=(), allow_slow=False):
        def fn(e, o=out, i=in_, asl=allow_slow):
            return e.dma_start(out=o, in_=i, allow_slow_non_contiguous=asl)
        return self._emit(eng, fn, waits, sem, 16)

    def mm(self, lhsT, rhs, out, start, stop, waits=(), sem=None, inc=1,
           tile_position=None, skip_group_check=False):
        def fn(e, o=out, l=lhsT, r=rhs, st=start, sp=stop, tp=tile_position,
               sg=skip_group_check):
            return e.matmul(o, l, r, start=st, stop=sp, tile_position=tp,
                            skip_group_check=sg)
        return self._emit("tensor", fn, waits, sem, inc)

    def trans(self, out, in_, iden, waits=(), sem=None, inc=1):
        def fn(e, o=out, i=in_, d=iden):
            return e.transpose(o, i, d)
        return self._emit("tensor", fn, waits, sem, inc)

    def act(self, eng, out, in_, func, bias=0.0, scale=1.0, accum_out=None,
            waits=(), sem=None, inc=1):
        def fn(e, o=out, i=in_, f=func, b=bias, s=scale, a=accum_out):
            return e.activation(o, i, f, bias=b, scale=s, accum_out=a)
        return self._emit(eng, fn, waits, sem, inc)

    def tt(self, eng, out, in0, in1, op, waits=(), sem=None, inc=1):
        def fn(e, o=out, a=in0, b=in1, op=op):
            return e.tensor_tensor(o, a, b, op)
        return self._emit(eng, fn, waits, sem, inc)

    def ts(self, eng, out, in0, s1, s2, op0, op1=ALU.bypass, waits=(), sem=None, inc=1):
        def fn(e, o=out, i=in0, s1=s1, s2=s2, op0=op0, op1=op1):
            return e.tensor_scalar(o, i, s1, s2, op0, op1)
        return self._emit(eng, fn, waits, sem, inc)

    def stt(self, eng, out, in0, scalar, in1, op0, op1, waits=(), sem=None, inc=1):
        def fn(e, o=out, a=in0, s=scalar, b=in1, op0=op0, op1=op1):
            return e.scalar_tensor_tensor(o, a, s, b, op0, op1)
        return self._emit(eng, fn, waits, sem, inc)

    def copy(self, eng, out, in_, waits=(), sem=None, inc=1):
        def fn(e, o=out, i=in_):
            return e.tensor_copy(o, i)
        return self._emit(eng, fn, waits, sem, inc)

    def memset(self, eng, ap, val, waits=(), sem=None, inc=1):
        def fn(e, a=ap, v=val):
            return e.memset(a, v)
        return self._emit(eng, fn, waits, sem, inc)

    def red(self, eng, out, in_, op, negate=False, waits=(), sem=None, inc=1):
        def fn(e, o=out, i=in_, op=op, ng=negate):
            return e.tensor_reduce(o, i, AX, op, negate=ng)
        return self._emit(eng, fn, waits, sem, inc)

    def recip(self, out, in_, waits=(), sem=None, inc=1):
        def fn(e, o=out, i=in_):
            return e.reciprocal(o, i)
        return self._emit("vector", fn, waits, sem, inc)

    def wait(self, eng, waits):
        self._emit(eng, lambda e: None, waits, None, 0)

    def barrier(self, engs=ENGS):
        waits = [(s, s.n) for s in self.sems.values() if s.n > 0]
        for e in engs:
            self.wait(e, waits)

    def finalize(self):
        nc = self.nc
        with nc.Block() as block:
            for eng in ENGS:
                fns = self.sched[eng]
                if not fns:
                    continue
                getattr(block, eng)(lambda e, fns=fns: [f(e) for f in fns])
        self.es.close()



def _rope_tables():
    inv_freq = (1.0 / (np.float32(10000.0) **
                       (np.arange(0, HD, 2, dtype=np.float32) / np.float32(HD)))).astype(np.float32)
    t = np.arange(L, dtype=np.float32)
    freqs = t[:, None] * inv_freq[None, :]
    emb = np.concatenate([freqs, freqs], axis=-1)
    return np.cos(emb).astype(np.float32), np.sin(emb).astype(np.float32)


def _fmT(w):
    wt = np.ascontiguousarray(w.T.astype(np.float32, copy=False))
    return wt.reshape(wt.shape[0] // 128, 128, wt.shape[1])


def _cols(v):
    return np.ascontiguousarray(v.reshape(-1, 128).T.astype(np.float32, copy=False))


def host_prep(inp):
    hp = {}
    x = inp["L0_attn_x"]
    xT = np.ascontiguousarray(x.transpose(0, 2, 1))
    xTc = []
    for c in range(NCORE):
        bi, hf = c // 2, c % 2
        xr = np.concatenate([xT[bi][:, hf * NOWN:], xT[bi][:, :hf * NOWN]], axis=1)
        xTc.append(np.ascontiguousarray(xr).reshape(8, 128, L))
    hp["xT_core"] = xTc
    cos, sin = _rope_tables()
    hp["cosT_core"] = [np.ascontiguousarray(np.concatenate(
        [cos[(c % 2) * NOWN:], cos[:(c % 2) * NOWN]], axis=0)) for c in range(NCORE)]
    hp["sinT_core"] = [np.ascontiguousarray(np.concatenate(
        [sin[(c % 2) * NOWN:], sin[:(c % 2) * NOWN]], axis=0)) for c in range(NCORE)]
    hp["qwT"] = _fmT(inp["L0_attn_q_w"])
    hp["kwT"] = _fmT(inp["L0_attn_k_w"])
    hp["vwT"] = _fmT(inp["L0_attn_v_w"])
    hp["owT"] = _fmT(inp["L0_attn_o_w"])
    hp["qb_r"] = np.ascontiguousarray(inp["L0_attn_q_b"].reshape(1, D))
    hp["kb_r"] = np.ascontiguousarray(inp["L0_attn_k_b"].reshape(1, D))
    hp["vb_r"] = np.ascontiguousarray(inp["L0_attn_v_b"].reshape(1, D))
    hp["ob_c"] = _cols(inp["L0_attn_o_b"])
    hp["qn_c"] = _cols(inp["L0_attn_qn_scale"])
    hp["kn_c"] = _cols(inp["L0_attn_kn_scale"])
    hp["ones"] = np.ones((128, 128), dtype=np.float32)
    hp["iden"] = np.eye(128, dtype=np.float32)
    cst = np.zeros((128, 8), dtype=np.float32)
    cst[:, 0] = 1e-5
    cst[:, 1] = 1e-8
    cst[:, 2] = 1e-6
    hp["consts"] = cst
    hp["rwT"] = _fmT(inp["L0_ffn_router_weight"])
    hp["rb_r"] = np.ascontiguousarray(inp["L0_ffn_router_bias"].reshape(1, 8))
    gu = inp["L0_ffn_gate_up_proj"]
    hp["guT"] = np.ascontiguousarray(gu.transpose(0, 2, 1)).reshape(8, 8, 128, 2 * 2048)
    hp["gub_c"] = np.ascontiguousarray(
        inp["L0_ffn_gate_up_proj_bias"].reshape(8, 32, 128).transpose(0, 2, 1))
    dn = inp["L0_ffn_down_proj"]
    hp["dnT"] = np.ascontiguousarray(dn.transpose(0, 2, 1)).reshape(8, 16, 128, 1024)
    hp["dnb"] = np.ascontiguousarray(inp["L0_ffn_down_proj_bias"])
    hp["inv1702"] = np.full((1, 128), 1.0 / 1.702, dtype=np.float32)
    hp["W1"] = np.ascontiguousarray(inp["L1_attn_W"]).reshape(8, 128, 3 * D)
    hp["w1T"] = _fmT(inp["L1_ffn_w1_w"])
    hp["w2T"] = _fmT(inp["L1_ffn_w2_w"])
    hp["h1T"] = _fmT(inp["L1_ffn_h1_w"])
    hp["h2T"] = _fmT(inp["L1_ffn_h2_w"])
    hp["w3T"] = _fmT(inp["L1_ffn_w3_w"])
    hp["s1_c"] = _cols(inp["L1_ffn_w1_s"])
    hp["s2_c"] = _cols(inp["L1_ffn_w2_s"])
    hp["sh1_c"] = _cols(inp["L1_ffn_h1_s"])
    hp["sh2_c"] = _cols(inp["L1_ffn_h2_s"])
    hp["s3_c"] = _cols(inp["L1_ffn_w3_s"])
    return hp



def build_launch_A(dbg=False):
    nc = bass.Bass("TRN2", target_bir_lowering=False, debug=False)
    b = Builder(nc)
    T = {}

    def din(name, shape, dtype=F32):
        T[name] = nc.dram_tensor(name, shape, dtype, kind="ExternalInput").ap()

    def dout(name, shape, dtype=F32):
        T[name] = nc.dram_tensor(name, shape, dtype, kind="ExternalOutput").ap()

    def dscr(name, shape, dtype):
        kind = "ExternalOutput" if dbg else "Internal"
        T[name] = nc.dram_tensor(name, shape, dtype, kind=kind).ap()

    din("xT", [8, 128, L])
    for w in ("qwT", "kwT", "vwT", "owT"):
        din(w, [8, 128, D])
    din("qb_r", [1, D]); din("kb_r", [1, D]); din("vb_r", [1, D])
    din("ob_c", [128, 8]); din("qn_c", [128, 8]); din("kn_c", [128, 8])
    din("cosT", [L, HD]); din("sinT", [L, HD])
    din("ones", [128, 128]); din("iden", [128, 128]); din("consts", [128, 8])
    din("rwT", [8, 128, 8]); din("rb_r", [1, 8])
    din("guT", [8, 8, 128, 4096]); din("gub_c", [8, 128, 32])
    din("dnT", [8, 16, 128, 1024]); din("dnb", [8, 1024])
    din("inv1702", [1, 128])
    dout("x1", [8, 128, NOWN])
    dscr("q_fm", [8, 128, NOWN], BF16)
    dscr("k_fm", [8, 128, L], BF16)
    dscr("v_tm", [32, 128, H * 65], BF16)
    dscr("xn0", [8, 128, NOWN], BF16)
    if dbg:
        dscr("xp2", [8, 128, NOWN], F32)
        dscr("combT_d", [8, NOWN], F32)
        dscr("comb_d", [128, 128], F32)
        dscr("lg_d", [128, 128], F32)
        dscr("mr_d", [128, 64], F32)
        dscr("at_d", [8, 128, NOWN], BF16)

    LD = b.sem("LD")
    LG = b.sem("LG")
    PE = b.sem("PE")
    VE = b.sem("VE")
    SE = b.sem("SE")
    OUT = b.sem("OUT")

    al = b.es.enter_context

    s_ones_f = al(nc.sbuf_tensor("s_ones_f", [128, 128], F32))
    s_ones_b = al(nc.sbuf_tensor("s_ones_b", [128, 128], BF16))
    s_id_b = al(nc.sbuf_tensor("s_id_b", [128, 128], BF16))
    s_id_f = al(nc.sbuf_tensor("s_id_f", [128, 128], F32))
    s_consts = al(nc.sbuf_tensor("s_consts", [128, 8], F32))
    s_qn = al(nc.sbuf_tensor("s_qn", [128, 8], F32))
    s_kn = al(nc.sbuf_tensor("s_kn", [128, 8], F32))
    s_ob = al(nc.sbuf_tensor("s_ob", [128, 8], F32))
    s_inv1702 = al(nc.sbuf_tensor("s_inv1702", [1, 128], F32))

    b.dma("sync", s_ones_f[:], T["ones"], LD)
    b.dma("sync", s_id_f[:], T["iden"], LD)
    b.dma("sync", s_consts[:], T["consts"], LD)
    b.dma("sync", s_qn[:], T["qn_c"], LD)
    b.dma("sync", s_kn[:], T["kn_c"], LD)
    b.dma("sync", s_ob[:], T["ob_c"], LD)
    b.dma("sync", s_inv1702[:], T["inv1702"], LD)
    b.dma("gpsimd", s_ones_b[:], T["ones"], LG)
    b.dma("gpsimd", s_id_b[:], T["iden"], LG)

    ones1_b = s_ones_b[:, 0:1]
    ones1r_f = s_ones_f[0:1, :]
    eps_ln = s_consts[0:1, 0:1]
    eps_rms = s_consts[0:1, 1:2]

    _phase_l0_pre(nc, b, T, LD, LG, PE, VE, SE, OUT,
                  s_ones_f, s_ones_b, s_id_b, s_qn, s_kn, eps_ln, eps_rms)
    b.barrier()
    x_acc = al(nc.sbuf_tensor("x_acc", [128, 8 * NOWN], F32))
    _phase_l0_attn(nc, b, T, LD, LG, PE, VE, SE, OUT, s_ones_f, s_ob, x_acc)
    b.barrier()
    if dbg:
        b.dma("sync", T["xp2"][:, :, :].rearrange("c p w -> p c w"),
              x_acc[:].rearrange("p (c w) -> p c w", c=8), OUT)
        b.barrier()
    _phase_l0_moe(nc, b, T, LD, LG, PE, VE, SE, OUT,
                  s_ones_f, s_id_f, s_inv1702, x_acc)

    b.wait("sync", [(OUT, OUT.n)])
    b.finalize()
    return nc


def _phase_l0_pre(nc, b, T, LD, LG, PE, VE, SE, OUT,
                  s_ones_f, s_ones_b, s_id_b, s_qn, s_kn, eps_ln, eps_rms):
    es = ExitStack()
    al = es.enter_context
    s_qwT = al(nc.sbuf_tensor("p1_qwT", [128, 8 * D], BF16))
    s_kwT = al(nc.sbuf_tensor("p1_kwT", [128, 8 * D], BF16))
    s_vwT = al(nc.sbuf_tensor("p1_vwT", [128, 8 * D], BF16))
    s_qb_bc = al(nc.sbuf_tensor("p1_qb_bc", [128, D], F32))
    s_kb_bc = al(nc.sbuf_tensor("p1_kb_bc", [128, D], F32))
    s_vb_bc = al(nc.sbuf_tensor("p1_vb_bc", [128, D], F32))
    s_brow = al(nc.sbuf_tensor("p1_brow", [1, D], BF16))
    s_xf = al(nc.sbuf_tensor("p1_xf", [128, 8 * CW], F32))
    s_xb = al(nc.sbuf_tensor("p1_xb", [128, 8 * CW], BF16))
    s_sq = al(nc.sbuf_tensor("p1_sq", [128, 8 * CW], BF16))
    s_xnf = al(nc.sbuf_tensor("p1_xnf", [128, 8 * CW], F32))
    s_xnb = al(nc.sbuf_tensor("p1_xnb", [128, 8 * CW], BF16))
    s_kin = al(nc.sbuf_tensor("p1_kin", [128, 8 * CW], BF16))
    s_qin = al(nc.sbuf_tensor("p1_qin", [128, 8 * CW], BF16))
    s_t = al(nc.sbuf_tensor("p1_t", [128, CW], F32))
    s_rows = al(nc.sbuf_tensor("p1_rows", [1, 3 * CW], F32))
    s_cos = al(nc.sbuf_tensor("p1_cos", [128, HD], F32))
    s_sin = al(nc.sbuf_tensor("p1_sin", [128, HD], F32))
    s_ktm = al(nc.sbuf_tensor("p1_ktm", [128, D], F32))
    s_rot = al(nc.sbuf_tensor("p1_rot", [128, D], F32))
    s_r1 = al(nc.sbuf_tensor("p1_r1", [128, D], F32))
    s_r2 = al(nc.sbuf_tensor("p1_r2", [128, D], F32))
    s_ktb = al(nc.sbuf_tensor("p1_ktb", [128, D], BF16))
    s_ktr = al(nc.sbuf_tensor("p1_ktr", [128, D], BF16))
    s_vst = al(nc.sbuf_tensor("p1_vst", [128, H * 65], BF16))

    p_sA = al(nc.psum_tensor("p1_sA", [1, CW], F32))
    p_sB = al(nc.psum_tensor("p1_sB", [1, CW], F32))
    p_m = al(nc.psum_tensor("p1_m", [128, CW], F32))
    p_r = al(nc.psum_tensor("p1_r", [128, CW], F32))
    p_k = al(nc.psum_tensor("p1_k", [128, 2 * CW], F32))
    p_tr = al(nc.psum_tensor("p1_tr", [128, D], BF16))

    mrow = s_rows[0:1, 0:CW]
    rstd = s_rows[0:1, CW:2 * CW]
    trow = s_rows[0:1, 2 * CW:3 * CW]
    erow = rstd
    varr = trow
    rrms = rstd

    for (wname, sdst) in (("qwT", s_qwT), ("kwT", s_kwT), ("vwT", s_vwT)):
        for dc in range(8):
            b.dma("gpsimd", sdst[:, dc * D:(dc + 1) * D], T[wname][dc], LG)
    for (bname, bdst) in (("qb_r", s_qb_bc), ("kb_r", s_kb_bc), ("vb_r", s_vb_bc)):
        lv = b.dma("gpsimd", s_brow[:], T[bname], LG, waits=[(PE, PE.n)])
        for oc in range(2):
            b.mm(s_ones_b[0:1, :], s_brow[0:1, oc * CW:(oc + 1) * CW],
                 p_m[:, 0:CW], True, True, waits=[(LG, lv), (VE, VE.n)], sem=PE)
            b.copy("vector", bdst[:, oc * CW:(oc + 1) * CW], p_m[:, 0:CW],
                   waits=[(PE, PE.n)], sem=VE)

    for ch in range(8):
        c0 = ch * CW
        own = ch < 4
        lvx = b.dma("sync", s_xf[:].rearrange("p (c w) -> p c w", c=8),
                    T["xT"][:, :, c0:c0 + CW].rearrange("c p w -> p c w"), LD,
                    waits=[(PE, PE.n), (VE, VE.n), (SE, SE.n)])
        lvb = b.dma("gpsimd", s_xb[:].rearrange("p (c w) -> p c w", c=8),
                    T["xT"][:, :, c0:c0 + CW].rearrange("c p w -> p c w"), LG,
                    waits=[(PE, PE.n), (VE, VE.n), (SE, SE.n)])
        sv = b.act("scalar", s_sq[:], s_xb[:], AF.Square, waits=[(LG, lvb)], sem=SE)
        for dc in range(8):
            b.mm(s_ones_b[:, 0:1], s_xb[:, dc * CW:(dc + 1) * CW], p_sA[:],
                 dc == 0, dc == 7, waits=[(LG, lvb), (VE, VE.n)],
                 sem=(PE if dc == 7 else None))
        for dc in range(8):
            b.mm(s_ones_b[:, 0:1], s_sq[:, dc * CW:(dc + 1) * CW], p_sB[:],
                 dc == 0, dc == 7, waits=[(SE, sv), (VE, VE.n)],
                 sem=(PE if dc == 7 else None))
        pe_stats = PE.n
        b.ts("vector", mrow, p_sA[:], 1.0 / D, None, ALU.mult, waits=[(PE, pe_stats)], sem=VE)
        b.ts("vector", erow, p_sB[:], 1.0 / D, None, ALU.mult, waits=[(PE, pe_stats)], sem=VE)
        b.act("scalar", trow, mrow, AF.Square, waits=[(VE, VE.n)], sem=SE)
        b.tt("vector", varr, erow, trow, ALU.subtract, waits=[(SE, SE.n)], sem=VE)
        b.act("scalar", trow, varr, AF.Ln, bias=eps_ln, waits=[(VE, VE.n)], sem=SE)
        b.act("scalar", rstd, trow, AF.Exp, scale=-0.5, waits=[(SE, SE.n)], sem=SE)
        sv_rstd = SE.n
        b.mm(s_ones_f[0:1, :], mrow, p_m[:, 0:CW], True, True,
             waits=[(VE, VE.n)], sem=PE)
        b.mm(s_ones_f[0:1, :], rstd, p_r[:, 0:CW], True, True,
             waits=[(SE, sv_rstd), (VE, VE.n)], sem=PE)
        pe_bc = PE.n
        for dc in range(8):
            sl = slice(dc * CW, (dc + 1) * CW)
            b.tt("vector", s_t[:], s_xf[:, sl], p_m[:, 0:CW], ALU.subtract,
                 waits=[(PE, pe_bc), (LD, lvx)], sem=None)
            b.tt("vector", s_xnf[:, sl], s_t[:], p_r[:, 0:CW], ALU.mult, sem=None)
        b.copy("vector", s_xnb[:], s_xnf[:], waits=[(OUT, OUT.n)], sem=VE)
        v_xn = VE.n
        if own:
            b.dma("sync", T["xn0"][:, :, c0:c0 + CW].rearrange("c p w -> p c w"),
                  s_xnb[:].rearrange("p (c w) -> p c w", c=8), OUT, waits=[(VE, v_xn)])
        sv2 = b.act("scalar", s_sq[:], s_xnb[:], AF.Square, waits=[(VE, v_xn)], sem=SE)
        for dc in range(8):
            b.mm(s_ones_b[:, 0:1], s_sq[:, dc * CW:(dc + 1) * CW], p_sA[:],
                 dc == 0, dc == 7, waits=[(SE, sv2), (VE, VE.n)],
                 sem=(PE if dc == 7 else None))
        b.ts("vector", trow, p_sA[:], 1.0 / D, None, ALU.mult, waits=[(PE, PE.n)], sem=VE)
        b.act("scalar", trow, trow, AF.Ln, bias=eps_rms, waits=[(VE, VE.n)], sem=SE)
        b.act("scalar", rrms, trow, AF.Exp, scale=-0.5, waits=[(SE, SE.n)], sem=SE)
        b.mm(s_ones_f[0:1, :], rrms, p_r[:, 0:CW], True, True,
             waits=[(SE, SE.n), (VE, VE.n)], sem=PE)
        pe_rr = PE.n
        for dc in range(8):
            sl = slice(dc * CW, (dc + 1) * CW)
            b.tt("vector", s_t[:], s_xnf[:, sl], p_r[:, 0:CW], ALU.mult,
                 waits=[(PE, pe_rr)], sem=None)
            b.ts("vector", s_kin[:, sl], s_t[:], s_kn[:, dc:dc + 1], None, ALU.mult,
                 sem=None)
            if own:
                b.ts("vector", s_qin[:, sl], s_t[:], s_qn[:, dc:dc + 1], None, ALU.mult,
                     sem=(VE if dc == 7 else None))
            elif dc == 7:
                b.memset("vector", s_t[0:1, 0:1], 0.0, sem=VE)
        v_in = VE.n

        for nt in range(4):
            tok0 = c0 + nt * 128
            lvc = b.dma("sync", s_cos[:], T["cosT"][tok0:tok0 + 128, :], LD,
                        waits=[(VE, VE.n)])
            lvs = b.dma("sync", s_sin[:], T["sinT"][tok0:tok0 + 128, :], LD,
                        waits=[(VE, VE.n)])

            def proj_rope(s_in, s_w, s_bb, dram, dram_col0):
                for oc in range(2):
                    for dc in range(8):
                        b.mm(s_in[:, dc * CW + nt * 128: dc * CW + nt * 128 + 128],
                             s_w[:, dc * D + oc * CW: dc * D + oc * CW + CW],
                             p_k[:, oc * CW:(oc + 1) * CW], dc == 0, dc == 7,
                             waits=[(VE, v_in), (VE, VE.n)],
                             sem=(PE if dc == 7 else None))
                pe_p = PE.n
                b.tt("vector", s_ktm[:], p_k[:, 0:D], s_bb[:], ALU.add,
                     waits=[(PE, pe_p), (LD, lvs)], sem=VE)
                kp = s_ktm[:].rearrange("p (k two) -> p k two", two=2)
                rp = s_rot[:].rearrange("p (k two) -> p k two", two=2)
                b.ts("vector", rp[:, :, 0:1], kp[:, :, 1:2], -1.0, None, ALU.mult, sem=None)
                b.copy("vector", rp[:, :, 1:2], kp[:, :, 0:1], sem=None)
                kv_h = s_ktm[:].rearrange("p (h d) -> p h d", h=H)
                rv_h = s_rot[:].rearrange("p (h d) -> p h d", h=H)
                cs1, cb1 = broadcast_tensor_aps(kv_h, s_cos[:].rearrange("p (o d) -> p o d", o=1))
                b.tt("vector", s_r1[:].rearrange("p (h d) -> p h d", h=H), cs1, cb1,
                     ALU.mult, sem=None)
                ss1, sb1 = broadcast_tensor_aps(rv_h, s_sin[:].rearrange("p (o d) -> p o d", o=1))
                b.tt("vector", s_r2[:].rearrange("p (h d) -> p h d", h=H), ss1, sb1,
                     ALU.mult, sem=None)
                b.tt("vector", s_ktb[:], s_r1[:], s_r2[:], ALU.add, sem=VE)
                v_rope = VE.n
                for dc in range(8):
                    b.trans(p_tr[:, dc * 128:(dc + 1) * 128],
                            s_ktb[:, dc * 128:(dc + 1) * 128], s_id_b[:],
                            waits=[(VE, v_rope)], sem=(PE if dc == 7 else None))
                b.copy("vector", s_ktr[:], p_tr[:], waits=[(PE, PE.n), (OUT, OUT.n)], sem=VE)
                b.dma("sync", dram[:, :, dram_col0:dram_col0 + 128].rearrange("c p f -> p c f"),
                      s_ktr[:].rearrange("p (c f) -> p c f", c=8), OUT,
                      waits=[(VE, VE.n)])

            proj_rope(s_kin, s_kwT, s_kb_bc, T["k_fm"], tok0)
            if own:
                proj_rope(s_qin, s_qwT, s_qb_bc, T["q_fm"], tok0)

            for oc in range(2):
                for dc in range(8):
                    b.mm(s_xnb[:, dc * CW + nt * 128: dc * CW + nt * 128 + 128],
                         s_vwT[:, dc * D + oc * CW: dc * D + oc * CW + CW],
                         p_k[:, oc * CW:(oc + 1) * CW], dc == 0, dc == 7,
                         waits=[(VE, v_in), (VE, VE.n)],
                         sem=(PE if dc == 7 else None))
            pe_v = PE.n
            b.tt("vector", s_ktm[:], p_k[:, 0:D], s_vb_bc[:], ALU.add,
                 waits=[(PE, pe_v)], sem=VE)
            b.copy("vector",
                   s_vst[:].rearrange("p (h c) -> p h c", c=65)[:, :, 0:64],
                   s_ktm[:].rearrange("p (h d) -> p h d", d=64),
                   waits=[(OUT, OUT.n)], sem=None)
            b.memset("vector", s_vst[:].rearrange("p (h c) -> p h c", c=65)[:, :, 64:65],
                     1.0, sem=VE)
            b.dma("sync", T["v_tm"][ch * 4 + nt], s_vst[:], OUT, waits=[(VE, VE.n)])
    es.close()


def _phase_l0_attn(nc, b, T, LD, LG, PE, VE, SE, OUT, s_ones_f, s_ob, x_acc):
    es = ExitStack()
    al = es.enter_context
    NK = 4
    s_kp = [al(nc.sbuf_tensor(f"p2_kp{i}", [128, L], BF16)) for i in range(2)]
    s_qp = [al(nc.sbuf_tensor(f"p2_qp{i}", [128, NOWN], BF16)) for i in range(2)]
    s_vp = [al(nc.sbuf_tensor(f"p2_vp{i}", [128, 32 * 130], BF16)) for i in range(2)]
    s_w = [al(nc.sbuf_tensor(f"p2_w{i}", [128, CW], BF16)) for i in range(NK)]
    s_srow = al(nc.sbuf_tensor("p2_srow", [128, CW], F32))
    s_rs = al(nc.sbuf_tensor("p2_rs", [1, CW], F32))
    s_rsb = al(nc.sbuf_tensor("p2_rsb", [64, CW], F32))
    s_at = al(nc.sbuf_tensor("p2_at", [128, 8 * NOWN], BF16))
    s_owT = al(nc.sbuf_tensor("p2_owT", [128, 8 * D], BF16))
    s_x0 = al(nc.sbuf_tensor("p2_x0", [128, CW], F32))

    p_s = [al(nc.psum_tensor(f"p2_s{i}", [128, CW], F32)) for i in range(2)]
    p_av = al(nc.psum_tensor("p2_av", [65, CW], F32))
    p_b = al(nc.psum_tensor("p2_b", [64, CW], F32))
    p_o = al(nc.psum_tensor("p2_o", [128, CW], F32))

    for dc in range(8):
        b.dma("gpsimd", s_owT[:, dc * D:(dc + 1) * D], T["owT"][dc], LG)
    lg_ow = LG.n

    sig_done = {}
    av_done = {}

    for hp_i in range(8):
        buf = hp_i % 2
        lk = b.dma("sync", s_kp[buf][:], T["k_fm"][hp_i], LD,
                   waits=[(PE, PE.n), (VE, VE.n)])
        lq = b.dma("sync", s_qp[buf][:], T["q_fm"][hp_i], LD)
        lv = b.dma("sync",
                   s_vp[buf][:].rearrange("p (t c) -> p t c", t=32),
                   T["v_tm"][:, :, 2 * hp_i * 65:(2 * hp_i + 2) * 65].rearrange("t p c -> p t c"), LD)
        for hh in range(2):
            base = 64 * hh
            for qc in range(4):
                q0 = qc * CW
                for kt in range(32):
                    slot = kt % NK
                    ps = p_s[kt % 2]
                    b.mm(s_kp[buf][base:base + 64, kt * 128:(kt + 1) * 128],
                         s_qp[buf][base:base + 64, q0:q0 + CW],
                         ps[:], True, True,
                         waits=[(LD, lv), (SE, sig_done.get(kt % 2, 0))],
                         sem=PE, tile_position=(base, 0))
                    pe_sc = PE.n
                    b.act("scalar", s_w[slot][:], ps[:], AF.Sigmoid, scale=0.125,
                          waits=[(PE, pe_sc), (PE, av_done.get(slot, 0))], sem=SE)
                    sig_done[kt % 2] = SE.n
                    sg = SE.n
                    b.mm(s_vp[buf][:, kt * 130 + hh * 65: kt * 130 + hh * 65 + 65],
                         s_w[slot][:], p_av[:], kt == 0, kt == 31,
                         waits=[(SE, sg)], sem=PE)
                    av_done[slot] = PE.n
                pe_av = PE.n
                b.copy("vector", s_srow[64:65, :], p_av[64:65, :],
                       waits=[(PE, pe_av), (LG, LG.n)], sem=VE)
                b.dma("gpsimd", s_rs[0:1, :], s_srow[64:65, :], LG,
                      waits=[(VE, VE.n)])
                b.ts("vector", s_rs[:], s_rs[:], 1e-8, None, ALU.add,
                     waits=[(LG, LG.n)], sem=VE)
                b.recip(s_rs[:], s_rs[:], sem=VE)
                b.mm(s_ones_f[0:1, 0:64], s_rs[0:1, :], p_b[:], True, True,
                     waits=[(VE, VE.n)], sem=PE)
                b.copy("vector", s_rsb[:], p_b[:], waits=[(PE, PE.n)], sem=VE)
                b.tt("vector",
                     s_at[base:base + 64, hp_i * NOWN + q0: hp_i * NOWN + q0 + CW],
                     p_av[0:64, :], s_rsb[:], ALU.mult, waits=[(VE, VE.n)], sem=VE)
                b.wait("tensor", [(VE, VE.n)])
    v_at = VE.n
    if "at_d" in T:
        b.dma("sync", T["at_d"][:, :, :].rearrange("c p w -> p c w"),
              s_at[:].rearrange("p (c w) -> p c w", c=8), OUT, waits=[(VE, v_at)])

    for oc in range(8):
        for ncx in range(4):
            lx = b.dma("sync", s_x0[:],
                       T["xT"][oc][:, ncx * CW: ncx * CW + CW], LD,
                       waits=[(VE, VE.n)])
            for dc in range(8):
                b.mm(s_owT[:, dc * D + oc * 128: dc * D + oc * 128 + 128],
                     s_at[:, dc * NOWN + ncx * CW: dc * NOWN + ncx * CW + CW],
                     p_o[:], dc == 0, dc == 7,
                     waits=[(VE, v_at), (LG, lg_ow), (VE, VE.n)],
                     sem=(PE if dc == 7 else None))
            b.stt("vector",
                  x_acc[:, oc * NOWN + ncx * CW: oc * NOWN + ncx * CW + CW],
                  p_o[:], s_ob[:, oc:oc + 1], s_x0[:], ALU.add, ALU.add,
                  waits=[(PE, PE.n), (LD, lx)], sem=VE)
    es.close()


def _phase_l0_moe(nc, b, T, LD, LG, PE, VE, SE, OUT,
                  s_ones_f, s_id_f, s_inv1702, x_acc, store_x1=True):
    es = ExitStack()
    al = es.enter_context
    s_xn = al(nc.sbuf_tensor("p3_xn", [128, 8 * NOWN], BF16))
    s_rw = al(nc.sbuf_tensor("p3_rw", [128, 8 * 8], BF16))
    s_rbb = al(nc.sbuf_tensor("p3_rbb", [128, 8], F32))
    s_rbr = al(nc.sbuf_tensor("p3_rbr", [1, 8], F32))
    s_lg = al(nc.sbuf_tensor("p3_lg", [128, 16 * 8], F32))
    s_l2 = al(nc.sbuf_tensor("p3_l2", [128, 2 * 8], F32))
    s_msk = al(nc.sbuf_tensor("p3_msk", [128, 8], F32))
    s_mrow = al(nc.sbuf_tensor("p3_mrow", [128, 4], F32))
    s_mr_d = al(nc.sbuf_tensor("p3_mr_d", [128, 64], F32)) if "mr_d" in T else None
    s_comb = al(nc.sbuf_tensor("p3_comb", [128, 16 * 8], F32))
    s_combT = al(nc.sbuf_tensor("p3_combT", [8, NOWN], F32))
    s_combTb = al(nc.sbuf_tensor("p3_combTb", [8, NOWN], BF16))
    s_cbr = al(nc.sbuf_tensor("p3_cbr", [1, CW], F32))
    s_cbb = al(nc.sbuf_tensor("p3_cbb", [128, NOWN], F32))
    s_dnb = al(nc.sbuf_tensor("p3_dnb", [8, 1024], BF16))
    s_gu = [al(nc.sbuf_tensor(f"p3_gu{i}", [128, 8 * 128], BF16)) for i in range(2)]
    s_gu2 = [al(nc.sbuf_tensor(f"p3_gu2{i}", [128, 8 * 128], BF16)) for i in range(2)]
    s_dn = [al(nc.sbuf_tensor("p3_dn0", [128, 16 * 128], BF16))]
    s_act = al(nc.sbuf_tensor("p3_act", [128, 16 * NOWN], BF16))
    s_sil = al(nc.sbuf_tensor("p3_sil", [128, CW], F32))
    s_gp = al(nc.sbuf_tensor("p3_gp", [128, CW], F32))
    s_up = al(nc.sbuf_tensor("p3_up", [128, CW], F32))
    s_gub = al(nc.sbuf_tensor("p3_gub", [128, 32], F32))

    p_r = al(nc.psum_tensor("p3_r", [128, 8], F32))
    p_cT = al(nc.psum_tensor("p3_cT", [8, CW], F32))
    p_g = [al(nc.psum_tensor(f"p3_g{i}", [128, CW], F32)) for i in range(2)]
    p_d = [al(nc.psum_tensor(f"p3_d{i}", [128, CW], F32)) for i in range(2)]
    p_cb = al(nc.psum_tensor("p3_cb", [128, CW], F32))

    lxn = b.dma("sync", s_xn[:].rearrange("p (c w) -> p c w", c=8),
                T["xn0"][:, :, :].rearrange("c p w -> p c w"), LD)
    for dc in range(8):
        b.dma("gpsimd", s_rw[:, dc * 8:(dc + 1) * 8], T["rwT"][dc], LG)
    b.dma("gpsimd", s_dnb[:], T["dnb"], LG)
    lg0 = LG.n
    lrb = b.dma("sync", s_rbr[:], T["rb_r"], LD)
    b.mm(s_ones_f[0:1, :], s_rbr[0:1, :], p_r[:], True, True,
         waits=[(LD, lrb), (VE, VE.n)], sem=PE)
    b.copy("vector", s_rbb[:], p_r[:], waits=[(PE, PE.n)], sem=VE)

    for nt in range(16):
        for dc in range(8):
            b.mm(s_xn[:, dc * NOWN + nt * 128: dc * NOWN + nt * 128 + 128],
                 s_rw[:, dc * 8:(dc + 1) * 8], p_r[:], dc == 0, dc == 7,
                 waits=[(LD, lxn), (LG, lg0), (VE, VE.n)], sem=(PE if dc == 7 else None))
        lgs = s_lg[:, nt * 8:(nt + 1) * 8]
        b.tt("vector", lgs, p_r[:], s_rbb[:], ALU.add, waits=[(PE, PE.n)], sem=VE)
        la = s_l2[:, 0:8]
        lb = s_l2[:, 8:16]
        b.strict = True
        b.copy("vector", la, lgs, waits=[(VE, VE.n)], sem=VE)
        b.red("vector", s_mrow[:, 0:1], la, ALU.max, negate=True,
              waits=[(VE, VE.n)], sem=VE)
        cur = la
        for it in range(3):
            nxt = lb if cur is la else la
            b.red("vector", s_mrow[:, 1:2], cur, ALU.max, waits=[(VE, VE.n)], sem=VE)
            b.ts("vector", s_msk[:], cur, s_mrow[:, 1:2], None, ALU.is_ge,
                 waits=[(VE, VE.n)], sem=VE)
            b.stt("vector", nxt, s_msk[:], -1e30, cur, ALU.mult, ALU.add,
                  waits=[(VE, VE.n)], sem=VE)
            cur = nxt
        b.red("vector", s_mrow[:, 2:3], cur, ALU.max, waits=[(VE, VE.n)], sem=VE)
        b.ts("vector", s_msk[:], lgs, s_mrow[:, 2:3], None, ALU.is_ge,
             waits=[(VE, VE.n)], sem=VE)
        b.act("scalar", cur, lgs, AF.Exp, bias=s_mrow[:, 0:1], waits=[(VE, VE.n)], sem=SE)
        b.tt("vector", cur, cur, s_msk[:], ALU.mult, waits=[(SE, SE.n), (VE, VE.n)], sem=VE)
        b.red("vector", s_mrow[:, 3:4], cur, ALU.add, waits=[(VE, VE.n)], sem=VE)
        b.recip(s_mrow[:, 2:3], s_mrow[:, 3:4], waits=[(VE, VE.n)], sem=VE)
        b.ts("vector", s_comb[:, nt * 8:(nt + 1) * 8], cur, s_mrow[:, 2:3], None,
             ALU.mult, waits=[(VE, VE.n)], sem=VE)
        b.strict = False
        if "mr_d" in T:
            b.copy("vector", s_mr_d[:, nt * 4:(nt + 1) * 4], s_mrow[:], sem=VE)
    v_comb = VE.n
    if "comb_d" in T:
        b.dma("sync", T["comb_d"][:, :], s_comb[:], OUT, waits=[(VE, v_comb)])
        b.dma("sync", T["lg_d"][:, :], s_lg[:], OUT, waits=[(VE, v_comb)])
        b.dma("sync", T["mr_d"][:, :], s_mr_d[:], OUT, waits=[(VE, v_comb)])
    for grp in range(4):
        for j in range(4):
            nt = grp * 4 + j
            b.trans(p_cT[:, j * 128:(j + 1) * 128], s_comb[:, nt * 8:(nt + 1) * 8],
                    s_id_f[:], waits=[(VE, v_comb), (VE, VE.n)],
                    sem=(PE if j == 3 else None))
        b.copy("vector", s_combT[:, grp * CW:(grp + 1) * CW], p_cT[:],
               waits=[(PE, PE.n)], sem=VE)
    b.copy("vector", s_combTb[:], s_combT[:], sem=VE)
    v_cT = VE.n
    if "combT_d" in T:
        b.dma("sync", T["combT_d"][:, :], s_combT[:], OUT, waits=[(VE, v_cT)])

    for dc in range(8):
        for ncx in range(4):
            b.mm(s_dnb[:, dc * 128:(dc + 1) * 128],
                 s_combTb[:, ncx * CW:(ncx + 1) * CW],
                 p_d[0][:], True, True, waits=[(VE, v_cT), (LG, lg0), (VE, VE.n)], sem=PE)
            b.tt("vector", x_acc[:, dc * NOWN + ncx * CW: dc * NOWN + ncx * CW + CW],
                 x_acc[:, dc * NOWN + ncx * CW: dc * NOWN + ncx * CW + CW],
                 p_d[0][:], ALU.add, waits=[(PE, PE.n)], sem=VE)

    for e in range(8):
        for ncx in range(4):
            lcb = b.dma("gpsimd", s_cbr[:], s_combT[e:e + 1, ncx * CW:(ncx + 1) * CW],
                        LG, waits=[(VE, v_cT), (PE, PE.n)])
            b.mm(s_inv1702[0:1, :], s_cbr[0:1, :], p_cb[:],
                 True, True, waits=[(LG, lcb), (VE, VE.n)], sem=PE)
            b.copy("vector", s_cbb[:, ncx * CW:(ncx + 1) * CW], p_cb[:],
                   waits=[(PE, PE.n)], sem=VE)
        v_cb = VE.n
        lgub = b.dma("sync", s_gub[:], T["gub_c"][e], LD, waits=[(VE, VE.n)])
        for ic in range(16):
            wb = ic % 2
            lwg = b.dma("gpsimd", s_gu[wb][:].rearrange("p (c f) -> p c f", c=8),
                        T["guT"][e][:, :, ic * 128:(ic + 1) * 128].rearrange("c p f -> p c f"), LG,
                        waits=[(PE, PE.n)])
            lwu = b.dma("gpsimd", s_gu2[wb][:].rearrange("p (c f) -> p c f", c=8),
                        T["guT"][e][:, :, 2048 + ic * 128: 2048 + ic * 128 + 128].rearrange("c p f -> p c f"), LG)
            for ncx in range(4):
                pg = p_g[ncx % 2]
                for dc in range(8):
                    b.mm(s_gu[wb][:, dc * 128:(dc + 1) * 128],
                         s_xn[:, dc * NOWN + ncx * CW: dc * NOWN + ncx * CW + CW],
                         pg[:], dc == 0, dc == 7,
                         waits=[(LG, lwg), (VE, VE.n)], sem=(PE if dc == 7 else None))
                b.ts("vector", s_gp[:], pg[:], s_gub[:, ic:ic + 1], 7.0, ALU.add, ALU.min,
                     waits=[(PE, PE.n), (LD, lgub)], sem=VE)
                b.act("scalar", s_sil[:], s_gp[:], AF.Silu, scale=1.702,
                      waits=[(VE, VE.n)], sem=SE)
                for dc in range(8):
                    b.mm(s_gu2[wb][:, dc * 128:(dc + 1) * 128],
                         s_xn[:, dc * NOWN + ncx * CW: dc * NOWN + ncx * CW + CW],
                         pg[:], dc == 0, dc == 7,
                         waits=[(LG, lwu), (VE, VE.n)], sem=(PE if dc == 7 else None))
                b.ts("vector", s_up[:], pg[:], s_gub[:, 16 + ic:17 + ic], -7.0,
                     ALU.add, ALU.max, waits=[(PE, PE.n), (LD, lgub)], sem=None)
                b.ts("vector", s_up[:], s_up[:], 7.0, 1.0, ALU.min, ALU.add, sem=None)
                b.tt("vector", s_up[:], s_up[:], s_cbb[:, ncx * CW:(ncx + 1) * CW],
                     ALU.mult, waits=[(VE, v_cb)], sem=None)
                b.tt("vector", s_act[:, ic * NOWN + ncx * CW: ic * NOWN + ncx * CW + CW],
                     s_up[:], s_sil[:], ALU.mult, waits=[(SE, SE.n)], sem=VE)
        v_act = VE.n
        for dc in range(8):
            wb = 0
            ldn = b.dma("gpsimd", s_dn[wb][:].rearrange("p (c f) -> p c f", c=16),
                        T["dnT"][e][:, :, dc * 128:(dc + 1) * 128].rearrange("c p f -> p c f"), LG,
                        waits=[(PE, PE.n)])
            for ncx in range(4):
                pd = p_d[ncx % 2]
                for ic in range(16):
                    b.mm(s_dn[wb][:, ic * 128:(ic + 1) * 128],
                         s_act[:, ic * NOWN + ncx * CW: ic * NOWN + ncx * CW + CW],
                         pd[:], ic == 0, ic == 15,
                         waits=[(LG, ldn), (VE, v_act), (VE, VE.n)],
                         sem=(PE if ic == 15 else None))
                b.tt("vector", x_acc[:, dc * NOWN + ncx * CW: dc * NOWN + ncx * CW + CW],
                     x_acc[:, dc * NOWN + ncx * CW: dc * NOWN + ncx * CW + CW],
                     pd[:], ALU.add, waits=[(PE, PE.n)], sem=VE)

    if store_x1:
        b.dma("sync", T["x1"][:, :, :].rearrange("c p w -> p c w"),
              x_acc[:].rearrange("p (c w) -> p c w", c=8), OUT,
              waits=[(VE, VE.n)])
    es.close()



def build_launch_B(dbg=False):
    nc = bass.Bass("TRN2", target_bir_lowering=False, debug=False)
    b = Builder(nc)
    T = {}

    def din(name, shape, dtype=F32):
        T[name] = nc.dram_tensor(name, shape, dtype, kind="ExternalInput").ap()

    def dout(name, shape, dtype=F32):
        T[name] = nc.dram_tensor(name, shape, dtype, kind="ExternalOutput").ap()

    def dscr(name, shape, dtype):
        kind = "ExternalOutput" if dbg else "Internal"
        T[name] = nc.dram_tensor(name, shape, dtype, kind=kind).ap()

    din("x1f", [8, 128, L])
    din("W1", [8, 128, 3 * D])
    din("w1T", [8, 128, 4096]); din("w2T", [8, 128, 4096])
    din("h1T", [32, 128, 4096]); din("h2T", [32, 128, 4096])
    din("w3T", [32, 128, 1024])
    din("s1_c", [128, 32]); din("s2_c", [128, 32])
    din("sh1_c", [128, 32]); din("sh2_c", [128, 32]); din("s3_c", [128, 8])
    din("ones", [128, 128]); din("consts", [128, 8])
    dout("x2", [8, 128, NOWN])
    dscr("k1_tm", [32, 128, D], BF16)
    dscr("v1_tm", [32, 128, D], BF16)
    dscr("xacc_d", [8, 128, NOWN], F32)
    if dbg:
        dscr("xn1_d", [8, 128, NOWN], BF16)
        dscr("kv_d", [8, 128, D], BF16)
        dscr("ks_d", [128, 8], F32)
        dscr("xat_d", [8, 128, NOWN], F32)

    LD = b.sem("LD"); LG = b.sem("LG"); PE = b.sem("PE")
    VE = b.sem("VE"); SE = b.sem("SE"); OUT = b.sem("OUT")

    al = b.es.enter_context
    s_ones_f = al(nc.sbuf_tensor("s_ones_f", [128, 128], F32))
    s_ones_b = al(nc.sbuf_tensor("s_ones_b", [128, 128], BF16))
    s_consts = al(nc.sbuf_tensor("s_consts", [128, 8], F32))
    b.dma("sync", s_ones_f[:], T["ones"], LD)
    b.dma("sync", s_consts[:], T["consts"], LD)
    b.dma("gpsimd", s_ones_b[:], T["ones"], LG)
    eps_ln = s_consts[0:1, 0:1]

    s_xn1 = al(nc.sbuf_tensor("b_xn1", [128, 8 * NOWN], BF16))
    es_attn = ExitStack()
    s_q1 = es_attn.enter_context(nc.sbuf_tensor("b_q1", [128, 8 * NOWN], BF16))
    s_ksb = es_attn.enter_context(nc.sbuf_tensor("b_ksb", [128, 8], BF16))
    s_kvb = es_attn.enter_context(nc.sbuf_tensor("b_kvb", [128, 8 * D], BF16))

    _phase_l1_pre(nc, b, T, LD, LG, PE, VE, SE, OUT,
                  s_ones_f, s_ones_b, eps_ln, s_xn1, s_q1, s_ksb)
    b.barrier()
    _phase_l1_kv(nc, b, T, LD, LG, PE, VE, SE, OUT, s_kvb)
    b.barrier()
    if dbg:
        b.dma("sync", T["xn1_d"][:, :, :].rearrange("c p w -> p c w"),
              s_xn1[:].rearrange("p (c w) -> p c w", c=8), OUT)
        b.dma("sync", T["kv_d"][:, :, :].rearrange("c p w -> p c w"),
              s_kvb[:].rearrange("p (c w) -> p c w", c=8), OUT)
        b.barrier()
    _phase_l1_attnfin(nc, b, T, LD, LG, PE, VE, SE, OUT,
                      s_ones_f, s_ones_b, s_q1, s_kvb, s_ksb)
    b.barrier()
    es_attn.close()
    _phase_l1_ffn(nc, b, T, LD, LG, PE, VE, SE, OUT, s_xn1)
    b.wait("sync", [(OUT, OUT.n)])
    b.finalize()
    return nc


def _phase_l1_pre(nc, b, T, LD, LG, PE, VE, SE, OUT,
                  s_ones_f, s_ones_b, eps_ln, s_xn1, s_q1, s_ksb):
    es = ExitStack()
    al = es.enter_context
    s_W1 = al(nc.sbuf_tensor("b1_W1", [128, 8 * 3 * D], BF16))
    s_xf = al(nc.sbuf_tensor("b1_xf", [128, 8 * CW], F32))
    s_xb = al(nc.sbuf_tensor("b1_xb", [128, 8 * CW], BF16))
    s_sq = al(nc.sbuf_tensor("b1_sq", [128, 8 * CW], BF16))
    s_xnf = al(nc.sbuf_tensor("b1_xnf", [128, 8 * CW], F32))
    s_t = al(nc.sbuf_tensor("b1_t", [128, CW], F32))
    s_e = al(nc.sbuf_tensor("b1_e", [128, CW], F32))
    s_rows = al(nc.sbuf_tensor("b1_rows", [1, 3 * CW], F32))
    s_ktm = al(nc.sbuf_tensor("b1_ktm", [128, D], BF16))
    s_vtm = al(nc.sbuf_tensor("b1_vtm", [128, D], BF16))
    s_te = al(nc.sbuf_tensor("b1_te", [128, D], F32))
    s_ee = al(nc.sbuf_tensor("b1_ee", [128, D], F32))
    p_sA = al(nc.psum_tensor("b1_sA", [1, CW], F32))
    p_sB = al(nc.psum_tensor("b1_sB", [1, CW], F32))
    p_m = al(nc.psum_tensor("b1_m", [128, CW], F32))
    p_r = al(nc.psum_tensor("b1_r", [128, CW], F32))
    p_k = al(nc.psum_tensor("b1_k", [128, 2 * CW], F32))
    p_ks = al(nc.psum_tensor("b1_ks", [128, 8], F32))

    mrow = s_rows[0:1, 0:CW]
    rstd = s_rows[0:1, CW:2 * CW]
    trow = s_e[0:1, 0:CW]
    erow = rstd
    varr = trow

    for dc in range(8):
        b.dma("gpsimd", s_W1[:, dc * 3 * D:(dc + 1) * 3 * D], T["W1"][dc], LG)
    lg_w = LG.n
    ks_count = [0]

    for ch in range(8):
        c0 = ch * CW
        own = ch < 4
        lvx = b.dma("sync", s_xf[:].rearrange("p (c w) -> p c w", c=8),
                    T["x1f"][:, :, c0:c0 + CW].rearrange("c p w -> p c w"), LD,
                    waits=[(PE, PE.n), (VE, VE.n), (SE, SE.n)])
        lvb = b.dma("gpsimd", s_xb[:].rearrange("p (c w) -> p c w", c=8),
                    T["x1f"][:, :, c0:c0 + CW].rearrange("c p w -> p c w"), LG,
                    waits=[(PE, PE.n), (VE, VE.n), (SE, SE.n)])
        sv = b.act("scalar", s_sq[:], s_xb[:], AF.Square, waits=[(LG, lvb)], sem=SE)
        for dc in range(8):
            b.mm(s_ones_b[:, 0:1], s_xb[:, dc * CW:(dc + 1) * CW], p_sA[:],
                 dc == 0, dc == 7, waits=[(LG, lvb), (VE, VE.n)],
                 sem=(PE if dc == 7 else None))
        for dc in range(8):
            b.mm(s_ones_b[:, 0:1], s_sq[:, dc * CW:(dc + 1) * CW], p_sB[:],
                 dc == 0, dc == 7, waits=[(SE, sv), (VE, VE.n)],
                 sem=(PE if dc == 7 else None))
        pe_stats = PE.n
        b.ts("vector", mrow, p_sA[:], 1.0 / D, None, ALU.mult, waits=[(PE, pe_stats)], sem=VE)
        b.ts("vector", erow, p_sB[:], 1.0 / D, None, ALU.mult, waits=[(PE, pe_stats)], sem=VE)
        b.act("scalar", trow, mrow, AF.Square, waits=[(VE, VE.n)], sem=SE)
        b.tt("vector", varr, erow, trow, ALU.subtract, waits=[(SE, SE.n)], sem=VE)
        b.act("scalar", trow, varr, AF.Ln, bias=eps_ln, waits=[(VE, VE.n)], sem=SE)
        b.act("scalar", rstd, trow, AF.Exp, scale=-0.5, waits=[(SE, SE.n)], sem=SE)
        b.mm(s_ones_f[0:1, :], mrow, p_m[:, 0:CW], True, True,
             waits=[(VE, VE.n)], sem=PE)
        b.mm(s_ones_f[0:1, :], rstd, p_r[:, 0:CW], True, True,
             waits=[(SE, SE.n), (VE, VE.n)], sem=PE)
        pe_bc = PE.n
        for dc in range(8):
            sl = slice(dc * CW, (dc + 1) * CW)
            b.tt("vector", s_t[:], s_xf[:, sl], p_m[:, 0:CW], ALU.subtract,
                 waits=[(PE, pe_bc), (LD, lvx)], sem=None)
            b.tt("vector", s_xnf[:, sl], s_t[:], p_r[:, 0:CW], ALU.mult,
                 sem=(VE if dc == 7 else None))
        b.copy("vector", s_xb[:], s_xnf[:], sem=VE)
        v_xn = VE.n
        if own:
            b.copy("vector", s_xn1[:].rearrange("p (c w) -> p c w", c=8)[:, :, c0:c0 + CW],
                   s_xb[:].rearrange("p (c w) -> p c w", c=8), sem=VE)
            for oc in range(8):
                for dc in range(8):
                    b.mm(s_W1[:, dc * 3 * D + oc * 128: dc * 3 * D + oc * 128 + 128],
                         s_xb[:, dc * CW:(dc + 1) * CW],
                         p_k[:, 0:CW], dc == 0, dc == 7,
                         waits=[(LG, lg_w), (VE, VE.n)],
                         sem=(PE if dc == 7 else None))
                pe_q = PE.n
                b.ts("vector", s_t[:], p_k[:, 0:CW], 0.0, None, ALU.min,
                     waits=[(PE, pe_q)], sem=VE)
                b.act("scalar", s_e[:], s_t[:], AF.Exp, waits=[(VE, VE.n)], sem=SE)
                b.stt("vector",
                      s_q1[:, oc * NOWN + c0: oc * NOWN + c0 + CW],
                      p_k[:, 0:CW], 1.0, s_e[:], ALU.add, ALU.max,
                      waits=[(SE, SE.n)], sem=VE)
        for nt in range(4):
            for (col0, dst, do_elu) in ((D, s_ktm, True), (2 * D, s_vtm, False)):
                for oc in range(2):
                    for dc in range(8):
                        b.mm(s_xb[:, dc * CW + nt * 128: dc * CW + nt * 128 + 128],
                             s_W1[:, dc * 3 * D + col0 + oc * CW: dc * 3 * D + col0 + oc * CW + CW],
                             p_k[:, oc * CW:(oc + 1) * CW], dc == 0, dc == 7,
                             waits=[(VE, v_xn), (VE, VE.n), (LG, lg_w)],
                             sem=(PE if dc == 7 else None))
                pe_p = PE.n
                if do_elu:
                    for oc2 in range(2):
                        psl = p_k[:, oc2 * CW:(oc2 + 1) * CW]
                        b.ts("vector", s_t[:], psl, 0.0, None, ALU.min,
                             waits=[(PE, pe_p)], sem=VE)
                        b.act("scalar", s_e[:], s_t[:], AF.Exp, waits=[(VE, VE.n)], sem=SE)
                        b.stt("vector", dst[:, oc2 * CW:(oc2 + 1) * CW], psl,
                              1.0, s_e[:], ALU.add, ALU.max,
                              waits=[(SE, SE.n), (OUT, OUT.n)], sem=VE)
                else:
                    b.copy("vector", dst[:], p_k[:, 0:D],
                           waits=[(PE, pe_p), (OUT, OUT.n)], sem=VE)
                v_d = VE.n
                if do_elu:
                    for dc in range(8):
                        first = (ks_count[0] == 0)
                        ks_count[0] += 1
                        last = (ks_count[0] == 256)
                        b.mm(dst[:, dc * 128:(dc + 1) * 128], s_ones_b[:, 0:1],
                             p_ks[:, dc:dc + 1], first, last,
                             waits=[(VE, v_d)], sem=(PE if last else None),
                             skip_group_check=True)
                    b.dma("sync", T["k1_tm"][ch * 4 + nt], dst[:], OUT,
                          waits=[(VE, v_d)])
                else:
                    b.dma("sync", T["v1_tm"][ch * 4 + nt], dst[:], OUT,
                          waits=[(VE, v_d)])
    b.copy("vector", s_ksb[:], p_ks[:], waits=[(PE, PE.n)], sem=VE)
    if "ks_d" in T:
        s_ksf = al(nc.sbuf_tensor("b1_ksf", [128, 8], F32))
        b.copy("vector", s_ksf[:], p_ks[:], sem=VE)
        b.dma("sync", T["ks_d"][:, :], s_ksf[:], OUT, waits=[(VE, VE.n)])
    es.close()


def _phase_l1_kv(nc, b, T, LD, LG, PE, VE, SE, OUT, s_kvb):
    es = ExitStack()
    al = es.enter_context
    s_kt = [al(nc.sbuf_tensor(f"b2_kt{i}", [128, D], BF16)) for i in range(2)]
    s_vt = [al(nc.sbuf_tensor(f"b2_vt{i}", [128, D], BF16)) for i in range(2)]
    p_kv = [al(nc.psum_tensor(f"b2_kv{i}", [128, CW], F32)) for i in range(8)]

    for vc in range(2):
        for lt in range(32):
            bu = lt % 2
            lk = b.dma("sync", s_kt[bu][:], T["k1_tm"][lt], LD,
                       waits=[(PE, PE.n)])
            lv = b.dma("sync", s_vt[bu][:], T["v1_tm"][lt], LD)
            for dc in range(8):
                b.mm(s_kt[bu][:, dc * 128:(dc + 1) * 128],
                     s_vt[bu][:, vc * CW:(vc + 1) * CW],
                     p_kv[dc][:], lt == 0, lt == 31,
                     waits=[(LD, lv), (VE, VE.n)],
                     sem=(PE if dc == 7 else None))
        for dc in range(8):
            b.copy("vector", s_kvb[:, dc * D + vc * CW: dc * D + vc * CW + CW],
                   p_kv[dc][:], waits=[(PE, PE.n)], sem=VE)
    es.close()


def _phase_l1_attnfin(nc, b, T, LD, LG, PE, VE, SE, OUT,
                      s_ones_f, s_ones_b, s_q1, s_kvb, s_ksb):
    es = ExitStack()
    al = es.enter_context
    s_zr = al(nc.sbuf_tensor("b3_zr", [1, 2 * CW], F32))
    s_zb = al(nc.sbuf_tensor("b3_zb", [128, CW], F32))
    s_x1 = al(nc.sbuf_tensor("b3_x1", [128, CW], F32))
    s_xa = al(nc.sbuf_tensor("b3_xa", [128, CW], F32))
    p_z = al(nc.psum_tensor("b3_z", [1, CW], F32))
    p_zb = al(nc.psum_tensor("b3_zbp", [128, CW], F32))
    p_V = al(nc.psum_tensor("b3_V", [128, CW], F32))

    for ncx in range(4):
        n0 = ncx * CW
        for dc in range(8):
            b.mm(s_ksb[:, dc:dc + 1], s_q1[:, dc * NOWN + n0: dc * NOWN + n0 + CW],
                 p_z[:], dc == 0, dc == 7, waits=[(VE, VE.n)],
                 sem=(PE if dc == 7 else None))
        b.strict = True
        b.ts("vector", s_zr[0:1, 0:CW], p_z[:], 1e-6, None, ALU.add,
             waits=[(PE, PE.n), (VE, VE.n)], sem=VE)
        b.recip(s_zr[0:1, CW:2 * CW], s_zr[0:1, 0:CW], waits=[(VE, VE.n)], sem=VE)
        b.strict = False
        b.mm(s_ones_f[0:1, :], s_zr[0:1, CW:2 * CW], p_zb[:], True, True,
             waits=[(VE, VE.n)], sem=PE)
        b.copy("vector", s_zb[:], p_zb[:], waits=[(PE, PE.n)], sem=VE)
        v_zb = VE.n
        for hc in range(8):
            for dc in range(8):
                b.mm(s_kvb[:, dc * D + hc * 128: dc * D + hc * 128 + 128],
                     s_q1[:, dc * NOWN + n0: dc * NOWN + n0 + CW],
                     p_V[:], dc == 0, dc == 7,
                     waits=[(VE, VE.n)], sem=(PE if dc == 7 else None))
            lx = b.dma("sync", s_x1[:], T["x1f"][hc][:, n0:n0 + CW], LD,
                       waits=[(VE, VE.n)])
            b.tt("vector", s_xa[:], p_V[:], s_zb[:], ALU.mult,
                 waits=[(PE, PE.n), (VE, v_zb), (OUT, OUT.n)], sem=None)
            b.tt("vector", s_xa[:], s_xa[:], s_x1[:], ALU.add,
                 waits=[(LD, lx)], sem=VE)
            b.dma("sync", T["xacc_d"][hc][:, n0:n0 + CW], s_xa[:], OUT,
                  waits=[(VE, VE.n)])
            if "xat_d" in T:
                b.dma("sync", T["xat_d"][hc][:, n0:n0 + CW], s_xa[:], OUT)
            b.wait("vector", [(OUT, OUT.n)])
    es.close()


def _phase_l1_ffn(nc, b, T, LD, LG, PE, VE, SE, OUT, s_xn1):
    es = ExitStack()
    al = es.enter_context
    if s_xn1 is None:
        s_xn1 = al(nc.sbuf_tensor("f4_xn1", [128, 8 * NOWN], BF16))
        b.dma("sync", s_xn1[:].rearrange("p (c w) -> p c w", c=8),
              T["xn1_d"][:, :, :].rearrange("c p w -> p c w"), LD,
              waits=[(OUT, OUT.n)])
    s_sc = al(nc.sbuf_tensor("b4_sc", [128, 32 * 4 + 8], F32))
    s_h0 = al(nc.sbuf_tensor("b4_h0", [128, 32 * 1024], BF16))
    s_h1 = al(nc.sbuf_tensor("b4_h1", [128, 32 * 1024], BF16))
    s_w = [al(nc.sbuf_tensor(f"b4_w{i}", [128, 32 * 128], BF16)) for i in range(2)]
    s_t = al(nc.sbuf_tensor("b4_t", [128, CW], F32))
    s_u = al(nc.sbuf_tensor("b4_u", [128, CW], F32))
    s_sil = al(nc.sbuf_tensor("b4_sil", [128, CW], F32))
    s_xa = al(nc.sbuf_tensor("b4_xa", [128, CW], F32))
    s_x2 = al(nc.sbuf_tensor("b4_x2", [128, CW], F32))
    p_a = al(nc.psum_tensor("b4_a", [128, CW], F32))
    p_b2 = al(nc.psum_tensor("b4_b", [128, CW], F32))

    l_sc = b.dma("sync", s_sc[:, 0:32], T["s1_c"], LD)
    b.dma("sync", s_sc[:, 32:64], T["s2_c"], LD)
    b.dma("sync", s_sc[:, 64:96], T["sh1_c"], LD)
    b.dma("sync", s_sc[:, 96:128], T["sh2_c"], LD)
    l_sc = b.dma("sync", s_sc[:, 128:136], T["s3_c"], LD)
    s1c = s_sc[:, 0:32]; s2c = s_sc[:, 32:64]
    sh1c = s_sc[:, 64:96]; sh2c = s_sc[:, 96:128]; s3c = s_sc[:, 128:136]

    for nh in range(2):
        h0 = nh * 1024
        for jc in range(32):
            wb = jc % 2
            lw1 = b.dma("gpsimd", s_w[wb][:, 0:1024].rearrange("p (c f) -> p c f", c=8),
                        T["w1T"][:, :, jc * 128:(jc + 1) * 128].rearrange("c p f -> p c f"),
                        LG, waits=[(PE, PE.n)])
            lw2 = b.dma("gpsimd", s_w[wb][:, 1024:2048].rearrange("p (c f) -> p c f", c=8),
                        T["w2T"][:, :, jc * 128:(jc + 1) * 128].rearrange("c p f -> p c f"),
                        LG)
            for nc2 in range(2):
                nn = h0 + nc2 * CW
                for dc in range(8):
                    b.mm(s_w[wb][:, dc * 128:(dc + 1) * 128],
                         s_xn1[:, dc * NOWN + nn: dc * NOWN + nn + CW],
                         p_a[:], dc == 0, dc == 7,
                         waits=[(LG, lw1), (LD, LD.n), (VE, VE.n)],
                         sem=(PE if dc == 7 else None))
                for dc in range(8):
                    b.mm(s_w[wb][:, 1024 + dc * 128: 1024 + dc * 128 + 128],
                         s_xn1[:, dc * NOWN + nn: dc * NOWN + nn + CW],
                         p_b2[:], dc == 0, dc == 7,
                         waits=[(LG, lw2), (VE, VE.n)], sem=(PE if dc == 7 else None))
                pe_w = PE.n
                b.ts("vector", s_t[:], p_a[:], s1c[:, jc:jc + 1], None, ALU.mult,
                     waits=[(PE, pe_w), (LD, l_sc)], sem=VE)
                b.act("scalar", s_sil[:], s_t[:], AF.Silu, waits=[(VE, VE.n)], sem=SE)
                b.ts("vector", s_u[:], p_b2[:], s2c[:, jc:jc + 1], None, ALU.mult,
                     sem=None)
                b.tt("vector", s_h0[:, jc * 1024 + nc2 * CW: jc * 1024 + nc2 * CW + CW],
                     s_u[:], s_sil[:], ALU.mult, waits=[(SE, SE.n)], sem=VE)
        def hidden_layer(wname, scl, src, dst):
            for oc in range(32):
                wb = oc % 2
                lw = b.dma("gpsimd", s_w[wb][:, 0:4096].rearrange("p (c f) -> p c f", c=32),
                           T[wname][:, :, oc * 128:(oc + 1) * 128].rearrange("c p f -> p c f"),
                           LG, waits=[(PE, PE.n)])
                for nc2 in range(2):
                    for ic in range(32):
                        b.mm(s_w[wb][:, ic * 128:(ic + 1) * 128],
                             src[:, ic * 1024 + nc2 * CW: ic * 1024 + nc2 * CW + CW],
                             p_a[:], ic == 0, ic == 31,
                             waits=[(LG, lw), (VE, VE.n)],
                             sem=(PE if ic == 31 else None))
                    b.ts("vector", s_t[:], p_a[:], scl[:, oc:oc + 1], None, ALU.mult,
                         waits=[(PE, PE.n)], sem=VE)
                    b.act("scalar", dst[:, oc * 1024 + nc2 * CW: oc * 1024 + nc2 * CW + CW],
                          s_t[:], AF.Gelu, waits=[(VE, VE.n)], sem=SE)
        hidden_layer("h1T", sh1c, s_h0, s_h1)
        b.barrier(("tensor", "vector", "scalar", "gpsimd"))
        hidden_layer("h2T", sh2c, s_h1, s_h0)
        b.barrier(("tensor", "vector", "scalar", "gpsimd"))
        for dc in range(8):
            wb = dc % 2
            lw = b.dma("gpsimd", s_w[wb][:, 0:4096].rearrange("p (c f) -> p c f", c=32),
                       T["w3T"][:, :, dc * 128:(dc + 1) * 128].rearrange("c p f -> p c f"),
                       LG, waits=[(PE, PE.n)])
            for nc2 in range(2):
                nn = h0 + nc2 * CW
                for ic in range(32):
                    b.mm(s_w[wb][:, ic * 128:(ic + 1) * 128],
                         s_h0[:, ic * 1024 + nc2 * CW: ic * 1024 + nc2 * CW + CW],
                         p_a[:], ic == 0, ic == 31,
                         waits=[(LG, lw), (VE, VE.n)],
                         sem=(PE if ic == 31 else None))
                lx = b.dma("sync", s_xa[:], T["xacc_d"][dc][:, nn:nn + CW], LD,
                           waits=[(VE, VE.n)])
                b.stt("vector", s_x2[:], p_a[:], s3c[:, dc:dc + 1], s_xa[:],
                      ALU.mult, ALU.add,
                      waits=[(PE, PE.n), (LD, lx), (LD, l_sc)], sem=VE)
                b.dma("sync", T["x2"][dc][:, nn:nn + CW], s_x2[:], OUT,
                      waits=[(VE, VE.n)])
                b.wait("vector", [(OUT, OUT.n)])
    es.close()




def build_fused(dbg=False):
    nc = bass.Bass("TRN2", target_bir_lowering=False, debug=False)
    b = Builder(nc)
    T = {}

    def din(name, shape, dtype=F32):
        T[name] = nc.dram_tensor(name, shape, dtype, kind="ExternalInput").ap()

    def dscr(name, shape, dtype):
        kind = "ExternalOutput" if dbg else "Internal"
        T[name] = nc.dram_tensor(name, shape, dtype, kind=kind).ap()

    din("xT", [8, 128, L])
    for w in ("qwT", "kwT", "vwT", "owT"):
        din(w, [8, 128, D])
    din("qb_r", [1, D]); din("kb_r", [1, D]); din("vb_r", [1, D])
    din("ob_c", [128, 8]); din("qn_c", [128, 8]); din("kn_c", [128, 8])
    din("cosT", [L, HD]); din("sinT", [L, HD])
    din("ones", [128, 128]); din("iden", [128, 128]); din("consts", [128, 8])
    din("rwT", [8, 128, 8]); din("rb_r", [1, 8])
    din("guT", [8, 8, 128, 4096]); din("gub_c", [8, 128, 32])
    din("dnT", [8, 16, 128, 1024]); din("dnb", [8, 1024])
    din("inv1702", [1, 128])
    din("W1", [8, 128, 3 * D])
    din("w1T", [8, 128, 4096]); din("w2T", [8, 128, 4096])
    din("h1T", [32, 128, 4096]); din("h2T", [32, 128, 4096])
    din("w3T", [32, 128, 1024])
    din("s1_c", [128, 32]); din("s2_c", [128, 32])
    din("sh1_c", [128, 32]); din("sh2_c", [128, 32]); din("s3_c", [128, 8])
    T["x2"] = nc.dram_tensor("x2", [8, 128, NOWN], F32, kind="ExternalOutput").ap()
    dscr("q_fm", [8, 128, NOWN], BF16)
    dscr("k_fm", [8, 128, L], BF16)
    dscr("v_tm", [32, 128, H * 65], BF16)
    dscr("xn0", [8, 128, NOWN], BF16)
    dscr("k1_tm", [16, 128, D], BF16)
    dscr("v1_tm", [16, 128, D], BF16)
    dscr("xacc_d", [8, 128, NOWN], F32)
    dscr("xn1_d", [8, 128, NOWN], BF16)
    if dbg:
        dscr("kvp_d", [8, 128, D], F32)
        dscr("kvf_d", [8, 128, D], BF16)
        dscr("ksf_d", [128, 8], BF16)
    cc_in_t = nc.dram_tensor("cc_in", [1024, 1032], F32, kind="Internal")
    cc_out_t = nc.dram_tensor("cc_out", [1024, 1032], F32, kind="Internal")
    T["cc_in"] = cc_in_t.ap()
    T["cc_out"] = cc_out_t.ap()
    T["cc_in_opt"] = cc_in_t.ap().opt()
    T["cc_out_opt"] = cc_out_t.ap().opt()

    LD = b.sem("LD"); LG = b.sem("LG"); PE = b.sem("PE")
    VE = b.sem("VE"); SE = b.sem("SE"); OUT = b.sem("OUT"); CC = b.sem("CC")

    al = b.es.enter_context
    s_ones_f = al(nc.sbuf_tensor("s_ones_f", [128, 128], F32))
    s_ones_b = al(nc.sbuf_tensor("s_ones_b", [128, 128], BF16))
    s_id_b = al(nc.sbuf_tensor("s_id_b", [128, 128], BF16))
    s_id_f = al(nc.sbuf_tensor("s_id_f", [128, 128], F32))
    s_consts = al(nc.sbuf_tensor("s_consts", [128, 8], F32))
    s_qn = al(nc.sbuf_tensor("s_qn", [128, 8], F32))
    s_kn = al(nc.sbuf_tensor("s_kn", [128, 8], F32))
    s_ob = al(nc.sbuf_tensor("s_ob", [128, 8], F32))
    s_inv1702 = al(nc.sbuf_tensor("s_inv1702", [1, 128], F32))

    b.dma("sync", s_ones_f[:], T["ones"], LD)
    b.dma("sync", s_id_f[:], T["iden"], LD)
    b.dma("sync", s_consts[:], T["consts"], LD)
    b.dma("sync", s_qn[:], T["qn_c"], LD)
    b.dma("sync", s_kn[:], T["kn_c"], LD)
    b.dma("sync", s_ob[:], T["ob_c"], LD)
    b.dma("sync", s_inv1702[:], T["inv1702"], LD)
    b.dma("gpsimd", s_ones_b[:], T["ones"], LG)
    b.dma("gpsimd", s_id_b[:], T["iden"], LG)

    eps_ln = s_consts[0:1, 0:1]
    eps_rms = s_consts[0:1, 1:2]

    _phase_l0_pre(nc, b, T, LD, LG, PE, VE, SE, OUT,
                  s_ones_f, s_ones_b, s_id_b, s_qn, s_kn, eps_ln, eps_rms)
    b.barrier()
    es_x = ExitStack()
    x_acc = es_x.enter_context(nc.sbuf_tensor("x_acc", [128, 8 * NOWN], F32))
    _phase_l0_attn(nc, b, T, LD, LG, PE, VE, SE, OUT, s_ones_f, s_ob, x_acc)
    b.barrier()
    _phase_l0_moe(nc, b, T, LD, LG, PE, VE, SE, OUT,
                  s_ones_f, s_id_f, s_inv1702, x_acc, store_x1=False)
    b.barrier()

    es_q = ExitStack()
    s_q1 = es_q.enter_context(nc.sbuf_tensor("b_q1", [128, 8 * NOWN], BF16))

    _phase_l1_pre_f(nc, b, T, LD, LG, PE, VE, SE, OUT, CC,
                    s_ones_f, s_ones_b, eps_ln, None, s_q1, x_acc)
    b.barrier()
    es_attn = ExitStack()
    s_ksb = es_attn.enter_context(nc.sbuf_tensor("b_ksb", [128, 8], BF16))
    s_kvb = es_attn.enter_context(nc.sbuf_tensor("b_kvb", [128, 8 * D], BF16))
    _phase_l1_kv_f(nc, b, T, LD, LG, PE, VE, SE, OUT, CC, s_kvb, s_ksb)
    b.barrier()
    _phase_l1_attnfin_f(nc, b, T, LD, LG, PE, VE, SE, OUT,
                        s_ones_f, s_q1, s_kvb, s_ksb, x_acc)
    b.barrier()
    es_attn.close()
    es_q.close()
    es_x.close()
    _phase_l1_ffn(nc, b, T, LD, LG, PE, VE, SE, OUT, None)
    b.wait("sync", [(OUT, OUT.n)])
    b.finalize()
    return nc


def _phase_l1_pre_f(nc, b, T, LD, LG, PE, VE, SE, OUT, CC,
                    s_ones_f, s_ones_b, eps_ln, s_xn1, s_q1, x_acc):
    es = ExitStack()
    al = es.enter_context
    if s_xn1 is None:
        s_xn1 = al(nc.sbuf_tensor("f1_xn1", [128, 8 * NOWN], BF16))
        spill_xn1 = True
    else:
        spill_xn1 = False
    s_W1 = al(nc.sbuf_tensor("f1_W1", [128, 8 * 3 * D], BF16))
    s_xb = al(nc.sbuf_tensor("f1_xb", [128, 8 * CW], BF16))
    s_sq = al(nc.sbuf_tensor("f1_sq", [128, 8 * CW], BF16))
    s_t = al(nc.sbuf_tensor("f1_t", [128, CW], F32))
    s_e = al(nc.sbuf_tensor("f1_e", [128, CW], F32))
    s_rows = al(nc.sbuf_tensor("f1_rows", [1, 2 * CW], F32))
    s_ktm = al(nc.sbuf_tensor("f1_ktm", [128, D], BF16))
    s_vtm = al(nc.sbuf_tensor("f1_vtm", [128, D], BF16))
    s_ksf = al(nc.sbuf_tensor("f1_ksf", [128, 8], F32))
    p_sA = al(nc.psum_tensor("f1_sA", [1, CW], F32))
    p_sB = al(nc.psum_tensor("f1_sB", [1, CW], F32))
    p_m = al(nc.psum_tensor("f1_m", [128, CW], F32))
    p_r = al(nc.psum_tensor("f1_r", [128, CW], F32))
    p_k = al(nc.psum_tensor("f1_k", [128, 2 * CW], F32))
    p_ks = al(nc.psum_tensor("f1_ks", [128, 8], F32))

    mrow = s_rows[0:1, 0:CW]
    rstd = s_rows[0:1, CW:2 * CW]
    trow = s_e[0:1, 0:CW]
    erow = rstd
    varr = trow

    for dc in range(8):
        b.dma("gpsimd", s_W1[:, dc * 3 * D:(dc + 1) * 3 * D], T["W1"][dc], LG)
    lg_w = LG.n
    ks_count = [0]

    for ch in range(4):
        c0 = ch * CW
        xsl = x_acc[:].rearrange("p (c n) -> p c n", c=8)[:, :, c0:c0 + CW]
        b.copy("vector", s_xb[:].rearrange("p (c w) -> p c w", c=8), xsl,
               waits=[(PE, PE.n), (SE, SE.n)], sem=VE)
        lvb = VE.n
        sv = b.act("scalar", s_sq[:], s_xb[:], AF.Square, waits=[(VE, lvb)], sem=SE)
        for dc in range(8):
            b.mm(s_ones_b[:, 0:1], s_xb[:, dc * CW:(dc + 1) * CW], p_sA[:],
                 dc == 0, dc == 7, waits=[(VE, lvb), (VE, VE.n)],
                 sem=(PE if dc == 7 else None))
        for dc in range(8):
            b.mm(s_ones_b[:, 0:1], s_sq[:, dc * CW:(dc + 1) * CW], p_sB[:],
                 dc == 0, dc == 7, waits=[(SE, sv), (VE, VE.n)],
                 sem=(PE if dc == 7 else None))
        pe_stats = PE.n
        b.ts("vector", mrow, p_sA[:], 1.0 / D, None, ALU.mult, waits=[(PE, pe_stats)], sem=VE)
        b.ts("vector", erow, p_sB[:], 1.0 / D, None, ALU.mult, waits=[(PE, pe_stats)], sem=VE)
        b.act("scalar", trow, mrow, AF.Square, waits=[(VE, VE.n)], sem=SE)
        b.tt("vector", varr, erow, trow, ALU.subtract, waits=[(SE, SE.n)], sem=VE)
        b.act("scalar", trow, varr, AF.Ln, bias=eps_ln, waits=[(VE, VE.n)], sem=SE)
        b.act("scalar", rstd, trow, AF.Exp, scale=-0.5, waits=[(SE, SE.n)], sem=SE)
        b.mm(s_ones_f[0:1, :], mrow, p_m[:, 0:CW], True, True,
             waits=[(VE, VE.n)], sem=PE)
        b.mm(s_ones_f[0:1, :], rstd, p_r[:, 0:CW], True, True,
             waits=[(SE, SE.n), (VE, VE.n)], sem=PE)
        pe_bc = PE.n
        for dc in range(8):
            sl = slice(dc * CW, (dc + 1) * CW)
            b.tt("vector", s_t[:], xsl[:, dc, :], p_m[:, 0:CW], ALU.subtract,
                 waits=[(PE, pe_bc), (PE, pe_stats), (SE, sv)], sem=None)
            b.tt("vector", s_xb[:, sl], s_t[:], p_r[:, 0:CW], ALU.mult,
                 sem=(VE if dc == 7 else None))
        v_xn = VE.n
        b.copy("vector", s_xn1[:].rearrange("p (c w) -> p c w", c=8)[:, :, c0:c0 + CW],
               s_xb[:].rearrange("p (c w) -> p c w", c=8), sem=VE)
        for oc in range(8):
            for dc in range(8):
                b.mm(s_W1[:, dc * 3 * D + oc * 128: dc * 3 * D + oc * 128 + 128],
                     s_xb[:, dc * CW:(dc + 1) * CW],
                     p_k[:, 0:CW], dc == 0, dc == 7,
                     waits=[(LG, lg_w), (VE, VE.n)],
                     sem=(PE if dc == 7 else None))
            pe_q = PE.n
            b.ts("vector", s_t[:], p_k[:, 0:CW], 0.0, None, ALU.min,
                 waits=[(PE, pe_q)], sem=VE)
            b.act("scalar", s_e[:], s_t[:], AF.Exp, waits=[(VE, VE.n)], sem=SE)
            b.stt("vector",
                  s_q1[:, oc * NOWN + c0: oc * NOWN + c0 + CW],
                  p_k[:, 0:CW], 1.0, s_e[:], ALU.add, ALU.max,
                  waits=[(SE, SE.n)], sem=VE)
        for nt in range(4):
            for (col0, dst, do_elu) in ((D, s_ktm, True), (2 * D, s_vtm, False)):
                for oc in range(2):
                    for dc in range(8):
                        b.mm(s_xb[:, dc * CW + nt * 128: dc * CW + nt * 128 + 128],
                             s_W1[:, dc * 3 * D + col0 + oc * CW: dc * 3 * D + col0 + oc * CW + CW],
                             p_k[:, oc * CW:(oc + 1) * CW], dc == 0, dc == 7,
                             waits=[(VE, v_xn), (VE, VE.n), (LG, lg_w)],
                             sem=(PE if dc == 7 else None))
                pe_p = PE.n
                if do_elu:
                    for oc2 in range(2):
                        psl = p_k[:, oc2 * CW:(oc2 + 1) * CW]
                        b.ts("vector", s_t[:], psl, 0.0, None, ALU.min,
                             waits=[(PE, pe_p)], sem=VE)
                        b.act("scalar", s_e[:], s_t[:], AF.Exp, waits=[(VE, VE.n)], sem=SE)
                        b.stt("vector", dst[:, oc2 * CW:(oc2 + 1) * CW], psl,
                              1.0, s_e[:], ALU.add, ALU.max,
                              waits=[(SE, SE.n), (OUT, OUT.n)], sem=VE)
                else:
                    b.copy("vector", dst[:], p_k[:, 0:D],
                           waits=[(PE, pe_p), (OUT, OUT.n)], sem=VE)
                v_d = VE.n
                if do_elu:
                    for dc in range(8):
                        first = (ks_count[0] == 0)
                        ks_count[0] += 1
                        last = (ks_count[0] == 128)
                        b.mm(dst[:, dc * 128:(dc + 1) * 128], s_ones_b[:, 0:1],
                             p_ks[:, dc:dc + 1], first, last,
                             waits=[(VE, v_d)], sem=(PE if last else None),
                             skip_group_check=True)
                    b.dma("sync", T["k1_tm"][ch * 4 + nt], dst[:], OUT,
                          waits=[(VE, v_d)])
                else:
                    b.dma("sync", T["v1_tm"][ch * 4 + nt], dst[:], OUT,
                          waits=[(VE, v_d)])
    if spill_xn1:
        b.dma("sync", T["xn1_d"][:, :, :].rearrange("c p w -> p c w"),
              s_xn1[:].rearrange("p (c w) -> p c w", c=8), OUT, waits=[(VE, VE.n)])
    b.copy("vector", s_ksf[:], p_ks[:], waits=[(PE, PE.n)], sem=VE)
    b.dma("sync",
          T["cc_in"][:, 1024:1025].rearrange("(c p) o -> p c o", p=128),
          s_ksf[:].rearrange("p (c o) -> p c o", o=1), OUT,
          waits=[(VE, VE.n)], allow_slow=True)
    es.close()


def _phase_l1_kv_f(nc, b, T, LD, LG, PE, VE, SE, OUT, CC, s_kvb, s_ksb):
    es = ExitStack()
    al = es.enter_context
    s_kt = [al(nc.sbuf_tensor(f"f2_kt{i}", [128, D], BF16)) for i in range(2)]
    s_vt = [al(nc.sbuf_tensor(f"f2_vt{i}", [128, D], BF16)) for i in range(2)]
    s_kv = al(nc.sbuf_tensor("f2_kv", [128, 8 * D], F32))
    p_kv = [al(nc.psum_tensor(f"f2_kv{i}", [128, CW], F32)) for i in range(8)]

    for vc in range(2):
        for lt in range(16):
            bu = lt % 2
            lk = b.dma("sync", s_kt[bu][:], T["k1_tm"][lt], LD,
                       waits=[(PE, PE.n)])
            lv = b.dma("sync", s_vt[bu][:], T["v1_tm"][lt], LD)
            for dc in range(8):
                b.mm(s_kt[bu][:, dc * 128:(dc + 1) * 128],
                     s_vt[bu][:, vc * CW:(vc + 1) * CW],
                     p_kv[dc][:], lt == 0, lt == 15,
                     waits=[(LD, lv), (VE, VE.n)],
                     sem=(PE if dc == 7 else None))
        for dc in range(8):
            b.copy("vector", s_kv[:, dc * D + vc * CW: dc * D + vc * CW + CW],
                   p_kv[dc][:], waits=[(PE, PE.n)], sem=VE)
    b.dma("sync", T["cc_in"][:, 0:1024].rearrange("(c p) h -> p c h", p=128),
          s_kv[:].rearrange("p (c h) -> p c h", c=8), OUT, waits=[(VE, VE.n)])
    if "kvp_d" in T:
        b.dma("sync", T["kvp_d"][:, :, :].rearrange("c p h -> p c h"),
              s_kv[:].rearrange("p (c h) -> p c h", c=8), OUT, waits=[(VE, VE.n)])
    def cc_fn(e):
        return e.collective_compute(
            "AllReduce", ALU.add,
            replica_groups=[[0, 1], [2, 3], [4, 5], [6, 7]],
            ins=[T["cc_in_opt"]], outs=[T["cc_out_opt"]],
        )
    b._emit("gpsimd", cc_fn, [(OUT, OUT.n)], CC, 1)
    b.seen["gpsimd"].pop(CC, None)
    b.dma("gpsimd", s_kvb[:].rearrange("p (c h) -> p c h", c=8),
          T["cc_out"][:, 0:1024].rearrange("(c p) h -> p c h", p=128), LG,
          waits=[(CC, CC.n)])
    b.dma("gpsimd", s_ksb[:].rearrange("p (c o) -> p c o", o=1),
          T["cc_out"][:, 1024:1025].rearrange("(c p) o -> p c o", p=128), LG,
          allow_slow=True)
    if "kvf_d" in T:
        b.dma("sync", T["kvf_d"][:, :, :].rearrange("c p h -> p c h"),
              s_kvb[:].rearrange("p (c h) -> p c h", c=8), OUT, waits=[(LG, LG.n)])
        b.dma("sync", T["ksf_d"][:, :], s_ksb[:], OUT, waits=[(LG, LG.n)])
    es.close()


def _phase_l1_attnfin_f(nc, b, T, LD, LG, PE, VE, SE, OUT,
                        s_ones_f, s_q1, s_kvb, s_ksb, x_acc):
    es = ExitStack()
    al = es.enter_context
    s_zr = al(nc.sbuf_tensor("f3_zr", [1, 2 * CW], F32))
    s_zb = al(nc.sbuf_tensor("f3_zb", [128, CW], F32))
    s_xa = al(nc.sbuf_tensor("f3_xa", [128, CW], F32))
    p_z = al(nc.psum_tensor("f3_z", [1, CW], F32))
    p_zb = al(nc.psum_tensor("f3_zbp", [128, CW], F32))
    p_V = al(nc.psum_tensor("f3_V", [128, CW], F32))

    for ncx in range(4):
        n0 = ncx * CW
        for dc in range(8):
            b.mm(s_ksb[:, dc:dc + 1], s_q1[:, dc * NOWN + n0: dc * NOWN + n0 + CW],
                 p_z[:], dc == 0, dc == 7, waits=[(LG, LG.n), (VE, VE.n)],
                 sem=(PE if dc == 7 else None))
        b.strict = True
        b.ts("vector", s_zr[0:1, 0:CW], p_z[:], 1e-6, None, ALU.add,
             waits=[(PE, PE.n), (VE, VE.n)], sem=VE)
        b.recip(s_zr[0:1, CW:2 * CW], s_zr[0:1, 0:CW], waits=[(VE, VE.n)], sem=VE)
        b.strict = False
        b.mm(s_ones_f[0:1, :], s_zr[0:1, CW:2 * CW], p_zb[:], True, True,
             waits=[(VE, VE.n)], sem=PE)
        b.copy("vector", s_zb[:], p_zb[:], waits=[(PE, PE.n)], sem=VE)
        v_zb = VE.n
        for hc in range(8):
            for dc in range(8):
                b.mm(s_kvb[:, dc * D + hc * 128: dc * D + hc * 128 + 128],
                     s_q1[:, dc * NOWN + n0: dc * NOWN + n0 + CW],
                     p_V[:], dc == 0, dc == 7,
                     waits=[(VE, VE.n), (LG, LG.n)], sem=(PE if dc == 7 else None))
            b.tt("vector", s_xa[:], p_V[:], s_zb[:], ALU.mult,
                 waits=[(PE, PE.n), (VE, v_zb)], sem=None)
            acc = x_acc[:, hc * NOWN + n0: hc * NOWN + n0 + CW]
            b.tt("vector", acc, acc, s_xa[:], ALU.add, sem=VE)
    b.dma("sync", T["xacc_d"][:, :, :].rearrange("c p w -> p c w"),
          x_acc[:].rearrange("p (c w) -> p c w", c=8), OUT, waits=[(VE, VE.n)])
    es.close()


_NC_A = None
_NC_B = None


def _get_programs():
    global _NC_A, _NC_B
    if _NC_A is None:
        _NC_A = build_launch_A(dbg=False)
    if _NC_B is None:
        _NC_B = build_launch_B(dbg=False)
    return _NC_A, _NC_B


def _in_map_A(hp, core):
    m = {"xT": hp["xT_core"][core],
         "cosT": hp["cosT_core"][core],
         "sinT": hp["sinT_core"][core]}
    for k in ("qwT", "kwT", "vwT", "owT", "qb_r", "kb_r", "vb_r", "ob_c", "qn_c",
              "kn_c", "ones", "iden", "consts", "rwT", "rb_r",
              "guT", "gub_c", "dnT", "dnb", "inv1702"):
        m[k] = hp[k]
    return m


def _in_map_B(hp, core, x1_parts):
    bi, hf = core // 2, core % 2
    m = {"x1f": np.concatenate([x1_parts[core], x1_parts[core ^ 1]], axis=2)}
    for k in ("W1", "w1T", "w2T", "h1T", "h2T", "w3T", "s1_c", "s2_c",
              "sh1_c", "sh2_c", "s3_c", "ones", "consts"):
        m[k] = hp[k]
    return m


_NC_F = None


def _get_fused():
    global _NC_F
    if _NC_F is None:
        _NC_F = build_fused(dbg=False)
    return _NC_F


def _in_map_F(hp, core):
    m = {"xT": hp["xT_core"][core],
         "cosT": hp["cosT_core"][core],
         "sinT": hp["sinT_core"][core]}
    for k in ("qwT", "kwT", "vwT", "owT", "qb_r", "kb_r", "vb_r", "ob_c", "qn_c",
              "kn_c", "ones", "iden", "consts", "rwT", "rb_r",
              "guT", "gub_c", "dnT", "dnb", "inv1702",
              "W1", "w1T", "w2T", "h1T", "h2T", "w3T", "s1_c", "s2_c",
              "sh1_c", "sh2_c", "s3_c"):
        m[k] = hp[k]
    return m


def kernel(**inputs):
    hp = host_prep(inputs)
    nc_f = _get_fused()
    in_maps = [_in_map_F(hp, c) for c in range(NCORE)]
    res = run_bass_kernel_spmd(nc_f, in_maps, core_ids=list(range(NCORE)))
    out = np.empty((B, L, D), dtype=np.float32)
    for c in range(NCORE):
        bi, hf = c // 2, c % 2
        x2 = res.results[c]["x2"].reshape(D, NOWN)
        out[bi, hf * NOWN:(hf + 1) * NOWN, :] = x2.T
    return out


def kernel_unfused(**inputs):
    hp = host_prep(inputs)
    nc_a, nc_b = _get_programs()
    in_maps_a = [_in_map_A(hp, c) for c in range(NCORE)]
    res_a = run_bass_kernel_spmd(nc_a, in_maps_a, core_ids=list(range(NCORE)))
    x1_parts = [np.ascontiguousarray(res_a.results[c]["x1"]) for c in range(NCORE)]
    in_maps_b = [_in_map_B(hp, c, x1_parts) for c in range(NCORE)]
    res_b = run_bass_kernel_spmd(nc_b, in_maps_b, core_ids=list(range(NCORE)))
    out = np.empty((B, L, D), dtype=np.float32)
    for c in range(NCORE):
        bi, hf = c // 2, c % 2
        x2 = res_b.results[c]["x2"].reshape(D, NOWN)
        out[bi, hf * NOWN:(hf + 1) * NOWN, :] = x2.T
    return out


# version 06
# speedup vs baseline: 1.4285x; 1.4285x over previous
import numpy as np
from contextlib import ExitStack

import concourse.bass as bass
import concourse.mybir as mybir
from concourse.bass import broadcast_tensor_aps
from concourse.bass_utils import run_bass_kernel_spmd

F32 = mybir.dt.float32
BF16 = mybir.dt.bfloat16
AF = mybir.ActivationFunctionType
ALU = mybir.AluOpType
AX = mybir.AxisListType.X

ENGS = ("sync", "tensor", "vector", "scalar", "gpsimd")

B, L, D = 4, 4096, 1024
H, HD = 16, 64
NCORE = 8
NOWN = 2048
CW = 512


class SemC:

    def __init__(self, h):
        self.h = h
        self.n = 0


class Builder:
    def __init__(self, nc):
        self.nc = nc
        self.es = ExitStack()
        self.sched = {e: [] for e in ENGS}
        self.seen = {e: {} for e in ENGS}
        self.sems = {}

    def sem(self, name):
        s = SemC(self.es.enter_context(self.nc.semaphore(name)))
        self.sems[name] = s
        return s

    strict = False

    def _emit(self, eng, fn, waits, sem, inc):
        tgt = None
        if sem is not None:
            sem.n += inc
            tgt = sem.n
        wl = []
        for ws, wv in waits:
            if wv is None:
                wv = ws.n
            if self.strict or self.seen[eng].get(ws, -1) < wv:
                wl.append((ws, wv))
                self.seen[eng][ws] = max(self.seen[eng].get(ws, -1), wv)
        if sem is not None:
            self.seen[eng][sem] = max(self.seen[eng].get(sem, -1), tgt)

        def f2(e, fn=fn, wl=tuple(wl), sh=(sem.h if sem else None), inc=inc):
            for ws, wv in wl:
                e.wait_ge(ws.h, wv)
            inst = fn(e)
            if sh is not None and inst is not None:
                inst.then_inc(sh, inc)
            return inst

        self.sched[eng].append(f2)
        return tgt

    def dma(self, eng, out, in_, sem, waits=(), allow_slow=False):
        def fn(e, o=out, i=in_, asl=allow_slow):
            return e.dma_start(out=o, in_=i, allow_slow_non_contiguous=asl)
        return self._emit(eng, fn, waits, sem, 16)

    def mm(self, lhsT, rhs, out, start, stop, waits=(), sem=None, inc=1,
           tile_position=None, skip_group_check=False):
        def fn(e, o=out, l=lhsT, r=rhs, st=start, sp=stop, tp=tile_position,
               sg=skip_group_check):
            return e.matmul(o, l, r, start=st, stop=sp, tile_position=tp,
                            skip_group_check=sg)
        return self._emit("tensor", fn, waits, sem, inc)

    def trans(self, out, in_, iden, waits=(), sem=None, inc=1):
        def fn(e, o=out, i=in_, d=iden):
            return e.transpose(o, i, d)
        return self._emit("tensor", fn, waits, sem, inc)

    def act(self, eng, out, in_, func, bias=0.0, scale=1.0, accum_out=None,
            waits=(), sem=None, inc=1):
        def fn(e, o=out, i=in_, f=func, b=bias, s=scale, a=accum_out):
            return e.activation(o, i, f, bias=b, scale=s, accum_out=a)
        return self._emit(eng, fn, waits, sem, inc)

    def tt(self, eng, out, in0, in1, op, waits=(), sem=None, inc=1):
        def fn(e, o=out, a=in0, b=in1, op=op):
            return e.tensor_tensor(o, a, b, op)
        return self._emit(eng, fn, waits, sem, inc)

    def ts(self, eng, out, in0, s1, s2, op0, op1=ALU.bypass, waits=(), sem=None, inc=1):
        def fn(e, o=out, i=in0, s1=s1, s2=s2, op0=op0, op1=op1):
            return e.tensor_scalar(o, i, s1, s2, op0, op1)
        return self._emit(eng, fn, waits, sem, inc)

    def stt(self, eng, out, in0, scalar, in1, op0, op1, waits=(), sem=None, inc=1):
        def fn(e, o=out, a=in0, s=scalar, b=in1, op0=op0, op1=op1):
            return e.scalar_tensor_tensor(o, a, s, b, op0, op1)
        return self._emit(eng, fn, waits, sem, inc)

    def copy(self, eng, out, in_, waits=(), sem=None, inc=1):
        def fn(e, o=out, i=in_):
            return e.tensor_copy(o, i)
        return self._emit(eng, fn, waits, sem, inc)

    def memset(self, eng, ap, val, waits=(), sem=None, inc=1):
        def fn(e, a=ap, v=val):
            return e.memset(a, v)
        return self._emit(eng, fn, waits, sem, inc)

    def red(self, eng, out, in_, op, negate=False, waits=(), sem=None, inc=1):
        def fn(e, o=out, i=in_, op=op, ng=negate):
            return e.tensor_reduce(o, i, AX, op, negate=ng)
        return self._emit(eng, fn, waits, sem, inc)

    def recip(self, out, in_, waits=(), sem=None, inc=1):
        def fn(e, o=out, i=in_):
            return e.reciprocal(o, i)
        return self._emit("vector", fn, waits, sem, inc)

    def wait(self, eng, waits):
        self._emit(eng, lambda e: None, waits, None, 0)

    def barrier(self, engs=ENGS):
        waits = [(s, s.n) for s in self.sems.values() if s.n > 0]
        for e in engs:
            self.wait(e, waits)

    def finalize(self):
        nc = self.nc
        with nc.Block() as block:
            for eng in ENGS:
                fns = self.sched[eng]
                if not fns:
                    continue
                getattr(block, eng)(lambda e, fns=fns: [f(e) for f in fns])
        self.es.close()



def _rope_tables():
    inv_freq = (1.0 / (np.float32(10000.0) **
                       (np.arange(0, HD, 2, dtype=np.float32) / np.float32(HD)))).astype(np.float32)
    t = np.arange(L, dtype=np.float32)
    freqs = t[:, None] * inv_freq[None, :]
    emb = np.concatenate([freqs, freqs], axis=-1)
    return np.cos(emb).astype(np.float32), np.sin(emb).astype(np.float32)


def _fmT(w):
    wt = np.ascontiguousarray(w.T.astype(np.float32, copy=False))
    return wt.reshape(wt.shape[0] // 128, 128, wt.shape[1])


def _cols(v):
    return np.ascontiguousarray(v.reshape(-1, 128).T.astype(np.float32, copy=False))


def host_prep(inp):
    hp = {}
    x = inp["L0_attn_x"]
    xT = np.ascontiguousarray(x.transpose(0, 2, 1))
    xTc = []
    for c in range(NCORE):
        bi, hf = c // 2, c % 2
        xr = np.concatenate([xT[bi][:, hf * NOWN:], xT[bi][:, :hf * NOWN]], axis=1)
        xTc.append(np.ascontiguousarray(xr).reshape(8, 128, L))
    hp["xT_core"] = xTc
    cos, sin = _rope_tables()
    hp["cosT_core"] = [np.ascontiguousarray(np.concatenate(
        [cos[(c % 2) * NOWN:], cos[:(c % 2) * NOWN]], axis=0)) for c in range(NCORE)]
    hp["sinT_core"] = [np.ascontiguousarray(np.concatenate(
        [sin[(c % 2) * NOWN:], sin[:(c % 2) * NOWN]], axis=0)) for c in range(NCORE)]
    hp["qwT"] = _fmT(inp["L0_attn_q_w"])
    hp["kwT"] = _fmT(inp["L0_attn_k_w"])
    hp["vwT"] = _fmT(inp["L0_attn_v_w"])
    hp["owT"] = _fmT(inp["L0_attn_o_w"])
    hp["qb_r"] = np.ascontiguousarray(inp["L0_attn_q_b"].reshape(1, D))
    hp["kb_r"] = np.ascontiguousarray(inp["L0_attn_k_b"].reshape(1, D))
    hp["vb_r"] = np.ascontiguousarray(inp["L0_attn_v_b"].reshape(1, D))
    hp["ob_c"] = _cols(inp["L0_attn_o_b"])
    hp["qn_c"] = _cols(inp["L0_attn_qn_scale"])
    hp["kn_c"] = _cols(inp["L0_attn_kn_scale"])
    hp["ones"] = np.ones((128, 128), dtype=np.float32)
    hp["iden"] = np.eye(128, dtype=np.float32)
    cst = np.zeros((128, 8), dtype=np.float32)
    cst[:, 0] = 1e-5
    cst[:, 1] = 1e-8
    cst[:, 2] = 1e-6
    hp["consts"] = cst
    hp["rwT"] = _fmT(inp["L0_ffn_router_weight"])
    hp["rb_r"] = np.ascontiguousarray(inp["L0_ffn_router_bias"].reshape(1, 8))
    gu = inp["L0_ffn_gate_up_proj"]
    hp["guT"] = np.ascontiguousarray(gu.transpose(0, 2, 1)).reshape(8, 8, 128, 2 * 2048)
    hp["gub_c"] = np.ascontiguousarray(
        inp["L0_ffn_gate_up_proj_bias"].reshape(8, 32, 128).transpose(0, 2, 1))
    dn = inp["L0_ffn_down_proj"]
    hp["dnT"] = np.ascontiguousarray(dn.transpose(0, 2, 1)).reshape(8, 16, 128, 1024)
    hp["dnb"] = np.ascontiguousarray(inp["L0_ffn_down_proj_bias"])
    hp["inv1702"] = np.full((1, 128), 1.0 / 1.702, dtype=np.float32)
    hp["W1"] = np.ascontiguousarray(inp["L1_attn_W"]).reshape(8, 128, 3 * D)
    hp["w1T"] = _fmT(inp["L1_ffn_w1_w"])
    hp["w2T"] = _fmT(inp["L1_ffn_w2_w"])
    hp["h1T"] = _fmT(inp["L1_ffn_h1_w"])
    hp["h2T"] = _fmT(inp["L1_ffn_h2_w"])
    hp["w3T"] = _fmT(inp["L1_ffn_w3_w"])
    hp["s1_c"] = _cols(inp["L1_ffn_w1_s"])
    hp["s2_c"] = _cols(inp["L1_ffn_w2_s"])
    hp["sh1_c"] = _cols(inp["L1_ffn_h1_s"])
    hp["sh2_c"] = _cols(inp["L1_ffn_h2_s"])
    hp["s3_c"] = _cols(inp["L1_ffn_w3_s"])
    return hp



def build_launch_A(dbg=False):
    nc = bass.Bass("TRN2", target_bir_lowering=False, debug=False)
    b = Builder(nc)
    T = {}

    def din(name, shape, dtype=F32):
        T[name] = nc.dram_tensor(name, shape, dtype, kind="ExternalInput").ap()

    def dout(name, shape, dtype=F32):
        T[name] = nc.dram_tensor(name, shape, dtype, kind="ExternalOutput").ap()

    def dscr(name, shape, dtype):
        kind = "ExternalOutput" if dbg else "Internal"
        T[name] = nc.dram_tensor(name, shape, dtype, kind=kind).ap()

    din("xT", [8, 128, L])
    for w in ("qwT", "kwT", "vwT", "owT"):
        din(w, [8, 128, D])
    din("qb_r", [1, D]); din("kb_r", [1, D]); din("vb_r", [1, D])
    din("ob_c", [128, 8]); din("qn_c", [128, 8]); din("kn_c", [128, 8])
    din("cosT", [L, HD]); din("sinT", [L, HD])
    din("ones", [128, 128]); din("iden", [128, 128]); din("consts", [128, 8])
    din("rwT", [8, 128, 8]); din("rb_r", [1, 8])
    din("guT", [8, 8, 128, 4096]); din("gub_c", [8, 128, 32])
    din("dnT", [8, 16, 128, 1024]); din("dnb", [8, 1024])
    din("inv1702", [1, 128])
    dout("x1", [8, 128, NOWN])
    dscr("q_fm", [8, 128, NOWN], BF16)
    dscr("k_fm", [8, 128, L], BF16)
    dscr("v_tm", [32, 128, H * 65], BF16)
    dscr("xn0", [8, 128, NOWN], BF16)
    if dbg:
        dscr("xp2", [8, 128, NOWN], F32)
        dscr("combT_d", [8, NOWN], F32)
        dscr("comb_d", [128, 128], F32)
        dscr("lg_d", [128, 128], F32)
        dscr("mr_d", [128, 64], F32)
        dscr("at_d", [8, 128, NOWN], BF16)

    LD = b.sem("LD")
    LG = b.sem("LG")
    PE = b.sem("PE")
    VE = b.sem("VE")
    SE = b.sem("SE")
    OUT = b.sem("OUT")

    al = b.es.enter_context

    s_ones_f = al(nc.sbuf_tensor("s_ones_f", [128, 128], F32))
    s_ones_b = al(nc.sbuf_tensor("s_ones_b", [128, 128], BF16))
    s_id_b = al(nc.sbuf_tensor("s_id_b", [128, 128], BF16))
    s_id_f = al(nc.sbuf_tensor("s_id_f", [128, 128], F32))
    s_consts = al(nc.sbuf_tensor("s_consts", [128, 8], F32))
    s_qn = al(nc.sbuf_tensor("s_qn", [128, 8], F32))
    s_kn = al(nc.sbuf_tensor("s_kn", [128, 8], F32))
    s_ob = al(nc.sbuf_tensor("s_ob", [128, 8], F32))
    s_inv1702 = al(nc.sbuf_tensor("s_inv1702", [1, 128], F32))

    b.dma("sync", s_ones_f[:], T["ones"], LD)
    b.dma("sync", s_id_f[:], T["iden"], LD)
    b.dma("sync", s_consts[:], T["consts"], LD)
    b.dma("sync", s_qn[:], T["qn_c"], LD)
    b.dma("sync", s_kn[:], T["kn_c"], LD)
    b.dma("sync", s_ob[:], T["ob_c"], LD)
    b.dma("sync", s_inv1702[:], T["inv1702"], LD)
    b.dma("gpsimd", s_ones_b[:], T["ones"], LG)
    b.dma("gpsimd", s_id_b[:], T["iden"], LG)

    ones1_b = s_ones_b[:, 0:1]
    ones1r_f = s_ones_f[0:1, :]
    eps_ln = s_consts[0:1, 0:1]
    eps_rms = s_consts[0:1, 1:2]

    _phase_l0_pre(nc, b, T, LD, LG, PE, VE, SE, OUT,
                  s_ones_f, s_ones_b, s_id_b, s_qn, s_kn, eps_ln, eps_rms)
    b.barrier()
    x_acc = al(nc.sbuf_tensor("x_acc", [128, 8 * NOWN], F32))
    _phase_l0_attn(nc, b, T, LD, LG, PE, VE, SE, OUT, s_ones_f, s_ob, x_acc)
    b.barrier()
    if dbg:
        b.dma("sync", T["xp2"][:, :, :].rearrange("c p w -> p c w"),
              x_acc[:].rearrange("p (c w) -> p c w", c=8), OUT)
        b.barrier()
    _phase_l0_moe(nc, b, T, LD, LG, PE, VE, SE, OUT,
                  s_ones_f, s_id_f, s_inv1702, x_acc)

    b.wait("sync", [(OUT, OUT.n)])
    b.finalize()
    return nc


def _phase_l0_pre(nc, b, T, LD, LG, PE, VE, SE, OUT,
                  s_ones_f, s_ones_b, s_id_b, s_qn, s_kn, eps_ln, eps_rms):
    es = ExitStack()
    al = es.enter_context
    s_qwT = al(nc.sbuf_tensor("p1_qwT", [128, 8 * D], BF16))
    s_kwT = al(nc.sbuf_tensor("p1_kwT", [128, 8 * D], BF16))
    s_vwT = al(nc.sbuf_tensor("p1_vwT", [128, 8 * D], BF16))
    s_qb_bc = al(nc.sbuf_tensor("p1_qb_bc", [128, D], F32))
    s_kb_bc = al(nc.sbuf_tensor("p1_kb_bc", [128, D], F32))
    s_vb_bc = al(nc.sbuf_tensor("p1_vb_bc", [128, D], F32))
    s_brow = al(nc.sbuf_tensor("p1_brow", [1, D], BF16))
    s_xf = al(nc.sbuf_tensor("p1_xf", [128, 8 * CW], F32))
    s_xb = al(nc.sbuf_tensor("p1_xb", [128, 8 * CW], BF16))
    s_sq = al(nc.sbuf_tensor("p1_sq", [128, 8 * CW], BF16))
    s_xnf = al(nc.sbuf_tensor("p1_xnf", [128, 8 * CW], F32))
    s_xnb = al(nc.sbuf_tensor("p1_xnb", [128, 8 * CW], BF16))
    s_kin = al(nc.sbuf_tensor("p1_kin", [128, 8 * CW], BF16))
    s_qin = al(nc.sbuf_tensor("p1_qin", [128, 8 * CW], BF16))
    s_t = al(nc.sbuf_tensor("p1_t", [128, CW], F32))
    s_rows = al(nc.sbuf_tensor("p1_rows", [1, 3 * CW], F32))
    s_cos = al(nc.sbuf_tensor("p1_cos", [128, HD], F32))
    s_sin = al(nc.sbuf_tensor("p1_sin", [128, HD], F32))
    s_ktm = al(nc.sbuf_tensor("p1_ktm", [128, D], F32))
    s_rot = al(nc.sbuf_tensor("p1_rot", [128, D], F32))
    s_r1 = al(nc.sbuf_tensor("p1_r1", [128, D], F32))
    s_r2 = al(nc.sbuf_tensor("p1_r2", [128, D], F32))
    s_ktb = al(nc.sbuf_tensor("p1_ktb", [128, D], BF16))
    s_ktr = al(nc.sbuf_tensor("p1_ktr", [128, D], BF16))
    s_vst = al(nc.sbuf_tensor("p1_vst", [128, H * 65], BF16))

    p_sA = al(nc.psum_tensor("p1_sA", [1, CW], F32))
    p_sB = al(nc.psum_tensor("p1_sB", [1, CW], F32))
    p_m = al(nc.psum_tensor("p1_m", [128, CW], F32))
    p_r = al(nc.psum_tensor("p1_r", [128, CW], F32))
    p_k = al(nc.psum_tensor("p1_k", [128, 2 * CW], F32))
    p_tr = al(nc.psum_tensor("p1_tr", [128, D], BF16))

    mrow = s_rows[0:1, 0:CW]
    rstd = s_rows[0:1, CW:2 * CW]
    trow = s_rows[0:1, 2 * CW:3 * CW]
    erow = rstd
    varr = trow
    rrms = rstd

    for (wname, sdst) in (("qwT", s_qwT), ("kwT", s_kwT), ("vwT", s_vwT)):
        for dc in range(8):
            b.dma("gpsimd", sdst[:, dc * D:(dc + 1) * D], T[wname][dc], LG)
    for (bname, bdst) in (("qb_r", s_qb_bc), ("kb_r", s_kb_bc), ("vb_r", s_vb_bc)):
        lv = b.dma("gpsimd", s_brow[:], T[bname], LG, waits=[(PE, PE.n)])
        for oc in range(2):
            b.mm(s_ones_b[0:1, :], s_brow[0:1, oc * CW:(oc + 1) * CW],
                 p_m[:, 0:CW], True, True, waits=[(LG, lv), (VE, VE.n)], sem=PE)
            b.copy("vector", bdst[:, oc * CW:(oc + 1) * CW], p_m[:, 0:CW],
                   waits=[(PE, PE.n)], sem=VE)

    for ch in range(8):
        c0 = ch * CW
        own = ch < 4
        lvx = b.dma("sync", s_xf[:].rearrange("p (c w) -> p c w", c=8),
                    T["xT"][:, :, c0:c0 + CW].rearrange("c p w -> p c w"), LD,
                    waits=[(PE, PE.n), (VE, VE.n), (SE, SE.n)])
        lvb = b.dma("gpsimd", s_xb[:].rearrange("p (c w) -> p c w", c=8),
                    T["xT"][:, :, c0:c0 + CW].rearrange("c p w -> p c w"), LG,
                    waits=[(PE, PE.n), (VE, VE.n), (SE, SE.n)])
        sv = b.act("scalar", s_sq[:], s_xb[:], AF.Square, waits=[(LG, lvb)], sem=SE)
        for dc in range(8):
            b.mm(s_ones_b[:, 0:1], s_xb[:, dc * CW:(dc + 1) * CW], p_sA[:],
                 dc == 0, dc == 7, waits=[(LG, lvb), (VE, VE.n)],
                 sem=(PE if dc == 7 else None))
        for dc in range(8):
            b.mm(s_ones_b[:, 0:1], s_sq[:, dc * CW:(dc + 1) * CW], p_sB[:],
                 dc == 0, dc == 7, waits=[(SE, sv), (VE, VE.n)],
                 sem=(PE if dc == 7 else None))
        pe_stats = PE.n
        b.ts("vector", mrow, p_sA[:], 1.0 / D, None, ALU.mult, waits=[(PE, pe_stats)], sem=VE)
        b.ts("vector", erow, p_sB[:], 1.0 / D, None, ALU.mult, waits=[(PE, pe_stats)], sem=VE)
        b.act("scalar", trow, mrow, AF.Square, waits=[(VE, VE.n)], sem=SE)
        b.tt("vector", varr, erow, trow, ALU.subtract, waits=[(SE, SE.n)], sem=VE)
        b.act("scalar", trow, varr, AF.Ln, bias=eps_ln, waits=[(VE, VE.n)], sem=SE)
        b.act("scalar", rstd, trow, AF.Exp, scale=-0.5, waits=[(SE, SE.n)], sem=SE)
        sv_rstd = SE.n
        b.mm(s_ones_f[0:1, :], mrow, p_m[:, 0:CW], True, True,
             waits=[(VE, VE.n)], sem=PE)
        b.mm(s_ones_f[0:1, :], rstd, p_r[:, 0:CW], True, True,
             waits=[(SE, sv_rstd), (VE, VE.n)], sem=PE)
        pe_bc = PE.n
        for dc in range(8):
            sl = slice(dc * CW, (dc + 1) * CW)
            b.tt("vector", s_t[:], s_xf[:, sl], p_m[:, 0:CW], ALU.subtract,
                 waits=[(PE, pe_bc), (LD, lvx)], sem=None)
            b.tt("vector", s_xnf[:, sl], s_t[:], p_r[:, 0:CW], ALU.mult, sem=None)
        b.copy("vector", s_xnb[:], s_xnf[:], waits=[(OUT, OUT.n)], sem=VE)
        v_xn = VE.n
        if own:
            b.dma("sync", T["xn0"][:, :, c0:c0 + CW].rearrange("c p w -> p c w"),
                  s_xnb[:].rearrange("p (c w) -> p c w", c=8), OUT, waits=[(VE, v_xn)])
        sv2 = b.act("scalar", s_sq[:], s_xnb[:], AF.Square, waits=[(VE, v_xn)], sem=SE)
        for dc in range(8):
            b.mm(s_ones_b[:, 0:1], s_sq[:, dc * CW:(dc + 1) * CW], p_sA[:],
                 dc == 0, dc == 7, waits=[(SE, sv2), (VE, VE.n)],
                 sem=(PE if dc == 7 else None))
        b.ts("vector", trow, p_sA[:], 1.0 / D, None, ALU.mult, waits=[(PE, PE.n)], sem=VE)
        b.act("scalar", trow, trow, AF.Ln, bias=eps_rms, waits=[(VE, VE.n)], sem=SE)
        b.act("scalar", rrms, trow, AF.Exp, scale=-0.5, waits=[(SE, SE.n)], sem=SE)
        b.mm(s_ones_f[0:1, :], rrms, p_r[:, 0:CW], True, True,
             waits=[(SE, SE.n), (VE, VE.n)], sem=PE)
        pe_rr = PE.n
        for dc in range(8):
            sl = slice(dc * CW, (dc + 1) * CW)
            b.tt("vector", s_t[:], s_xnf[:, sl], p_r[:, 0:CW], ALU.mult,
                 waits=[(PE, pe_rr)], sem=None)
            b.ts("vector", s_kin[:, sl], s_t[:], s_kn[:, dc:dc + 1], None, ALU.mult,
                 sem=None)
            if own:
                b.ts("vector", s_qin[:, sl], s_t[:], s_qn[:, dc:dc + 1], None, ALU.mult,
                     sem=(VE if dc == 7 else None))
            elif dc == 7:
                b.memset("vector", s_t[0:1, 0:1], 0.0, sem=VE)
        v_in = VE.n

        for nt in range(4):
            tok0 = c0 + nt * 128
            lvc = b.dma("sync", s_cos[:], T["cosT"][tok0:tok0 + 128, :], LD,
                        waits=[(VE, VE.n)])
            lvs = b.dma("sync", s_sin[:], T["sinT"][tok0:tok0 + 128, :], LD,
                        waits=[(VE, VE.n)])

            def proj_rope(s_in, s_w, s_bb, dram, dram_col0):
                for dc in range(8):
                    for oc in range(2):
                        b.mm(s_in[:, dc * CW + nt * 128: dc * CW + nt * 128 + 128],
                             s_w[:, dc * D + oc * CW: dc * D + oc * CW + CW],
                             p_k[:, oc * CW:(oc + 1) * CW], dc == 0, dc == 7,
                             waits=[(VE, v_in), (VE, VE.n)],
                             sem=(PE if (dc == 7 and oc == 1) else None))
                pe_p = PE.n
                b.tt("vector", s_ktm[:], p_k[:, 0:D], s_bb[:], ALU.add,
                     waits=[(PE, pe_p), (LD, lvs)], sem=VE)
                kp = s_ktm[:].rearrange("p (k two) -> p k two", two=2)
                rp = s_rot[:].rearrange("p (k two) -> p k two", two=2)
                b.ts("vector", rp[:, :, 0:1], kp[:, :, 1:2], -1.0, None, ALU.mult, sem=None)
                b.copy("vector", rp[:, :, 1:2], kp[:, :, 0:1], sem=None)
                kv_h = s_ktm[:].rearrange("p (h d) -> p h d", h=H)
                rv_h = s_rot[:].rearrange("p (h d) -> p h d", h=H)
                cs1, cb1 = broadcast_tensor_aps(kv_h, s_cos[:].rearrange("p (o d) -> p o d", o=1))
                b.tt("vector", s_r1[:].rearrange("p (h d) -> p h d", h=H), cs1, cb1,
                     ALU.mult, sem=None)
                ss1, sb1 = broadcast_tensor_aps(rv_h, s_sin[:].rearrange("p (o d) -> p o d", o=1))
                b.tt("vector", s_r2[:].rearrange("p (h d) -> p h d", h=H), ss1, sb1,
                     ALU.mult, sem=None)
                b.tt("vector", s_ktb[:], s_r1[:], s_r2[:], ALU.add, sem=VE)
                v_rope = VE.n
                for dc in range(8):
                    b.trans(p_tr[:, dc * 128:(dc + 1) * 128],
                            s_ktb[:, dc * 128:(dc + 1) * 128], s_id_b[:],
                            waits=[(VE, v_rope)], sem=(PE if dc == 7 else None))
                b.copy("vector", s_ktr[:], p_tr[:], waits=[(PE, PE.n), (OUT, OUT.n)], sem=VE)
                b.dma("sync", dram[:, :, dram_col0:dram_col0 + 128].rearrange("c p f -> p c f"),
                      s_ktr[:].rearrange("p (c f) -> p c f", c=8), OUT,
                      waits=[(VE, VE.n)])

            proj_rope(s_kin, s_kwT, s_kb_bc, T["k_fm"], tok0)
            if own:
                proj_rope(s_qin, s_qwT, s_qb_bc, T["q_fm"], tok0)

            for dc in range(8):
                for oc in range(2):
                    b.mm(s_xnb[:, dc * CW + nt * 128: dc * CW + nt * 128 + 128],
                         s_vwT[:, dc * D + oc * CW: dc * D + oc * CW + CW],
                         p_k[:, oc * CW:(oc + 1) * CW], dc == 0, dc == 7,
                         waits=[(VE, v_in), (VE, VE.n)],
                         sem=(PE if (dc == 7 and oc == 1) else None))
            pe_v = PE.n
            b.tt("vector", s_ktm[:], p_k[:, 0:D], s_vb_bc[:], ALU.add,
                 waits=[(PE, pe_v)], sem=VE)
            b.copy("vector",
                   s_vst[:].rearrange("p (h c) -> p h c", c=65)[:, :, 0:64],
                   s_ktm[:].rearrange("p (h d) -> p h d", d=64),
                   waits=[(OUT, OUT.n)], sem=None)
            b.memset("vector", s_vst[:].rearrange("p (h c) -> p h c", c=65)[:, :, 64:65],
                     1.0, sem=VE)
            b.dma("sync", T["v_tm"][ch * 4 + nt], s_vst[:], OUT, waits=[(VE, VE.n)])
    es.close()


def _phase_l0_attn(nc, b, T, LD, LG, PE, VE, SE, OUT, s_ones_f, s_ob, x_acc):
    es = ExitStack()
    al = es.enter_context
    NK = 4
    s_kp = [al(nc.sbuf_tensor(f"p2_kp{i}", [128, L], BF16)) for i in range(2)]
    s_qp = [al(nc.sbuf_tensor(f"p2_qp{i}", [128, NOWN], BF16)) for i in range(2)]
    s_vp = [al(nc.sbuf_tensor(f"p2_vp{i}", [128, 32 * 130], BF16)) for i in range(2)]
    s_w = [al(nc.sbuf_tensor(f"p2_w{i}", [128, CW], BF16)) for i in range(NK)]
    s_avc = [al(nc.sbuf_tensor(f"p2_avc{i}", [65, CW], F32)) for i in range(2)]
    s_rs = al(nc.sbuf_tensor("p2_rs", [1, CW], F32))
    s_rsb = al(nc.sbuf_tensor("p2_rsb", [64, CW], F32))
    s_at = al(nc.sbuf_tensor("p2_at", [128, 8 * NOWN], BF16))
    s_owT = al(nc.sbuf_tensor("p2_owT", [128, 8 * D], BF16))
    s_x0 = al(nc.sbuf_tensor("p2_x0", [128, 4 * CW], F32))

    p_s = [al(nc.psum_tensor(f"p2_s{i}", [128, CW], F32)) for i in range(2)]
    p_av = al(nc.psum_tensor("p2_av", [65, CW], F32))
    p_b = al(nc.psum_tensor("p2_b", [64, CW], F32))
    p_o = [al(nc.psum_tensor(f"p2_o{i}", [128, CW], F32)) for i in range(4)]

    for dc in range(8):
        b.dma("gpsimd", s_owT[:, dc * D:(dc + 1) * D], T["owT"][dc], LG)
    lg_ow = LG.n

    sig_done = {}
    av_done = {}
    pe_use = {0: 0, 1: 0}
    pend_norm = [None]
    avc_free = {}

    for hp_i in range(8):
        buf = hp_i % 2
        lk = b.dma("sync", s_kp[buf][:], T["k_fm"][hp_i], LD,
                   waits=[(PE, pe_use[buf])])
        lq = b.dma("sync", s_qp[buf][:], T["q_fm"][hp_i], LD)
        lv = b.dma("sync",
                   s_vp[buf][:].rearrange("p (t c) -> p t c", t=32),
                   T["v_tm"][:, :, 2 * hp_i * 65:(2 * hp_i + 2) * 65].rearrange("t p c -> p t c"), LD)
        for hh in range(2):
            base = 64 * hh
            for qc in range(4):
                q0 = qc * CW
                for kt in range(32):
                    slot = kt % NK
                    ps = p_s[kt % 2]
                    b.mm(s_kp[buf][base:base + 64, kt * 128:(kt + 1) * 128],
                         s_qp[buf][base:base + 64, q0:q0 + CW],
                         ps[:], True, True,
                         waits=[(LD, lv), (SE, sig_done.get(kt % 2, 0))],
                         sem=PE, tile_position=(base, 0))
                    pe_sc = PE.n
                    b.act("scalar", s_w[slot][:], ps[:], AF.Sigmoid, scale=0.125,
                          waits=[(PE, pe_sc), (PE, av_done.get(slot, 0))], sem=SE)
                    sig_done[kt % 2] = SE.n
                    sg = SE.n
                    b.mm(s_vp[buf][:, kt * 130 + hh * 65: kt * 130 + hh * 65 + 65],
                         s_w[slot][:], p_av[:], kt == 0, kt == 31,
                         waits=[(SE, sg)], sem=PE)
                    av_done[slot] = PE.n

                pe_av = PE.n
                ab = (hp_i * 8 + hh * 4 + qc) % 2
                b.copy("vector", s_avc[ab][:], p_av[:],
                       waits=[(PE, pe_av), (VE, avc_free.get(ab, 0))], sem=VE)
                b.wait("tensor", [(VE, VE.n)])

                def mk_norm(ab=ab, base=base, hp_i=hp_i, q0=q0):
                    def emit():
                        b.strict = True
                        b.ts("vector", s_rs[0:1, :], s_avc[ab][64:65, :], 1e-8, None,
                             ALU.add, waits=[(VE, VE.n)], sem=VE)
                        b.recip(s_rs[0:1, :], s_rs[0:1, :], waits=[(VE, VE.n)], sem=VE)
                        b.strict = False
                        b.mm(s_ones_f[0:1, 0:64], s_rs[0:1, :], p_b[:], True, True,
                             waits=[(VE, VE.n)], sem=PE)
                        b.copy("vector", s_rsb[:], p_b[:], waits=[(PE, PE.n)], sem=VE)
                        b.tt("vector",
                             s_at[base:base + 64,
                                  hp_i * NOWN + q0: hp_i * NOWN + q0 + CW],
                             s_avc[ab][0:64, :], s_rsb[:], ALU.mult,
                             waits=[(VE, VE.n)], sem=VE)
                        avc_free[ab] = VE.n
                    return emit
                mk_norm()()
        pe_use[buf] = PE.n
    v_at = VE.n
    if "at_d" in T:
        b.dma("sync", T["at_d"][:, :, :].rearrange("c p w -> p c w"),
              s_at[:].rearrange("p (c w) -> p c w", c=8), OUT, waits=[(VE, v_at)])

    for oc in range(8):
        lx = b.dma("sync", s_x0[:].rearrange("p (c w) -> p c w", c=4),
                   T["xT"][oc][:, 0:4 * CW].rearrange("p (c w) -> p c w", c=4), LD,
                   waits=[(VE, VE.n)])
        for dc in range(8):
            for ncx in range(4):
                b.mm(s_owT[:, dc * D + oc * 128: dc * D + oc * 128 + 128],
                     s_at[:, dc * NOWN + ncx * CW: dc * NOWN + ncx * CW + CW],
                     p_o[ncx][:], dc == 0, dc == 7,
                     waits=[(VE, v_at), (LG, lg_ow), (VE, VE.n)],
                     sem=(PE if (dc == 7 and ncx == 3) else None))
        pe_o = PE.n
        for ncx in range(4):
            b.stt("vector",
                  x_acc[:, oc * NOWN + ncx * CW: oc * NOWN + ncx * CW + CW],
                  p_o[ncx][:], s_ob[:, oc:oc + 1], s_x0[:, ncx * CW:(ncx + 1) * CW],
                  ALU.add, ALU.add,
                  waits=[(PE, pe_o), (LD, lx)], sem=(VE if ncx == 3 else None))
    es.close()


def _phase_l0_moe(nc, b, T, LD, LG, PE, VE, SE, OUT,
                  s_ones_f, s_id_f, s_inv1702, x_acc, store_x1=True):
    es = ExitStack()
    al = es.enter_context
    s_xn = al(nc.sbuf_tensor("p3_xn", [128, 8 * NOWN], BF16))
    s_combT = al(nc.sbuf_tensor("p3_combT", [8, NOWN], F32))
    s_cbr = al(nc.sbuf_tensor("p3_cbr", [1, 4 * CW], BF16))
    s_invb = al(nc.sbuf_tensor("p3_invb", [1, 128], BF16))
    cb_free = {0: 0}
    s_cbb = al(nc.sbuf_tensor("p3_cbb", [128, NOWN], BF16))
    es_r = ExitStack()
    alr = es_r.enter_context
    s_rw = alr(nc.sbuf_tensor("p3_rw", [128, 8 * 8], BF16))
    s_rbb = alr(nc.sbuf_tensor("p3_rbb", [128, 8], F32))
    s_rbr = alr(nc.sbuf_tensor("p3_rbr", [1, 8], F32))
    s_lg = alr(nc.sbuf_tensor("p3_lg", [128, 16 * 8], F32))
    s_l2 = alr(nc.sbuf_tensor("p3_l2", [128, 2 * 8], F32))
    s_msk = alr(nc.sbuf_tensor("p3_msk", [128, 8], F32))
    s_mrow = alr(nc.sbuf_tensor("p3_mrow", [128, 4], F32))
    s_mr_d = alr(nc.sbuf_tensor("p3_mr_d", [128, 64], F32)) if "mr_d" in T else None
    s_comb = alr(nc.sbuf_tensor("p3_comb", [128, 16 * 8], F32))
    s_dnb = alr(nc.sbuf_tensor("p3_dnb", [8, 1024], F32))

    p_gu = [al(nc.psum_tensor(f"p3_gu_{i}", [128, CW], F32)) for i in range(8)]
    p_d4 = [p_gu[i] for i in range(4)]
    p_d = [p_gu[0], p_gu[1]]
    p_cb2 = p_gu[4]
    p_r = p_gu[6]
    p_cT = p_gu[7]

    lxn = b.dma("sync", s_xn[:].rearrange("p (c w) -> p c w", c=8),
                T["xn0"][:, :, :].rearrange("c p w -> p c w"), LD)
    for dc in range(8):
        b.dma("gpsimd", s_rw[:, dc * 8:(dc + 1) * 8], T["rwT"][dc], LG)
    ldnb = b.dma("sync", s_dnb[:], T["dnb"], LD)
    lg0 = LG.n
    lrb = b.dma("sync", s_rbr[:], T["rb_r"], LD)
    b.mm(s_ones_f[0:1, :], s_rbr[0:1, :], p_r[:, 0:8], True, True,
         waits=[(LD, lrb), (VE, VE.n)], sem=PE)
    b.copy("vector", s_rbb[:], p_r[:, 0:8], waits=[(PE, PE.n)], sem=VE)

    for nt in range(16):
        for dc in range(8):
            b.mm(s_xn[:, dc * NOWN + nt * 128: dc * NOWN + nt * 128 + 128],
                 s_rw[:, dc * 8:(dc + 1) * 8], p_r[:, 0:8], dc == 0, dc == 7,
                 waits=[(LD, lxn), (LG, lg0), (VE, VE.n)], sem=(PE if dc == 7 else None))
        lgs = s_lg[:, nt * 8:(nt + 1) * 8]
        b.tt("vector", lgs, p_r[:, 0:8], s_rbb[:], ALU.add, waits=[(PE, PE.n)], sem=VE)
        la = s_l2[:, 0:8]
        lb = s_l2[:, 8:16]
        b.strict = True
        b.copy("vector", la, lgs, waits=[(VE, VE.n)], sem=VE)
        b.red("vector", s_mrow[:, 0:1], la, ALU.max, negate=True,
              waits=[(VE, VE.n)], sem=VE)
        cur = la
        for it in range(3):
            nxt = lb if cur is la else la
            b.red("vector", s_mrow[:, 1:2], cur, ALU.max, waits=[(VE, VE.n)], sem=VE)
            b.ts("vector", s_msk[:], cur, s_mrow[:, 1:2], None, ALU.is_ge,
                 waits=[(VE, VE.n)], sem=VE)
            b.stt("vector", nxt, s_msk[:], -1e30, cur, ALU.mult, ALU.add,
                  waits=[(VE, VE.n)], sem=VE)
            cur = nxt
        b.red("vector", s_mrow[:, 2:3], cur, ALU.max, waits=[(VE, VE.n)], sem=VE)
        b.ts("vector", s_msk[:], lgs, s_mrow[:, 2:3], None, ALU.is_ge,
             waits=[(VE, VE.n)], sem=VE)
        b.act("scalar", cur, lgs, AF.Exp, bias=s_mrow[:, 0:1], waits=[(VE, VE.n)], sem=SE)
        b.tt("vector", cur, cur, s_msk[:], ALU.mult, waits=[(SE, SE.n), (VE, VE.n)], sem=VE)
        b.red("vector", s_mrow[:, 3:4], cur, ALU.add, waits=[(VE, VE.n)], sem=VE)
        b.recip(s_mrow[:, 2:3], s_mrow[:, 3:4], waits=[(VE, VE.n)], sem=VE)
        b.ts("vector", s_comb[:, nt * 8:(nt + 1) * 8], cur, s_mrow[:, 2:3], None,
             ALU.mult, waits=[(VE, VE.n)], sem=VE)
        b.strict = False
        if "mr_d" in T:
            b.copy("vector", s_mr_d[:, nt * 4:(nt + 1) * 4], s_mrow[:], sem=VE)
    v_comb = VE.n
    if "comb_d" in T:
        b.dma("sync", T["comb_d"][:, :], s_comb[:], OUT, waits=[(VE, v_comb)])
        b.dma("sync", T["lg_d"][:, :], s_lg[:], OUT, waits=[(VE, v_comb)])
        b.dma("sync", T["mr_d"][:, :], s_mr_d[:], OUT, waits=[(VE, v_comb)])
    for grp in range(4):
        for j in range(4):
            nt = grp * 4 + j
            b.trans(p_cT[0:8, j * 128:(j + 1) * 128], s_comb[:, nt * 8:(nt + 1) * 8],
                    s_id_f[:], waits=[(VE, v_comb), (VE, VE.n)],
                    sem=(PE if j == 3 else None))
        b.copy("vector", s_combT[:, grp * CW:(grp + 1) * CW], p_cT[0:8, :],
               waits=[(PE, PE.n)], sem=VE)
    v_cT = VE.n
    if "combT_d" in T:
        b.dma("sync", T["combT_d"][:, :], s_combT[:], OUT, waits=[(VE, v_cT)])

    for dc in range(8):
        for ncx in range(4):
            b.mm(s_dnb[:, dc * 128:(dc + 1) * 128],
                 s_combT[:, ncx * CW:(ncx + 1) * CW],
                 p_d[0][:], True, True,
                 waits=[(VE, v_cT), (LD, ldnb), (VE, VE.n)], sem=PE)
            b.tt("vector", x_acc[:, dc * NOWN + ncx * CW: dc * NOWN + ncx * CW + CW],
                 x_acc[:, dc * NOWN + ncx * CW: dc * NOWN + ncx * CW + CW],
                 p_d[0][:], ALU.add, waits=[(PE, PE.n)], sem=VE)

    b.copy("vector", s_invb[:], s_inv1702[:], waits=[(LD, LD.n)], sem=VE)
    es_r.close()
    s_gu = [al(nc.sbuf_tensor(f"p3_gu{i}", [128, 8 * 128], BF16)) for i in range(2)]
    s_gu2 = [al(nc.sbuf_tensor(f"p3_gu2{i}", [128, 8 * 128], BF16)) for i in range(2)]
    s_dn = [al(nc.sbuf_tensor(f"p3_dn{i}", [128, 16 * 128], BF16)) for i in range(2)]
    s_act = al(nc.sbuf_tensor("p3_act", [128, 16 * NOWN], BF16))
    s_sil4 = al(nc.sbuf_tensor("p3_sil4", [128, 4 * CW], BF16))
    s_gp4 = al(nc.sbuf_tensor("p3_gp4", [128, 4 * CW], BF16))
    s_up = al(nc.sbuf_tensor("p3_up", [128, CW], F32))
    s_gub = al(nc.sbuf_tensor("p3_gub", [128, 32], F32))
    pe_use = {0: 0, 1: 0}
    pe_use_dn = {0: 0, 1: 0}
    gfree = {i: VE.n for i in range(4)}
    ufree = {i: VE.n for i in range(4)}
    dfree = {i: VE.n for i in range(4)}
    cb_free[0] = VE.n
    for e in range(8):
        lcbs = []
        for ncx in range(4):
            lcbs.append(b.dma("gpsimd", s_cbr[:, ncx * CW:(ncx + 1) * CW],
                              s_combT[e:e + 1, ncx * CW:(ncx + 1) * CW],
                              LG, waits=[(VE, v_cT), (PE, PE.n)]))
        lgub = b.dma("sync", s_gub[:], T["gub_c"][e], LD, waits=[(VE, VE.n)])
        v_cb = None
        for ic in range(16):
            wb = ic % 2
            lwg = b.dma("gpsimd", s_gu[wb][:].rearrange("p (c f) -> p c f", c=8),
                        T["guT"][e][:, :, ic * 128:(ic + 1) * 128].rearrange("c p f -> p c f"), LG,
                        waits=[(PE, pe_use[wb])])
            lwu = b.dma("gpsimd", s_gu2[wb][:].rearrange("p (c f) -> p c f", c=8),
                        T["guT"][e][:, :, 2048 + ic * 128: 2048 + ic * 128 + 128].rearrange("c p f -> p c f"), LG)
            for dc in range(8):
                for ncx in range(4):
                    b.mm(s_gu[wb][:, dc * 128:(dc + 1) * 128],
                         s_xn[:, dc * NOWN + ncx * CW: dc * NOWN + ncx * CW + CW],
                         p_gu[ncx][:], dc == 0, dc == 7,
                         waits=[(LG, lwg), (VE, max(gfree[ncx], dfree[ncx]))],
                         sem=(PE if (dc == 7 and ncx == 3) else None))
            pe_g = PE.n
            if ic == 0:
                for ncx in range(4):
                    b.mm(s_invb[0:1, :], s_cbr[0:1, ncx * CW:(ncx + 1) * CW],
                         p_cb2[:], True, True,
                         waits=[(LG, lcbs[ncx]), (VE, cb_free[0]), (VE, ufree[0])],
                         sem=PE)
                    b.copy("vector", s_cbb[:, ncx * CW:(ncx + 1) * CW], p_cb2[:],
                           waits=[(PE, PE.n)], sem=VE)
                    cb_free[0] = VE.n
                v_cb = VE.n
            for ncx in range(4):
                b.ts("vector", s_gp4[:, ncx * CW:(ncx + 1) * CW], p_gu[ncx][:],
                     s_gub[:, ic:ic + 1], 7.0, ALU.add, ALU.min,
                     waits=[(PE, pe_g), (LD, lgub)], sem=VE)
                gfree[ncx] = VE.n
            b.act("scalar", s_sil4[:], s_gp4[:], AF.Silu, scale=1.702,
                  waits=[(VE, VE.n)], sem=SE)
            for dc in range(8):
                for ncx in range(4):
                    uw = ufree[ncx] if not (ic == 0 and ncx == 0) else max(ufree[0], v_cb)
                    b.mm(s_gu2[wb][:, dc * 128:(dc + 1) * 128],
                         s_xn[:, dc * NOWN + ncx * CW: dc * NOWN + ncx * CW + CW],
                         p_gu[4 + ncx][:], dc == 0, dc == 7,
                         waits=[(LG, lwu), (VE, uw)],
                         sem=(PE if (dc == 7 and ncx == 3) else None))
            pe_u = PE.n
            pe_use[wb] = PE.n
            for ncx in range(4):
                b.ts("vector", s_up[:], p_gu[4 + ncx][:], s_gub[:, 16 + ic:17 + ic],
                     -7.0, ALU.add, ALU.max, waits=[(PE, pe_u), (LD, lgub)], sem=VE)
                ufree[ncx] = VE.n
                b.ts("vector", s_up[:], s_up[:], 7.0, 1.0, ALU.min, ALU.add, sem=None)
                b.tt("vector", s_up[:], s_up[:], s_cbb[:, ncx * CW:(ncx + 1) * CW],
                     ALU.mult, waits=[(VE, v_cb)], sem=None)
                b.tt("vector", s_act[:, ic * NOWN + ncx * CW: ic * NOWN + ncx * CW + CW],
                     s_up[:], s_sil4[:, ncx * CW:(ncx + 1) * CW], ALU.mult,
                     waits=[(SE, SE.n)], sem=VE)
        v_act = VE.n
        for dc in range(8):
            wb = dc % 2
            ldn = b.dma("gpsimd", s_dn[wb][:].rearrange("p (c f) -> p c f", c=16),
                        T["dnT"][e][:, :, dc * 128:(dc + 1) * 128].rearrange("c p f -> p c f"), LG,
                        waits=[(PE, pe_use_dn[wb])])
            for ic in range(16):
                for ncx in range(4):
                    b.mm(s_dn[wb][:, ic * 128:(ic + 1) * 128],
                         s_act[:, ic * NOWN + ncx * CW: ic * NOWN + ncx * CW + CW],
                         p_d4[ncx][:], ic == 0, ic == 15,
                         waits=[(LG, ldn), (VE, v_act), (VE, dfree[ncx])],
                         sem=(PE if (ic == 15 and ncx == 3) else None))
            pe_use_dn[wb] = PE.n
            for ncx in range(4):
                b.tt("vector", x_acc[:, dc * NOWN + ncx * CW: dc * NOWN + ncx * CW + CW],
                     x_acc[:, dc * NOWN + ncx * CW: dc * NOWN + ncx * CW + CW],
                     p_d4[ncx][:], ALU.add, waits=[(PE, PE.n)], sem=VE)
                dfree[ncx] = VE.n

    if store_x1:
        b.dma("sync", T["x1"][:, :, :].rearrange("c p w -> p c w"),
              x_acc[:].rearrange("p (c w) -> p c w", c=8), OUT,
              waits=[(VE, VE.n)])
    es.close()



def _phase_l1_ffn(nc, b, T, LD, LG, PE, VE, SE, OUT, s_xn1):
    es = ExitStack()
    al = es.enter_context
    if s_xn1 is None:
        s_xn1 = al(nc.sbuf_tensor("f4_xn1", [128, 8 * NOWN], BF16))
        b.dma("sync", s_xn1[:].rearrange("p (c w) -> p c w", c=8),
              T["xn1_d"][:, :, :].rearrange("c p w -> p c w"), LD,
              waits=[(OUT, OUT.n)])
    s_sc = al(nc.sbuf_tensor("b4_sc", [128, 32 * 4 + 8], F32))
    s_h0 = al(nc.sbuf_tensor("b4_h0", [128, 32 * 1024], BF16))
    s_h1 = al(nc.sbuf_tensor("b4_h1", [128, 32 * 1024], BF16))
    s_w = [al(nc.sbuf_tensor(f"b4_w{i}", [128, 32 * 128], BF16)) for i in range(2)]
    s_t = al(nc.sbuf_tensor("b4_t", [128, CW], F32))
    s_u = al(nc.sbuf_tensor("b4_u", [128, CW], F32))
    s_sil = al(nc.sbuf_tensor("b4_sil", [128, CW], F32))
    s_xa = al(nc.sbuf_tensor("b4_xa", [128, CW], F32))
    s_x2 = al(nc.sbuf_tensor("b4_x2", [128, CW], F32))
    p_a = [al(nc.psum_tensor(f"b4_a{i}", [128, CW], F32)) for i in range(2)]
    p_b2 = [al(nc.psum_tensor(f"b4_b{i}", [128, CW], F32)) for i in range(2)]

    l_sc = b.dma("sync", s_sc[:, 0:32], T["s1_c"], LD)
    b.dma("sync", s_sc[:, 32:64], T["s2_c"], LD)
    b.dma("sync", s_sc[:, 64:96], T["sh1_c"], LD)
    b.dma("sync", s_sc[:, 96:128], T["sh2_c"], LD)
    l_sc = b.dma("sync", s_sc[:, 128:136], T["s3_c"], LD)
    s1c = s_sc[:, 0:32]; s2c = s_sc[:, 32:64]
    sh1c = s_sc[:, 64:96]; sh2c = s_sc[:, 96:128]; s3c = s_sc[:, 128:136]

    pe_use = {0: 0, 1: 0}
    for nh in range(2):
        h0 = nh * 1024
        for jc in range(32):
            wb = jc % 2
            lw1 = b.dma("gpsimd", s_w[wb][:, 0:1024].rearrange("p (c f) -> p c f", c=8),
                        T["w1T"][:, :, jc * 128:(jc + 1) * 128].rearrange("c p f -> p c f"),
                        LG, waits=[(PE, pe_use[wb])])
            lw2 = b.dma("gpsimd", s_w[wb][:, 1024:2048].rearrange("p (c f) -> p c f", c=8),
                        T["w2T"][:, :, jc * 128:(jc + 1) * 128].rearrange("c p f -> p c f"),
                        LG)
            for dc in range(8):
                for nc2 in range(2):
                    b.mm(s_w[wb][:, dc * 128:(dc + 1) * 128],
                         s_xn1[:, dc * NOWN + h0 + nc2 * CW: dc * NOWN + h0 + nc2 * CW + CW],
                         p_a[nc2][:], dc == 0, dc == 7,
                         waits=[(LG, lw1), (LD, LD.n), (VE, VE.n)],
                         sem=(PE if (dc == 7 and nc2 == 1) else None))
            for dc in range(8):
                for nc2 in range(2):
                    b.mm(s_w[wb][:, 1024 + dc * 128: 1024 + dc * 128 + 128],
                         s_xn1[:, dc * NOWN + h0 + nc2 * CW: dc * NOWN + h0 + nc2 * CW + CW],
                         p_b2[nc2][:], dc == 0, dc == 7,
                         waits=[(LG, lw2), (VE, VE.n)],
                         sem=(PE if (dc == 7 and nc2 == 1) else None))
            pe_w = PE.n
            pe_use[wb] = PE.n
            for nc2 in range(2):
                b.ts("vector", s_t[:], p_a[nc2][:], s1c[:, jc:jc + 1], None, ALU.mult,
                     waits=[(PE, pe_w), (LD, l_sc)], sem=VE)
                b.act("scalar", s_sil[:], s_t[:], AF.Silu, waits=[(VE, VE.n)], sem=SE)
                b.ts("vector", s_u[:], p_b2[nc2][:], s2c[:, jc:jc + 1], None, ALU.mult,
                     sem=None)
                b.tt("vector", s_h0[:, jc * 1024 + nc2 * CW: jc * 1024 + nc2 * CW + CW],
                     s_u[:], s_sil[:], ALU.mult, waits=[(SE, SE.n)], sem=VE)
        def hidden_layer(wname, scl, src_t, dst, se_src=0):
            for oc in range(32):
                wb = oc % 2
                lw = b.dma("gpsimd", s_w[wb][:, 0:4096].rearrange("p (c f) -> p c f", c=32),
                           T[wname][:, :, oc * 128:(oc + 1) * 128].rearrange("c p f -> p c f"),
                           LG, waits=[(PE, pe_use[wb])])
                for ic in range(32):
                    for nc2 in range(2):
                        b.mm(s_w[wb][:, ic * 128:(ic + 1) * 128],
                             src_t[:, ic * 1024 + nc2 * CW: ic * 1024 + nc2 * CW + CW],
                             p_a[nc2][:], ic == 0, ic == 31,
                             waits=[(LG, lw), (VE, VE.n), (SE, se_src)],
                             sem=(PE if (ic == 31 and nc2 == 1) else None))
                pe_h = PE.n
                pe_use[wb] = PE.n
                for nc2 in range(2):
                    b.ts("vector", s_t[:], p_a[nc2][:], scl[:, oc:oc + 1], None, ALU.mult,
                         waits=[(PE, pe_h)], sem=VE)
                    b.act("scalar", dst[:, oc * 1024 + nc2 * CW: oc * 1024 + nc2 * CW + CW],
                          s_t[:], AF.Gelu, waits=[(VE, VE.n)], sem=SE)
        hidden_layer("h1T", sh1c, s_h0, s_h1)
        se_h1 = SE.n
        hidden_layer("h2T", sh2c, s_h1, s_h0, se_src=se_h1)
        se_h2 = SE.n
        for dc in range(8):
            wb = dc % 2
            lw = b.dma("gpsimd", s_w[wb][:, 0:4096].rearrange("p (c f) -> p c f", c=32),
                       T["w3T"][:, :, dc * 128:(dc + 1) * 128].rearrange("c p f -> p c f"),
                       LG, waits=[(PE, pe_use[wb])])
            for ic in range(32):
                for nc2 in range(2):
                    b.mm(s_w[wb][:, ic * 128:(ic + 1) * 128],
                         s_h0[:, ic * 1024 + nc2 * CW: ic * 1024 + nc2 * CW + CW],
                         p_a[nc2][:], ic == 0, ic == 31,
                         waits=[(LG, lw), (VE, VE.n), (SE, se_h2)],
                         sem=(PE if (ic == 31 and nc2 == 1) else None))
            pe_3 = PE.n
            pe_use[wb] = PE.n
            for nc2 in range(2):
                nn = h0 + nc2 * CW
                lx = b.dma("sync", s_xa[:], T["xacc_d"][dc][:, nn:nn + CW], LD,
                           waits=[(VE, VE.n)])
                b.stt("vector", s_x2[:], p_a[nc2][:], s3c[:, dc:dc + 1], s_xa[:],
                      ALU.mult, ALU.add,
                      waits=[(PE, pe_3), (LD, lx), (LD, l_sc), (OUT, OUT.n)], sem=VE)
                b.dma("sync", T["x2"][dc][:, nn:nn + CW], s_x2[:], OUT,
                      waits=[(VE, VE.n)])
        es.close()




def build_fused(dbg=False):
    nc = bass.Bass("TRN2", target_bir_lowering=False, debug=False)
    b = Builder(nc)
    T = {}

    def din(name, shape, dtype=F32):
        T[name] = nc.dram_tensor(name, shape, dtype, kind="ExternalInput").ap()

    def dscr(name, shape, dtype):
        kind = "ExternalOutput" if dbg else "Internal"
        T[name] = nc.dram_tensor(name, shape, dtype, kind=kind).ap()

    din("xT", [8, 128, L])
    for w in ("qwT", "kwT", "vwT", "owT"):
        din(w, [8, 128, D])
    din("qb_r", [1, D]); din("kb_r", [1, D]); din("vb_r", [1, D])
    din("ob_c", [128, 8]); din("qn_c", [128, 8]); din("kn_c", [128, 8])
    din("cosT", [L, HD]); din("sinT", [L, HD])
    din("ones", [128, 128]); din("iden", [128, 128]); din("consts", [128, 8])
    din("rwT", [8, 128, 8]); din("rb_r", [1, 8])
    din("guT", [8, 8, 128, 4096]); din("gub_c", [8, 128, 32])
    din("dnT", [8, 16, 128, 1024]); din("dnb", [8, 1024])
    din("inv1702", [1, 128])
    din("W1", [8, 128, 3 * D])
    din("w1T", [8, 128, 4096]); din("w2T", [8, 128, 4096])
    din("h1T", [32, 128, 4096]); din("h2T", [32, 128, 4096])
    din("w3T", [32, 128, 1024])
    din("s1_c", [128, 32]); din("s2_c", [128, 32])
    din("sh1_c", [128, 32]); din("sh2_c", [128, 32]); din("s3_c", [128, 8])
    T["x2"] = nc.dram_tensor("x2", [8, 128, NOWN], F32, kind="ExternalOutput").ap()
    dscr("q_fm", [8, 128, NOWN], BF16)
    dscr("k_fm", [8, 128, L], BF16)
    dscr("v_tm", [32, 128, H * 65], BF16)
    dscr("xn0", [8, 128, NOWN], BF16)
    dscr("k1_tm", [16, 128, D], BF16)
    dscr("v1_tm", [16, 128, D + 8], BF16)
    dscr("xacc_d", [8, 128, NOWN], F32)
    dscr("xn1_d", [8, 128, NOWN], BF16)
    if dbg:
        dscr("kvp_d", [8, 128, D + 8], F32)
        dscr("kvf_d", [8, 128, D + 8], BF16)
    cc_in_t = nc.dram_tensor("cc_in", [1024, 1032], F32, kind="Internal")
    cc_out_t = nc.dram_tensor("cc_out", [1024, 1032], F32, kind="Internal")
    T["cc_in"] = cc_in_t.ap()
    T["cc_out"] = cc_out_t.ap()
    T["cc_in_opt"] = cc_in_t.ap().opt()
    T["cc_out_opt"] = cc_out_t.ap().opt()

    LD = b.sem("LD"); LG = b.sem("LG"); PE = b.sem("PE")
    VE = b.sem("VE"); SE = b.sem("SE"); OUT = b.sem("OUT"); CC = b.sem("CC")

    al = b.es.enter_context
    s_ones_f = al(nc.sbuf_tensor("s_ones_f", [128, 128], F32))
    s_ones_b = al(nc.sbuf_tensor("s_ones_b", [128, 128], BF16))
    s_id_b = al(nc.sbuf_tensor("s_id_b", [128, 128], BF16))
    s_id_f = al(nc.sbuf_tensor("s_id_f", [128, 128], F32))
    s_consts = al(nc.sbuf_tensor("s_consts", [128, 8], F32))
    s_qn = al(nc.sbuf_tensor("s_qn", [128, 8], F32))
    s_kn = al(nc.sbuf_tensor("s_kn", [128, 8], F32))
    s_ob = al(nc.sbuf_tensor("s_ob", [128, 8], F32))
    s_inv1702 = al(nc.sbuf_tensor("s_inv1702", [1, 128], F32))

    b.dma("sync", s_ones_f[:], T["ones"], LD)
    b.dma("sync", s_id_f[:], T["iden"], LD)
    b.dma("sync", s_consts[:], T["consts"], LD)
    b.dma("sync", s_qn[:], T["qn_c"], LD)
    b.dma("sync", s_kn[:], T["kn_c"], LD)
    b.dma("sync", s_ob[:], T["ob_c"], LD)
    b.dma("sync", s_inv1702[:], T["inv1702"], LD)
    b.dma("gpsimd", s_ones_b[:], T["ones"], LG)
    b.dma("gpsimd", s_id_b[:], T["iden"], LG)

    eps_ln = s_consts[0:1, 0:1]
    eps_rms = s_consts[0:1, 1:2]

    _phase_l0_pre(nc, b, T, LD, LG, PE, VE, SE, OUT,
                  s_ones_f, s_ones_b, s_id_b, s_qn, s_kn, eps_ln, eps_rms)
    b.barrier()
    es_x = ExitStack()
    x_acc = es_x.enter_context(nc.sbuf_tensor("x_acc", [128, 8 * NOWN], F32))
    _phase_l0_attn(nc, b, T, LD, LG, PE, VE, SE, OUT, s_ones_f, s_ob, x_acc)
    b.barrier()
    _phase_l0_moe(nc, b, T, LD, LG, PE, VE, SE, OUT,
                  s_ones_f, s_id_f, s_inv1702, x_acc, store_x1=False)
    b.barrier()

    es_q = ExitStack()
    s_q1 = es_q.enter_context(nc.sbuf_tensor("b_q1", [128, 8 * NOWN], BF16))

    _phase_l1_pre_f(nc, b, T, LD, LG, PE, VE, SE, OUT, CC,
                    s_ones_f, s_ones_b, eps_ln, None, s_q1, x_acc)
    b.barrier()
    es_attn = ExitStack()
    s_kvb = es_attn.enter_context(nc.sbuf_tensor("b_kvb", [128, 8 * (D + 8)], BF16))
    s_ksb = None
    _phase_l1_kv_f(nc, b, T, LD, LG, PE, VE, SE, OUT, CC, s_kvb, s_ksb)
    b.barrier()
    _phase_l1_attnfin_f(nc, b, T, LD, LG, PE, VE, SE, OUT,
                        s_ones_f, s_q1, s_kvb, s_ksb, x_acc)
    b.barrier()
    es_attn.close()
    es_q.close()
    es_x.close()
    _phase_l1_ffn(nc, b, T, LD, LG, PE, VE, SE, OUT, None)
    b.wait("sync", [(OUT, OUT.n)])
    b.finalize()
    return nc


def _phase_l1_pre_f(nc, b, T, LD, LG, PE, VE, SE, OUT, CC,
                    s_ones_f, s_ones_b, eps_ln, s_xn1, s_q1, x_acc):
    es = ExitStack()
    al = es.enter_context
    if s_xn1 is None:
        s_xn1 = al(nc.sbuf_tensor("f1_xn1", [128, 8 * NOWN], BF16))
        spill_xn1 = True
    else:
        spill_xn1 = False
    s_W1 = al(nc.sbuf_tensor("f1_W1", [128, 8 * 3 * D], BF16))
    s_xb = al(nc.sbuf_tensor("f1_xb", [128, 8 * CW], BF16))
    s_sq = al(nc.sbuf_tensor("f1_sq", [128, 8 * CW], BF16))
    s_t = al(nc.sbuf_tensor("f1_t", [128, CW], F32))
    s_e = al(nc.sbuf_tensor("f1_e", [128, CW], F32))
    s_rows = al(nc.sbuf_tensor("f1_rows", [1, 2 * CW], F32))
    s_ktm = al(nc.sbuf_tensor("f1_ktm", [128, D], BF16))
    s_vtm = al(nc.sbuf_tensor("f1_vtm", [128, D + 8], BF16))
    p_sA = al(nc.psum_tensor("f1_sA", [1, CW], F32))
    p_sB = al(nc.psum_tensor("f1_sB", [1, CW], F32))
    p_m = al(nc.psum_tensor("f1_m", [128, CW], F32))
    p_r = al(nc.psum_tensor("f1_r", [128, CW], F32))
    p_k = al(nc.psum_tensor("f1_k", [128, 2 * CW], F32))

    mrow = s_rows[0:1, 0:CW]
    rstd = s_rows[0:1, CW:2 * CW]
    trow = s_e[0:1, 0:CW]
    erow = rstd
    varr = trow

    for dc in range(8):
        b.dma("gpsimd", s_W1[:, dc * 3 * D:(dc + 1) * 3 * D], T["W1"][dc], LG)
    lg_w = LG.n

    for ch in range(4):
        c0 = ch * CW
        xsl = x_acc[:].rearrange("p (c n) -> p c n", c=8)[:, :, c0:c0 + CW]
        b.copy("vector", s_xb[:].rearrange("p (c w) -> p c w", c=8), xsl,
               waits=[(PE, PE.n), (SE, SE.n)], sem=VE)
        lvb = VE.n
        sv = b.act("scalar", s_sq[:], s_xb[:], AF.Square, waits=[(VE, lvb)], sem=SE)
        for dc in range(8):
            b.mm(s_ones_b[:, 0:1], s_xb[:, dc * CW:(dc + 1) * CW], p_sA[:],
                 dc == 0, dc == 7, waits=[(VE, lvb), (VE, VE.n)],
                 sem=(PE if dc == 7 else None))
        for dc in range(8):
            b.mm(s_ones_b[:, 0:1], s_sq[:, dc * CW:(dc + 1) * CW], p_sB[:],
                 dc == 0, dc == 7, waits=[(SE, sv), (VE, VE.n)],
                 sem=(PE if dc == 7 else None))
        pe_stats = PE.n
        b.ts("vector", mrow, p_sA[:], 1.0 / D, None, ALU.mult, waits=[(PE, pe_stats)], sem=VE)
        b.ts("vector", erow, p_sB[:], 1.0 / D, None, ALU.mult, waits=[(PE, pe_stats)], sem=VE)
        b.act("scalar", trow, mrow, AF.Square, waits=[(VE, VE.n)], sem=SE)
        b.tt("vector", varr, erow, trow, ALU.subtract, waits=[(SE, SE.n)], sem=VE)
        b.act("scalar", trow, varr, AF.Ln, bias=eps_ln, waits=[(VE, VE.n)], sem=SE)
        b.act("scalar", rstd, trow, AF.Exp, scale=-0.5, waits=[(SE, SE.n)], sem=SE)
        b.mm(s_ones_f[0:1, :], mrow, p_m[:, 0:CW], True, True,
             waits=[(VE, VE.n)], sem=PE)
        b.mm(s_ones_f[0:1, :], rstd, p_r[:, 0:CW], True, True,
             waits=[(SE, SE.n), (VE, VE.n)], sem=PE)
        pe_bc = PE.n
        for dc in range(8):
            sl = slice(dc * CW, (dc + 1) * CW)
            b.tt("vector", s_t[:], xsl[:, dc, :], p_m[:, 0:CW], ALU.subtract,
                 waits=[(PE, pe_bc), (PE, pe_stats), (SE, sv)], sem=None)
            b.tt("vector", s_xb[:, sl], s_t[:], p_r[:, 0:CW], ALU.mult,
                 sem=(VE if dc == 7 else None))
        v_xn = VE.n
        b.copy("vector", s_xn1[:].rearrange("p (c w) -> p c w", c=8)[:, :, c0:c0 + CW],
               s_xb[:].rearrange("p (c w) -> p c w", c=8), sem=VE)
        for oc in range(8):
            for dc in range(8):
                b.mm(s_W1[:, dc * 3 * D + oc * 128: dc * 3 * D + oc * 128 + 128],
                     s_xb[:, dc * CW:(dc + 1) * CW],
                     p_k[:, 0:CW], dc == 0, dc == 7,
                     waits=[(LG, lg_w), (VE, VE.n)],
                     sem=(PE if dc == 7 else None))
            pe_q = PE.n
            b.ts("vector", s_t[:], p_k[:, 0:CW], 0.0, None, ALU.min,
                 waits=[(PE, pe_q)], sem=VE)
            b.act("scalar", s_e[:], s_t[:], AF.Exp, waits=[(VE, VE.n)], sem=SE)
            b.stt("vector",
                  s_q1[:, oc * NOWN + c0: oc * NOWN + c0 + CW],
                  p_k[:, 0:CW], 1.0, s_e[:], ALU.add, ALU.max,
                  waits=[(SE, SE.n)], sem=VE)
        for nt in range(4):
            for (col0, dst, do_elu) in ((D, s_ktm, True), (2 * D, s_vtm, False)):
                for dc in range(8):
                    for oc in range(2):
                        b.mm(s_xb[:, dc * CW + nt * 128: dc * CW + nt * 128 + 128],
                             s_W1[:, dc * 3 * D + col0 + oc * CW: dc * 3 * D + col0 + oc * CW + CW],
                             p_k[:, oc * CW:(oc + 1) * CW], dc == 0, dc == 7,
                             waits=[(VE, v_xn), (VE, VE.n), (LG, lg_w)],
                             sem=(PE if (dc == 7 and oc == 1) else None))
                pe_p = PE.n
                if do_elu:
                    for oc2 in range(2):
                        psl = p_k[:, oc2 * CW:(oc2 + 1) * CW]
                        b.ts("vector", s_t[:], psl, 0.0, None, ALU.min,
                             waits=[(PE, pe_p)], sem=VE)
                        b.act("scalar", s_e[:], s_t[:], AF.Exp, waits=[(VE, VE.n)], sem=SE)
                        b.stt("vector", dst[:, oc2 * CW:(oc2 + 1) * CW], psl,
                              1.0, s_e[:], ALU.add, ALU.max,
                              waits=[(SE, SE.n), (OUT, OUT.n)], sem=VE)
                else:
                    b.copy("vector", dst[:, 0:D], p_k[:, 0:D],
                           waits=[(PE, pe_p), (OUT, OUT.n)], sem=VE)
                    b.memset("vector", dst[:, D:D + 8], 0.0, sem=None)
                    b.memset("vector", dst[:, D:D + 1], 1.0, sem=VE)
                v_d = VE.n
                if do_elu:
                    b.dma("sync", T["k1_tm"][ch * 4 + nt], dst[:], OUT,
                          waits=[(VE, v_d)])
                else:
                    b.dma("sync", T["v1_tm"][ch * 4 + nt], dst[:], OUT,
                          waits=[(VE, v_d)])
    if spill_xn1:
        b.dma("sync", T["xn1_d"][:, :, :].rearrange("c p w -> p c w"),
              s_xn1[:].rearrange("p (c w) -> p c w", c=8), OUT, waits=[(VE, VE.n)])
    es.close()


def _phase_l1_kv_f(nc, b, T, LD, LG, PE, VE, SE, OUT, CC, s_kvb, s_ksb):
    es = ExitStack()
    al = es.enter_context
    CA = D + 8
    s_kt = [al(nc.sbuf_tensor(f"f2_kt{i}", [128, D], BF16)) for i in range(2)]
    s_vt = [al(nc.sbuf_tensor(f"f2_vt{i}", [128, CA], BF16)) for i in range(2)]
    s_kv = al(nc.sbuf_tensor("f2_kv", [128, 8 * CA], F32))
    p_kv = [al(nc.psum_tensor(f"f2_kv{i}", [128, CW], F32)) for i in range(8)]

    for vc in range(3):
        c0 = vc * CW
        cw = CW if vc < 2 else 8
        for lt in range(16):
            bu = lt % 2
            lk = b.dma("sync", s_kt[bu][:], T["k1_tm"][lt], LD,
                       waits=[(PE, PE.n)])
            lv = b.dma("sync", s_vt[bu][:], T["v1_tm"][lt], LD)
            for dc in range(8):
                b.mm(s_kt[bu][:, dc * 128:(dc + 1) * 128],
                     s_vt[bu][:, c0:c0 + cw],
                     p_kv[dc][:, 0:cw], lt == 0, lt == 15,
                     waits=[(LD, lv), (VE, VE.n)],
                     sem=(PE if dc == 7 else None))
        for dc in range(8):
            b.copy("vector", s_kv[:, dc * CA + c0: dc * CA + c0 + cw],
                   p_kv[dc][:, 0:cw], waits=[(PE, PE.n)], sem=VE)
    b.dma("sync", T["cc_in"][:, :].rearrange("(c p) h -> p c h", p=128),
          s_kv[:].rearrange("p (c h) -> p c h", c=8), OUT, waits=[(VE, VE.n)])
    if "kvp_d" in T:
        b.dma("sync", T["kvp_d"][:, :, :].rearrange("c p h -> p c h"),
              s_kv[:].rearrange("p (c h) -> p c h", c=8), OUT, waits=[(VE, VE.n)])
    def cc_fn(e):
        return e.collective_compute(
            "AllReduce", ALU.add,
            replica_groups=[[0, 1], [2, 3], [4, 5], [6, 7]],
            ins=[T["cc_in_opt"]], outs=[T["cc_out_opt"]],
        )
    b._emit("gpsimd", cc_fn, [(OUT, OUT.n)], CC, 1)
    b.seen["gpsimd"].pop(CC, None)
    b.dma("gpsimd", s_kvb[:].rearrange("p (c h) -> p c h", c=8),
          T["cc_out"][:, :].rearrange("(c p) h -> p c h", p=128), LG,
          waits=[(CC, CC.n)])
    if "kvf_d" in T:
        b.dma("sync", T["kvf_d"][:, :, :].rearrange("c p h -> p c h"),
              s_kvb[:].rearrange("p (c h) -> p c h", c=8), OUT, waits=[(LG, LG.n)])
    es.close()


def _phase_l1_attnfin_f(nc, b, T, LD, LG, PE, VE, SE, OUT,
                        s_ones_f, s_q1, s_kvb, s_ksb, x_acc):
    es = ExitStack()
    al = es.enter_context
    CA = D + 8
    s_zr = al(nc.sbuf_tensor("f3_zr", [1, 2 * CW], F32))
    s_zb4 = al(nc.sbuf_tensor("f3_zb4", [128, 4 * CW], F32))
    s_xa = al(nc.sbuf_tensor("f3_xa", [128, CW], F32))
    p_z = al(nc.psum_tensor("f3_z", [1, CW], F32))
    p_zb = al(nc.psum_tensor("f3_zbp", [128, CW], F32))
    p_V = [al(nc.psum_tensor(f"f3_V{i}", [128, CW], F32)) for i in range(4)]

    for ncx in range(4):
        n0 = ncx * CW
        for dc in range(8):
            b.mm(s_kvb[:, dc * CA + D: dc * CA + D + 1],
                 s_q1[:, dc * NOWN + n0: dc * NOWN + n0 + CW],
                 p_z[:], dc == 0, dc == 7, waits=[(LG, LG.n), (VE, VE.n)],
                 sem=(PE if dc == 7 else None))
        b.strict = True
        b.ts("vector", s_zr[0:1, 0:CW], p_z[:], 1e-6, None, ALU.add,
             waits=[(PE, PE.n), (VE, VE.n)], sem=VE)
        b.recip(s_zr[0:1, CW:2 * CW], s_zr[0:1, 0:CW], waits=[(VE, VE.n)], sem=VE)
        b.strict = False
        b.mm(s_ones_f[0:1, :], s_zr[0:1, CW:2 * CW], p_zb[:], True, True,
             waits=[(VE, VE.n)], sem=PE)
        b.copy("vector", s_zb4[:, n0:n0 + CW], p_zb[:], waits=[(PE, PE.n)], sem=VE)
    v_zb = VE.n
    for hc in range(8):
        for dc in range(8):
            for ncx in range(4):
                b.mm(s_kvb[:, dc * CA + hc * 128: dc * CA + hc * 128 + 128],
                     s_q1[:, dc * NOWN + ncx * CW: dc * NOWN + ncx * CW + CW],
                     p_V[ncx][:], dc == 0, dc == 7,
                     waits=[(VE, VE.n), (LG, LG.n)],
                     sem=(PE if (dc == 7 and ncx == 3) else None))
        pe_V = PE.n
        for ncx in range(4):
            n0 = ncx * CW
            b.tt("vector", s_xa[:], p_V[ncx][:], s_zb4[:, n0:n0 + CW], ALU.mult,
                 waits=[(PE, pe_V), (VE, v_zb), (OUT, OUT.n)], sem=None)
            acc = x_acc[:, hc * NOWN + n0: hc * NOWN + n0 + CW]
            b.tt("vector", acc, acc, s_xa[:], ALU.add, sem=(VE if ncx == 3 else None))
    b.dma("sync", T["xacc_d"][:, :, :].rearrange("c p w -> p c w"),
          x_acc[:].rearrange("p (c w) -> p c w", c=8), OUT, waits=[(VE, VE.n)])
    es.close()


def _in_map_A(hp, core):
    m = {"xT": hp["xT_core"][core],
         "cosT": hp["cosT_core"][core],
         "sinT": hp["sinT_core"][core]}
    for k in ("qwT", "kwT", "vwT", "owT", "qb_r", "kb_r", "vb_r", "ob_c", "qn_c",
              "kn_c", "ones", "iden", "consts", "rwT", "rb_r",
              "guT", "gub_c", "dnT", "dnb", "inv1702"):
        m[k] = hp[k]
    return m


_NC_F = None


def _get_fused():
    global _NC_F
    if _NC_F is None:
        _NC_F = build_fused(dbg=False)
    return _NC_F


def _in_map_F(hp, core):
    m = {"xT": hp["xT_core"][core],
         "cosT": hp["cosT_core"][core],
         "sinT": hp["sinT_core"][core]}
    for k in ("qwT", "kwT", "vwT", "owT", "qb_r", "kb_r", "vb_r", "ob_c", "qn_c",
              "kn_c", "ones", "iden", "consts", "rwT", "rb_r",
              "guT", "gub_c", "dnT", "dnb", "inv1702",
              "W1", "w1T", "w2T", "h1T", "h2T", "w3T", "s1_c", "s2_c",
              "sh1_c", "sh2_c", "s3_c"):
        m[k] = hp[k]
    return m


def kernel(**inputs):
    hp = host_prep(inputs)
    nc_f = _get_fused()
    in_maps = [_in_map_F(hp, c) for c in range(NCORE)]
    res = run_bass_kernel_spmd(nc_f, in_maps, core_ids=list(range(NCORE)))
    out = np.empty((B, L, D), dtype=np.float32)
    for c in range(NCORE):
        bi, hf = c // 2, c % 2
        x2 = res.results[c]["x2"].reshape(D, NOWN)
        out[bi, hf * NOWN:(hf + 1) * NOWN, :] = x2.T
    return out




# version 07 (final)
# speedup vs baseline: 1.4525x; 1.0168x over previous
import numpy as np
from contextlib import ExitStack

import concourse.bass as bass
import concourse.mybir as mybir
from concourse.bass import broadcast_tensor_aps
from concourse.bass_utils import run_bass_kernel_spmd

F32 = mybir.dt.float32
BF16 = mybir.dt.bfloat16
AF = mybir.ActivationFunctionType
ALU = mybir.AluOpType
AX = mybir.AxisListType.X

ENGS = ("sync", "tensor", "vector", "scalar", "gpsimd")

B, L, D = 4, 4096, 1024
H, HD = 16, 64
NCORE = 8
NOWN = 2048
CW = 512


class SemC:

    def __init__(self, h):
        self.h = h
        self.n = 0


class Builder:
    def __init__(self, nc):
        self.nc = nc
        self.es = ExitStack()
        self.sched = {e: [] for e in ENGS}
        self.seen = {e: {} for e in ENGS}
        self.sems = {}

    def sem(self, name):
        s = SemC(self.es.enter_context(self.nc.semaphore(name)))
        self.sems[name] = s
        return s

    strict = False

    def _emit(self, eng, fn, waits, sem, inc):
        tgt = None
        if sem is not None:
            sem.n += inc
            tgt = sem.n
        wl = []
        for ws, wv in waits:
            if wv is None:
                wv = ws.n
            if self.strict or self.seen[eng].get(ws, -1) < wv:
                wl.append((ws, wv))
                self.seen[eng][ws] = max(self.seen[eng].get(ws, -1), wv)
        if sem is not None:
            self.seen[eng][sem] = max(self.seen[eng].get(sem, -1), tgt)

        def f2(e, fn=fn, wl=tuple(wl), sh=(sem.h if sem else None), inc=inc):
            for ws, wv in wl:
                e.wait_ge(ws.h, wv)
            inst = fn(e)
            if sh is not None and inst is not None:
                inst.then_inc(sh, inc)
            return inst

        self.sched[eng].append(f2)
        return tgt

    def dma(self, eng, out, in_, sem, waits=(), allow_slow=False):
        def fn(e, o=out, i=in_, asl=allow_slow):
            return e.dma_start(out=o, in_=i, allow_slow_non_contiguous=asl)
        return self._emit(eng, fn, waits, sem, 16)

    def mm(self, lhsT, rhs, out, start, stop, waits=(), sem=None, inc=1,
           tile_position=None, skip_group_check=False):
        def fn(e, o=out, l=lhsT, r=rhs, st=start, sp=stop, tp=tile_position,
               sg=skip_group_check):
            return e.matmul(o, l, r, start=st, stop=sp, tile_position=tp,
                            skip_group_check=sg)
        return self._emit("tensor", fn, waits, sem, inc)

    def trans(self, out, in_, iden, waits=(), sem=None, inc=1):
        def fn(e, o=out, i=in_, d=iden):
            return e.transpose(o, i, d)
        return self._emit("tensor", fn, waits, sem, inc)

    def act(self, eng, out, in_, func, bias=0.0, scale=1.0, accum_out=None,
            waits=(), sem=None, inc=1):
        def fn(e, o=out, i=in_, f=func, b=bias, s=scale, a=accum_out):
            return e.activation(o, i, f, bias=b, scale=s, accum_out=a)
        return self._emit(eng, fn, waits, sem, inc)

    def tt(self, eng, out, in0, in1, op, waits=(), sem=None, inc=1):
        def fn(e, o=out, a=in0, b=in1, op=op):
            return e.tensor_tensor(o, a, b, op)
        return self._emit(eng, fn, waits, sem, inc)

    def ts(self, eng, out, in0, s1, s2, op0, op1=ALU.bypass, waits=(), sem=None, inc=1):
        def fn(e, o=out, i=in0, s1=s1, s2=s2, op0=op0, op1=op1):
            return e.tensor_scalar(o, i, s1, s2, op0, op1)
        return self._emit(eng, fn, waits, sem, inc)

    def stt(self, eng, out, in0, scalar, in1, op0, op1, waits=(), sem=None, inc=1):
        def fn(e, o=out, a=in0, s=scalar, b=in1, op0=op0, op1=op1):
            return e.scalar_tensor_tensor(o, a, s, b, op0, op1)
        return self._emit(eng, fn, waits, sem, inc)

    def copy(self, eng, out, in_, waits=(), sem=None, inc=1):
        def fn(e, o=out, i=in_):
            return e.tensor_copy(o, i)
        return self._emit(eng, fn, waits, sem, inc)

    def memset(self, eng, ap, val, waits=(), sem=None, inc=1):
        def fn(e, a=ap, v=val):
            return e.memset(a, v)
        return self._emit(eng, fn, waits, sem, inc)

    def red(self, eng, out, in_, op, negate=False, waits=(), sem=None, inc=1):
        def fn(e, o=out, i=in_, op=op, ng=negate):
            return e.tensor_reduce(o, i, AX, op, negate=ng)
        return self._emit(eng, fn, waits, sem, inc)

    def recip(self, out, in_, waits=(), sem=None, inc=1):
        def fn(e, o=out, i=in_):
            return e.reciprocal(o, i)
        return self._emit("vector", fn, waits, sem, inc)

    def wait(self, eng, waits):
        self._emit(eng, lambda e: None, waits, None, 0)

    def barrier(self, engs=ENGS):
        waits = [(s, s.n) for s in self.sems.values() if s.n > 0]
        for e in engs:
            self.wait(e, waits)

    def finalize(self):
        nc = self.nc
        with nc.Block() as block:
            for eng in ENGS:
                fns = self.sched[eng]
                if not fns:
                    continue
                getattr(block, eng)(lambda e, fns=fns: [f(e) for f in fns])
        self.es.close()



def _rope_tables():
    inv_freq = (1.0 / (np.float32(10000.0) **
                       (np.arange(0, HD, 2, dtype=np.float32) / np.float32(HD)))).astype(np.float32)
    t = np.arange(L, dtype=np.float32)
    freqs = t[:, None] * inv_freq[None, :]
    emb = np.concatenate([freqs, freqs], axis=-1)
    return np.cos(emb).astype(np.float32), np.sin(emb).astype(np.float32)


def _fmT(w):
    wt = np.ascontiguousarray(w.T.astype(np.float32, copy=False))
    return wt.reshape(wt.shape[0] // 128, 128, wt.shape[1])


def _cols(v):
    return np.ascontiguousarray(v.reshape(-1, 128).T.astype(np.float32, copy=False))


def host_prep(inp):
    hp = {}
    x = inp["L0_attn_x"]
    xT = np.ascontiguousarray(x.transpose(0, 2, 1))
    xTc = []
    for c in range(NCORE):
        bi, hf = c // 2, c % 2
        xr = np.concatenate([xT[bi][:, hf * NOWN:], xT[bi][:, :hf * NOWN]], axis=1)
        xTc.append(np.ascontiguousarray(xr).reshape(8, 128, L))
    hp["xT_core"] = xTc
    cos, sin = _rope_tables()
    hp["cosT_core"] = [np.ascontiguousarray(np.concatenate(
        [cos[(c % 2) * NOWN:], cos[:(c % 2) * NOWN]], axis=0)) for c in range(NCORE)]
    hp["sinT_core"] = [np.ascontiguousarray(np.concatenate(
        [sin[(c % 2) * NOWN:], sin[:(c % 2) * NOWN]], axis=0)) for c in range(NCORE)]
    hp["qwT"] = _fmT(inp["L0_attn_q_w"])
    hp["kwT"] = _fmT(inp["L0_attn_k_w"])
    hp["vwT"] = _fmT(inp["L0_attn_v_w"])
    hp["owT"] = _fmT(inp["L0_attn_o_w"])
    hp["qb_r"] = np.ascontiguousarray(inp["L0_attn_q_b"].reshape(1, D))
    hp["kb_r"] = np.ascontiguousarray(inp["L0_attn_k_b"].reshape(1, D))
    hp["vb_r"] = np.ascontiguousarray(inp["L0_attn_v_b"].reshape(1, D))
    hp["ob_c"] = _cols(inp["L0_attn_o_b"])
    hp["qn_c"] = _cols(inp["L0_attn_qn_scale"])
    hp["kn_c"] = _cols(inp["L0_attn_kn_scale"])
    hp["ones"] = np.ones((128, 128), dtype=np.float32)
    hp["iden"] = np.eye(128, dtype=np.float32)
    cst = np.zeros((128, 8), dtype=np.float32)
    cst[:, 0] = 1e-5
    cst[:, 1] = 1e-8
    cst[:, 2] = 1e-6
    hp["consts"] = cst
    hp["rwT"] = _fmT(inp["L0_ffn_router_weight"])
    hp["rb_r"] = np.ascontiguousarray(inp["L0_ffn_router_bias"].reshape(1, 8))
    gu = inp["L0_ffn_gate_up_proj"]
    hp["guT"] = np.ascontiguousarray(gu.transpose(0, 2, 1)).reshape(8, 8, 128, 2 * 2048)
    hp["gub_c"] = np.ascontiguousarray(
        inp["L0_ffn_gate_up_proj_bias"].reshape(8, 32, 128).transpose(0, 2, 1))
    dn = inp["L0_ffn_down_proj"]
    hp["dnT"] = np.ascontiguousarray(dn.transpose(0, 2, 1)).reshape(8, 16, 128, 1024)
    hp["dnb"] = np.ascontiguousarray(inp["L0_ffn_down_proj_bias"])
    hp["inv1702"] = np.full((1, 128), 1.0 / 1.702, dtype=np.float32)
    hp["W1"] = np.ascontiguousarray(inp["L1_attn_W"]).reshape(8, 128, 3 * D)
    hp["w1T"] = _fmT(inp["L1_ffn_w1_w"])
    hp["w2T"] = _fmT(inp["L1_ffn_w2_w"])
    hp["h1T"] = _fmT(inp["L1_ffn_h1_w"])
    hp["h2T"] = _fmT(inp["L1_ffn_h2_w"])
    hp["w3T"] = _fmT(inp["L1_ffn_w3_w"])
    hp["s1_c"] = _cols(inp["L1_ffn_w1_s"])
    hp["s2_c"] = _cols(inp["L1_ffn_w2_s"])
    hp["sh1_c"] = _cols(inp["L1_ffn_h1_s"])
    hp["sh2_c"] = _cols(inp["L1_ffn_h2_s"])
    hp["s3_c"] = _cols(inp["L1_ffn_w3_s"])
    return hp



def build_launch_A(dbg=False):
    nc = bass.Bass("TRN2", target_bir_lowering=False, debug=False)
    b = Builder(nc)
    T = {}

    def din(name, shape, dtype=F32):
        T[name] = nc.dram_tensor(name, shape, dtype, kind="ExternalInput").ap()

    def dout(name, shape, dtype=F32):
        T[name] = nc.dram_tensor(name, shape, dtype, kind="ExternalOutput").ap()

    def dscr(name, shape, dtype):
        kind = "ExternalOutput" if dbg else "Internal"
        T[name] = nc.dram_tensor(name, shape, dtype, kind=kind).ap()

    din("xT", [8, 128, L])
    for w in ("qwT", "kwT", "vwT", "owT"):
        din(w, [8, 128, D])
    din("qb_r", [1, D]); din("kb_r", [1, D]); din("vb_r", [1, D])
    din("ob_c", [128, 8]); din("qn_c", [128, 8]); din("kn_c", [128, 8])
    din("cosT", [L, HD]); din("sinT", [L, HD])
    din("ones", [128, 128]); din("iden", [128, 128]); din("consts", [128, 8])
    din("rwT", [8, 128, 8]); din("rb_r", [1, 8])
    din("guT", [8, 8, 128, 4096]); din("gub_c", [8, 128, 32])
    din("dnT", [8, 16, 128, 1024]); din("dnb", [8, 1024])
    din("inv1702", [1, 128])
    dout("x1", [8, 128, NOWN])
    dscr("q_fm", [8, 128, NOWN], BF16)
    dscr("k_fm", [8, 128, L], BF16)
    dscr("v_tm", [32, 128, H * 65], BF16)
    dscr("xn0", [8, 128, NOWN], BF16)
    if dbg:
        dscr("xp2", [8, 128, NOWN], F32)
        dscr("combT_d", [8, NOWN], F32)
        dscr("comb_d", [128, 128], F32)
        dscr("lg_d", [128, 128], F32)
        dscr("mr_d", [128, 64], F32)
        dscr("at_d", [8, 128, NOWN], BF16)

    LD = b.sem("LD")
    LG = b.sem("LG")
    PE = b.sem("PE")
    VE = b.sem("VE")
    SE = b.sem("SE")
    OUT = b.sem("OUT")

    al = b.es.enter_context

    s_ones_f = al(nc.sbuf_tensor("s_ones_f", [128, 128], F32))
    s_ones_b = al(nc.sbuf_tensor("s_ones_b", [128, 128], BF16))
    s_id_b = al(nc.sbuf_tensor("s_id_b", [128, 128], BF16))
    s_id_f = al(nc.sbuf_tensor("s_id_f", [128, 128], F32))
    s_consts = al(nc.sbuf_tensor("s_consts", [128, 8], F32))
    s_qn = al(nc.sbuf_tensor("s_qn", [128, 8], F32))
    s_kn = al(nc.sbuf_tensor("s_kn", [128, 8], F32))
    s_ob = al(nc.sbuf_tensor("s_ob", [128, 8], F32))
    s_inv1702 = al(nc.sbuf_tensor("s_inv1702", [1, 128], F32))

    b.dma("sync", s_ones_f[:], T["ones"], LD)
    b.dma("sync", s_id_f[:], T["iden"], LD)
    b.dma("sync", s_consts[:], T["consts"], LD)
    b.dma("sync", s_qn[:], T["qn_c"], LD)
    b.dma("sync", s_kn[:], T["kn_c"], LD)
    b.dma("sync", s_ob[:], T["ob_c"], LD)
    b.dma("sync", s_inv1702[:], T["inv1702"], LD)
    b.dma("gpsimd", s_ones_b[:], T["ones"], LG)
    b.dma("gpsimd", s_id_b[:], T["iden"], LG)

    ones1_b = s_ones_b[:, 0:1]
    ones1r_f = s_ones_f[0:1, :]
    eps_ln = s_consts[0:1, 0:1]
    eps_rms = s_consts[0:1, 1:2]

    _phase_l0_pre(nc, b, T, LD, LG, PE, VE, SE, OUT,
                  s_ones_f, s_ones_b, s_id_b, s_qn, s_kn, eps_ln, eps_rms)
    b.barrier()
    x_acc = al(nc.sbuf_tensor("x_acc", [128, 8 * NOWN], F32))
    _phase_l0_attn(nc, b, T, LD, LG, PE, VE, SE, OUT, s_ones_f, s_ob, x_acc)
    b.barrier()
    if dbg:
        b.dma("sync", T["xp2"][:, :, :].rearrange("c p w -> p c w"),
              x_acc[:].rearrange("p (c w) -> p c w", c=8), OUT)
        b.barrier()
    _phase_l0_moe(nc, b, T, LD, LG, PE, VE, SE, OUT,
                  s_ones_f, s_id_f, s_inv1702, x_acc)

    b.wait("sync", [(OUT, OUT.n)])
    b.finalize()
    return nc


def _phase_l0_pre(nc, b, T, LD, LG, PE, VE, SE, OUT,
                  s_ones_f, s_ones_b, s_id_b, s_qn, s_kn, eps_ln, eps_rms):
    es = ExitStack()
    al = es.enter_context
    s_qwT = al(nc.sbuf_tensor("p1_qwT", [128, 8 * D], BF16))
    s_kwT = al(nc.sbuf_tensor("p1_kwT", [128, 8 * D], BF16))
    s_vwT = al(nc.sbuf_tensor("p1_vwT", [128, 8 * D], BF16))
    s_qb_bc = al(nc.sbuf_tensor("p1_qb_bc", [128, D], F32))
    s_kb_bc = al(nc.sbuf_tensor("p1_kb_bc", [128, D], F32))
    s_vb_bc = al(nc.sbuf_tensor("p1_vb_bc", [128, D], F32))
    s_brow = al(nc.sbuf_tensor("p1_brow", [1, D], BF16))
    s_xf = al(nc.sbuf_tensor("p1_xf", [128, 8 * CW], F32))
    s_xb = al(nc.sbuf_tensor("p1_xb", [128, 8 * CW], BF16))
    s_sq = al(nc.sbuf_tensor("p1_sq", [128, 8 * CW], BF16))
    s_xnf = al(nc.sbuf_tensor("p1_xnf", [128, 8 * CW], F32))
    s_xnb = al(nc.sbuf_tensor("p1_xnb", [128, 8 * CW], BF16))
    s_kin = al(nc.sbuf_tensor("p1_kin", [128, 8 * CW], BF16))
    s_qin = al(nc.sbuf_tensor("p1_qin", [128, 8 * CW], BF16))
    s_t = al(nc.sbuf_tensor("p1_t", [128, CW], F32))
    s_rows = al(nc.sbuf_tensor("p1_rows", [1, 3 * CW], F32))
    s_cos = al(nc.sbuf_tensor("p1_cos", [128, HD], F32))
    s_sin = al(nc.sbuf_tensor("p1_sin", [128, HD], F32))
    s_ktm = al(nc.sbuf_tensor("p1_ktm", [128, D], F32))
    s_rot = al(nc.sbuf_tensor("p1_rot", [128, D], F32))
    s_r1 = al(nc.sbuf_tensor("p1_r1", [128, D], F32))
    s_r2 = al(nc.sbuf_tensor("p1_r2", [128, D], F32))
    s_ktb = al(nc.sbuf_tensor("p1_ktb", [128, D], BF16))
    s_ktr = al(nc.sbuf_tensor("p1_ktr", [128, D], BF16))
    s_vst = al(nc.sbuf_tensor("p1_vst", [128, H * 65], BF16))

    p_sA = al(nc.psum_tensor("p1_sA", [1, CW], F32))
    p_sB = al(nc.psum_tensor("p1_sB", [1, CW], F32))
    p_m = al(nc.psum_tensor("p1_m", [128, CW], F32))
    p_r = al(nc.psum_tensor("p1_r", [128, CW], F32))
    p_k = al(nc.psum_tensor("p1_k", [128, 2 * CW], F32))
    p_tr = al(nc.psum_tensor("p1_tr", [128, D], BF16))

    mrow = s_rows[0:1, 0:CW]
    rstd = s_rows[0:1, CW:2 * CW]
    trow = s_rows[0:1, 2 * CW:3 * CW]
    erow = rstd
    varr = trow
    rrms = rstd

    for (wname, sdst) in (("qwT", s_qwT), ("kwT", s_kwT), ("vwT", s_vwT)):
        for dc in range(8):
            b.dma("gpsimd", sdst[:, dc * D:(dc + 1) * D], T[wname][dc], LG)
    for (bname, bdst) in (("qb_r", s_qb_bc), ("kb_r", s_kb_bc), ("vb_r", s_vb_bc)):
        lv = b.dma("gpsimd", s_brow[:], T[bname], LG, waits=[(PE, PE.n)])
        for oc in range(2):
            b.mm(s_ones_b[0:1, :], s_brow[0:1, oc * CW:(oc + 1) * CW],
                 p_m[:, 0:CW], True, True, waits=[(LG, lv), (VE, VE.n)], sem=PE)
            b.copy("vector", bdst[:, oc * CW:(oc + 1) * CW], p_m[:, 0:CW],
                   waits=[(PE, PE.n)], sem=VE)

    for ch in range(8):
        c0 = ch * CW
        own = ch < 4
        lvx = b.dma("sync", s_xf[:].rearrange("p (c w) -> p c w", c=8),
                    T["xT"][:, :, c0:c0 + CW].rearrange("c p w -> p c w"), LD,
                    waits=[(PE, PE.n), (VE, VE.n), (SE, SE.n)])
        lvb = b.dma("gpsimd", s_xb[:].rearrange("p (c w) -> p c w", c=8),
                    T["xT"][:, :, c0:c0 + CW].rearrange("c p w -> p c w"), LG,
                    waits=[(PE, PE.n), (VE, VE.n), (SE, SE.n)])
        sv = b.act("scalar", s_sq[:], s_xb[:], AF.Square, waits=[(LG, lvb)], sem=SE)
        for dc in range(8):
            b.mm(s_ones_b[:, 0:1], s_xb[:, dc * CW:(dc + 1) * CW], p_sA[:],
                 dc == 0, dc == 7, waits=[(LG, lvb), (VE, VE.n)],
                 sem=(PE if dc == 7 else None))
        for dc in range(8):
            b.mm(s_ones_b[:, 0:1], s_sq[:, dc * CW:(dc + 1) * CW], p_sB[:],
                 dc == 0, dc == 7, waits=[(SE, sv), (VE, VE.n)],
                 sem=(PE if dc == 7 else None))
        pe_stats = PE.n
        b.ts("vector", mrow, p_sA[:], 1.0 / D, None, ALU.mult, waits=[(PE, pe_stats)], sem=VE)
        b.ts("vector", erow, p_sB[:], 1.0 / D, None, ALU.mult, waits=[(PE, pe_stats)], sem=VE)
        b.act("scalar", trow, mrow, AF.Square, waits=[(VE, VE.n)], sem=SE)
        b.tt("vector", varr, erow, trow, ALU.subtract, waits=[(SE, SE.n)], sem=VE)
        b.act("scalar", trow, varr, AF.Ln, bias=eps_ln, waits=[(VE, VE.n)], sem=SE)
        b.act("scalar", rstd, trow, AF.Exp, scale=-0.5, waits=[(SE, SE.n)], sem=SE)
        sv_rstd = SE.n
        b.mm(s_ones_f[0:1, :], mrow, p_m[:, 0:CW], True, True,
             waits=[(VE, VE.n)], sem=PE)
        b.mm(s_ones_f[0:1, :], rstd, p_r[:, 0:CW], True, True,
             waits=[(SE, sv_rstd), (VE, VE.n)], sem=PE)
        pe_bc = PE.n
        for dc in range(8):
            sl = slice(dc * CW, (dc + 1) * CW)
            b.tt("vector", s_t[:], s_xf[:, sl], p_m[:, 0:CW], ALU.subtract,
                 waits=[(PE, pe_bc), (LD, lvx)], sem=None)
            b.tt("vector", s_xnf[:, sl], s_t[:], p_r[:, 0:CW], ALU.mult, sem=None)
        b.copy("vector", s_xnb[:], s_xnf[:], waits=[(OUT, OUT.n)], sem=VE)
        v_xn = VE.n
        if own:
            b.dma("sync", T["xn0"][:, :, c0:c0 + CW].rearrange("c p w -> p c w"),
                  s_xnb[:].rearrange("p (c w) -> p c w", c=8), OUT, waits=[(VE, v_xn)])
        sv2 = b.act("scalar", s_sq[:], s_xnb[:], AF.Square, waits=[(VE, v_xn)], sem=SE)
        for dc in range(8):
            b.mm(s_ones_b[:, 0:1], s_sq[:, dc * CW:(dc + 1) * CW], p_sA[:],
                 dc == 0, dc == 7, waits=[(SE, sv2), (VE, VE.n)],
                 sem=(PE if dc == 7 else None))
        b.ts("vector", trow, p_sA[:], 1.0 / D, None, ALU.mult, waits=[(PE, PE.n)], sem=VE)
        b.act("scalar", trow, trow, AF.Ln, bias=eps_rms, waits=[(VE, VE.n)], sem=SE)
        b.act("scalar", rrms, trow, AF.Exp, scale=-0.5, waits=[(SE, SE.n)], sem=SE)
        b.mm(s_ones_f[0:1, :], rrms, p_r[:, 0:CW], True, True,
             waits=[(SE, SE.n), (VE, VE.n)], sem=PE)
        pe_rr = PE.n
        for dc in range(8):
            sl = slice(dc * CW, (dc + 1) * CW)
            b.tt("vector", s_t[:], s_xnf[:, sl], p_r[:, 0:CW], ALU.mult,
                 waits=[(PE, pe_rr)], sem=None)
            b.ts("vector", s_kin[:, sl], s_t[:], s_kn[:, dc:dc + 1], None, ALU.mult,
                 sem=None)
            if own:
                b.ts("vector", s_qin[:, sl], s_t[:], s_qn[:, dc:dc + 1], None, ALU.mult,
                     sem=(VE if dc == 7 else None))
            elif dc == 7:
                b.memset("vector", s_t[0:1, 0:1], 0.0, sem=VE)
        v_in = VE.n

        for nt in range(4):
            tok0 = c0 + nt * 128
            lvc = b.dma("sync", s_cos[:], T["cosT"][tok0:tok0 + 128, :], LD,
                        waits=[(VE, VE.n)])
            lvs = b.dma("sync", s_sin[:], T["sinT"][tok0:tok0 + 128, :], LD,
                        waits=[(VE, VE.n)])

            def proj_rope(s_in, s_w, s_bb, dram, dram_col0):
                for dc in range(8):
                    for oc in range(2):
                        b.mm(s_in[:, dc * CW + nt * 128: dc * CW + nt * 128 + 128],
                             s_w[:, dc * D + oc * CW: dc * D + oc * CW + CW],
                             p_k[:, oc * CW:(oc + 1) * CW], dc == 0, dc == 7,
                             waits=[(VE, v_in), (VE, VE.n)],
                             sem=(PE if (dc == 7 and oc == 1) else None))
                pe_p = PE.n
                b.tt("vector", s_ktm[:], p_k[:, 0:D], s_bb[:], ALU.add,
                     waits=[(PE, pe_p), (LD, lvs)], sem=VE)
                kp = s_ktm[:].rearrange("p (k two) -> p k two", two=2)
                rp = s_rot[:].rearrange("p (k two) -> p k two", two=2)
                b.ts("vector", rp[:, :, 0:1], kp[:, :, 1:2], -1.0, None, ALU.mult, sem=None)
                b.copy("vector", rp[:, :, 1:2], kp[:, :, 0:1], sem=None)
                kv_h = s_ktm[:].rearrange("p (h d) -> p h d", h=H)
                rv_h = s_rot[:].rearrange("p (h d) -> p h d", h=H)
                cs1, cb1 = broadcast_tensor_aps(kv_h, s_cos[:].rearrange("p (o d) -> p o d", o=1))
                b.tt("vector", s_r1[:].rearrange("p (h d) -> p h d", h=H), cs1, cb1,
                     ALU.mult, sem=None)
                ss1, sb1 = broadcast_tensor_aps(rv_h, s_sin[:].rearrange("p (o d) -> p o d", o=1))
                b.tt("vector", s_r2[:].rearrange("p (h d) -> p h d", h=H), ss1, sb1,
                     ALU.mult, sem=None)
                b.tt("vector", s_ktb[:], s_r1[:], s_r2[:], ALU.add, sem=VE)
                v_rope = VE.n
                for dc in range(8):
                    b.trans(p_tr[:, dc * 128:(dc + 1) * 128],
                            s_ktb[:, dc * 128:(dc + 1) * 128], s_id_b[:],
                            waits=[(VE, v_rope)], sem=(PE if dc == 7 else None))
                b.copy("vector", s_ktr[:], p_tr[:], waits=[(PE, PE.n), (OUT, OUT.n)], sem=VE)
                b.dma("sync", dram[:, :, dram_col0:dram_col0 + 128].rearrange("c p f -> p c f"),
                      s_ktr[:].rearrange("p (c f) -> p c f", c=8), OUT,
                      waits=[(VE, VE.n)])

            proj_rope(s_kin, s_kwT, s_kb_bc, T["k_fm"], tok0)
            if own:
                proj_rope(s_qin, s_qwT, s_qb_bc, T["q_fm"], tok0)

            for dc in range(8):
                for oc in range(2):
                    b.mm(s_xnb[:, dc * CW + nt * 128: dc * CW + nt * 128 + 128],
                         s_vwT[:, dc * D + oc * CW: dc * D + oc * CW + CW],
                         p_k[:, oc * CW:(oc + 1) * CW], dc == 0, dc == 7,
                         waits=[(VE, v_in), (VE, VE.n)],
                         sem=(PE if (dc == 7 and oc == 1) else None))
            pe_v = PE.n
            b.tt("vector", s_ktm[:], p_k[:, 0:D], s_vb_bc[:], ALU.add,
                 waits=[(PE, pe_v)], sem=VE)
            b.copy("vector",
                   s_vst[:].rearrange("p (h c) -> p h c", c=65)[:, :, 0:64],
                   s_ktm[:].rearrange("p (h d) -> p h d", d=64),
                   waits=[(OUT, OUT.n)], sem=None)
            b.memset("vector", s_vst[:].rearrange("p (h c) -> p h c", c=65)[:, :, 64:65],
                     1.0, sem=VE)
            b.dma("sync", T["v_tm"][ch * 4 + nt], s_vst[:], OUT, waits=[(VE, VE.n)])
    es.close()


def _phase_l0_attn(nc, b, T, LD, LG, PE, VE, SE, OUT, s_ones_f, s_ob, x_acc):
    es = ExitStack()
    al = es.enter_context
    NK = 4
    s_kp = [al(nc.sbuf_tensor(f"p2_kp{i}", [128, L], BF16)) for i in range(2)]
    s_qp = [al(nc.sbuf_tensor(f"p2_qp{i}", [128, NOWN], BF16)) for i in range(2)]
    s_vp = [al(nc.sbuf_tensor(f"p2_vp{i}", [128, 32 * 130], BF16)) for i in range(2)]
    s_w = [al(nc.sbuf_tensor(f"p2_w{i}", [128, CW], BF16)) for i in range(NK)]
    s_avc = [al(nc.sbuf_tensor(f"p2_avc{i}", [65, CW], F32)) for i in range(2)]
    s_rs = al(nc.sbuf_tensor("p2_rs", [1, CW], F32))
    s_rsb = al(nc.sbuf_tensor("p2_rsb", [64, CW], F32))
    s_at = al(nc.sbuf_tensor("p2_at", [128, 8 * NOWN], BF16))
    s_owT = al(nc.sbuf_tensor("p2_owT", [128, 8 * D], BF16))
    s_x0 = al(nc.sbuf_tensor("p2_x0", [128, 4 * CW], F32))

    p_s = [al(nc.psum_tensor(f"p2_s{i}", [128, CW], F32)) for i in range(2)]
    p_av = al(nc.psum_tensor("p2_av", [65, CW], F32))
    p_b = al(nc.psum_tensor("p2_b", [64, CW], F32))
    p_o = [al(nc.psum_tensor(f"p2_o{i}", [128, CW], F32)) for i in range(4)]

    for dc in range(8):
        b.dma("gpsimd", s_owT[:, dc * D:(dc + 1) * D], T["owT"][dc], LG)
    lg_ow = LG.n

    sig_done = {}
    av_done = {}
    pe_use = {0: 0, 1: 0}
    pend_norm = [None]
    avc_free = {}

    for hp_i in range(8):
        buf = hp_i % 2
        lk = b.dma("sync", s_kp[buf][:], T["k_fm"][hp_i], LD,
                   waits=[(PE, pe_use[buf])])
        lq = b.dma("sync", s_qp[buf][:], T["q_fm"][hp_i], LD)
        lv = b.dma("sync",
                   s_vp[buf][:].rearrange("p (t c) -> p t c", t=32),
                   T["v_tm"][:, :, 2 * hp_i * 65:(2 * hp_i + 2) * 65].rearrange("t p c -> p t c"), LD)
        for hh in range(2):
            base = 64 * hh
            for qc in range(4):
                q0 = qc * CW
                for kt in range(32):
                    slot = kt % NK
                    ps = p_s[kt % 2]
                    b.mm(s_kp[buf][base:base + 64, kt * 128:(kt + 1) * 128],
                         s_qp[buf][base:base + 64, q0:q0 + CW],
                         ps[:], True, True,
                         waits=[(LD, lv), (SE, sig_done.get(kt % 2, 0))],
                         sem=PE, tile_position=(base, 0))
                    pe_sc = PE.n
                    b.act("scalar", s_w[slot][:], ps[:], AF.Sigmoid, scale=0.125,
                          waits=[(PE, pe_sc), (PE, av_done.get(slot, 0))], sem=SE)
                    sig_done[kt % 2] = SE.n
                    sg = SE.n
                    b.mm(s_vp[buf][:, kt * 130 + hh * 65: kt * 130 + hh * 65 + 65],
                         s_w[slot][:], p_av[:], kt == 0, kt == 31,
                         waits=[(SE, sg)], sem=PE)
                    av_done[slot] = PE.n

                pe_av = PE.n
                ab = (hp_i * 8 + hh * 4 + qc) % 2
                b.copy("vector", s_avc[ab][:], p_av[:],
                       waits=[(PE, pe_av), (VE, avc_free.get(ab, 0))], sem=VE)
                b.wait("tensor", [(VE, VE.n)])

                def mk_norm(ab=ab, base=base, hp_i=hp_i, q0=q0):
                    def emit():
                        b.strict = True
                        b.ts("vector", s_rs[0:1, :], s_avc[ab][64:65, :], 1e-8, None,
                             ALU.add, waits=[(VE, VE.n)], sem=VE)
                        b.recip(s_rs[0:1, :], s_rs[0:1, :], waits=[(VE, VE.n)], sem=VE)
                        b.strict = False
                        b.mm(s_ones_f[0:1, 0:64], s_rs[0:1, :], p_b[:], True, True,
                             waits=[(VE, VE.n)], sem=PE)
                        b.copy("vector", s_rsb[:], p_b[:], waits=[(PE, PE.n)], sem=VE)
                        b.tt("vector",
                             s_at[base:base + 64,
                                  hp_i * NOWN + q0: hp_i * NOWN + q0 + CW],
                             s_avc[ab][0:64, :], s_rsb[:], ALU.mult,
                             waits=[(VE, VE.n)], sem=VE)
                        avc_free[ab] = VE.n
                    return emit
                mk_norm()()
        pe_use[buf] = PE.n
    v_at = VE.n
    if "at_d" in T:
        b.dma("sync", T["at_d"][:, :, :].rearrange("c p w -> p c w"),
              s_at[:].rearrange("p (c w) -> p c w", c=8), OUT, waits=[(VE, v_at)])

    for oc in range(8):
        lx = b.dma("sync", s_x0[:].rearrange("p (c w) -> p c w", c=4),
                   T["xT"][oc][:, 0:4 * CW].rearrange("p (c w) -> p c w", c=4), LD,
                   waits=[(VE, VE.n)])
        for dc in range(8):
            for ncx in range(4):
                b.mm(s_owT[:, dc * D + oc * 128: dc * D + oc * 128 + 128],
                     s_at[:, dc * NOWN + ncx * CW: dc * NOWN + ncx * CW + CW],
                     p_o[ncx][:], dc == 0, dc == 7,
                     waits=[(VE, v_at), (LG, lg_ow), (VE, VE.n)],
                     sem=(PE if (dc == 7 and ncx == 3) else None))
        pe_o = PE.n
        for ncx in range(4):
            b.stt("vector",
                  x_acc[:, oc * NOWN + ncx * CW: oc * NOWN + ncx * CW + CW],
                  p_o[ncx][:], s_ob[:, oc:oc + 1], s_x0[:, ncx * CW:(ncx + 1) * CW],
                  ALU.add, ALU.add,
                  waits=[(PE, pe_o), (LD, lx)], sem=(VE if ncx == 3 else None))
    es.close()


def _phase_l0_moe(nc, b, T, LD, LG, PE, VE, SE, OUT,
                  s_ones_f, s_id_f, s_inv1702, x_acc, store_x1=True):
    es = ExitStack()
    al = es.enter_context
    s_xn = al(nc.sbuf_tensor("p3_xn", [128, 8 * NOWN], BF16))
    s_combT = al(nc.sbuf_tensor("p3_combT", [8, NOWN], F32))
    s_cbr = al(nc.sbuf_tensor("p3_cbr", [1, 4 * CW], BF16))
    s_invb = al(nc.sbuf_tensor("p3_invb", [1, 128], BF16))
    cb_free = {0: 0}
    s_cbb = al(nc.sbuf_tensor("p3_cbb", [128, NOWN], BF16))
    es_r = ExitStack()
    alr = es_r.enter_context
    s_rw = alr(nc.sbuf_tensor("p3_rw", [128, 8 * 8], BF16))
    s_rbb = alr(nc.sbuf_tensor("p3_rbb", [128, 8], F32))
    s_rbr = alr(nc.sbuf_tensor("p3_rbr", [1, 8], F32))
    s_lg = alr(nc.sbuf_tensor("p3_lg", [128, 16 * 8], F32))
    s_l2 = alr(nc.sbuf_tensor("p3_l2", [128, 2 * 8], F32))
    s_msk = alr(nc.sbuf_tensor("p3_msk", [128, 8], F32))
    s_mrow = alr(nc.sbuf_tensor("p3_mrow", [128, 4], F32))
    s_mr_d = alr(nc.sbuf_tensor("p3_mr_d", [128, 64], F32)) if "mr_d" in T else None
    s_comb = alr(nc.sbuf_tensor("p3_comb", [128, 16 * 8], F32))
    s_dnb = alr(nc.sbuf_tensor("p3_dnb", [8, 1024], F32))

    p_gu = [al(nc.psum_tensor(f"p3_gu_{i}", [128, CW], F32)) for i in range(8)]
    p_d4 = [p_gu[i] for i in range(4)]
    p_d = [p_gu[0], p_gu[1]]
    p_cb2 = p_gu[4]
    p_r = p_gu[6]
    p_cT = p_gu[7]

    lxn = b.dma("sync", s_xn[:].rearrange("p (c w) -> p c w", c=8),
                T["xn0"][:, :, :].rearrange("c p w -> p c w"), LD)
    for dc in range(8):
        b.dma("gpsimd", s_rw[:, dc * 8:(dc + 1) * 8], T["rwT"][dc], LG)
    ldnb = b.dma("sync", s_dnb[:], T["dnb"], LD)
    lg0 = LG.n
    lrb = b.dma("sync", s_rbr[:], T["rb_r"], LD)
    b.mm(s_ones_f[0:1, :], s_rbr[0:1, :], p_r[:, 0:8], True, True,
         waits=[(LD, lrb), (VE, VE.n)], sem=PE)
    b.copy("vector", s_rbb[:], p_r[:, 0:8], waits=[(PE, PE.n)], sem=VE)

    for nt in range(16):
        for dc in range(8):
            b.mm(s_xn[:, dc * NOWN + nt * 128: dc * NOWN + nt * 128 + 128],
                 s_rw[:, dc * 8:(dc + 1) * 8], p_r[:, 0:8], dc == 0, dc == 7,
                 waits=[(LD, lxn), (LG, lg0), (VE, VE.n)], sem=(PE if dc == 7 else None))
        lgs = s_lg[:, nt * 8:(nt + 1) * 8]
        b.tt("vector", lgs, p_r[:, 0:8], s_rbb[:], ALU.add, waits=[(PE, PE.n)], sem=VE)
        la = s_l2[:, 0:8]
        lb = s_l2[:, 8:16]
        b.strict = True
        b.copy("vector", la, lgs, waits=[(VE, VE.n)], sem=VE)
        b.red("vector", s_mrow[:, 0:1], la, ALU.max, negate=True,
              waits=[(VE, VE.n)], sem=VE)
        cur = la
        for it in range(3):
            nxt = lb if cur is la else la
            b.red("vector", s_mrow[:, 1:2], cur, ALU.max, waits=[(VE, VE.n)], sem=VE)
            b.ts("vector", s_msk[:], cur, s_mrow[:, 1:2], None, ALU.is_ge,
                 waits=[(VE, VE.n)], sem=VE)
            b.stt("vector", nxt, s_msk[:], -1e30, cur, ALU.mult, ALU.add,
                  waits=[(VE, VE.n)], sem=VE)
            cur = nxt
        b.red("vector", s_mrow[:, 2:3], cur, ALU.max, waits=[(VE, VE.n)], sem=VE)
        b.ts("vector", s_msk[:], lgs, s_mrow[:, 2:3], None, ALU.is_ge,
             waits=[(VE, VE.n)], sem=VE)
        b.act("scalar", cur, lgs, AF.Exp, bias=s_mrow[:, 0:1], waits=[(VE, VE.n)], sem=SE)
        b.tt("vector", cur, cur, s_msk[:], ALU.mult, waits=[(SE, SE.n), (VE, VE.n)], sem=VE)
        b.red("vector", s_mrow[:, 3:4], cur, ALU.add, waits=[(VE, VE.n)], sem=VE)
        b.recip(s_mrow[:, 2:3], s_mrow[:, 3:4], waits=[(VE, VE.n)], sem=VE)
        b.ts("vector", s_comb[:, nt * 8:(nt + 1) * 8], cur, s_mrow[:, 2:3], None,
             ALU.mult, waits=[(VE, VE.n)], sem=VE)
        b.strict = False
        if "mr_d" in T:
            b.copy("vector", s_mr_d[:, nt * 4:(nt + 1) * 4], s_mrow[:], sem=VE)
    v_comb = VE.n
    if "comb_d" in T:
        b.dma("sync", T["comb_d"][:, :], s_comb[:], OUT, waits=[(VE, v_comb)])
        b.dma("sync", T["lg_d"][:, :], s_lg[:], OUT, waits=[(VE, v_comb)])
        b.dma("sync", T["mr_d"][:, :], s_mr_d[:], OUT, waits=[(VE, v_comb)])
    for grp in range(4):
        for j in range(4):
            nt = grp * 4 + j
            b.trans(p_cT[0:8, j * 128:(j + 1) * 128], s_comb[:, nt * 8:(nt + 1) * 8],
                    s_id_f[:], waits=[(VE, v_comb), (VE, VE.n)],
                    sem=(PE if j == 3 else None))
        b.copy("vector", s_combT[:, grp * CW:(grp + 1) * CW], p_cT[0:8, :],
               waits=[(PE, PE.n)], sem=VE)
    v_cT = VE.n
    if "combT_d" in T:
        b.dma("sync", T["combT_d"][:, :], s_combT[:], OUT, waits=[(VE, v_cT)])

    for dc in range(8):
        for ncx in range(4):
            b.mm(s_dnb[:, dc * 128:(dc + 1) * 128],
                 s_combT[:, ncx * CW:(ncx + 1) * CW],
                 p_d[0][:], True, True,
                 waits=[(VE, v_cT), (LD, ldnb), (VE, VE.n)], sem=PE)
            b.tt("vector", x_acc[:, dc * NOWN + ncx * CW: dc * NOWN + ncx * CW + CW],
                 x_acc[:, dc * NOWN + ncx * CW: dc * NOWN + ncx * CW + CW],
                 p_d[0][:], ALU.add, waits=[(PE, PE.n)], sem=VE)

    b.copy("vector", s_invb[:], s_inv1702[:], waits=[(LD, LD.n)], sem=VE)
    es_r.close()
    s_gu = [al(nc.sbuf_tensor(f"p3_gu{i}", [128, 8 * 128], BF16)) for i in range(2)]
    s_gu2 = [al(nc.sbuf_tensor(f"p3_gu2{i}", [128, 8 * 128], BF16)) for i in range(2)]
    s_dn = [al(nc.sbuf_tensor(f"p3_dn{i}", [128, 16 * 128], BF16)) for i in range(2)]
    s_act = al(nc.sbuf_tensor("p3_act", [128, 16 * NOWN], BF16))
    s_sil4 = al(nc.sbuf_tensor("p3_sil4", [128, 4 * CW], BF16))
    s_gp4 = al(nc.sbuf_tensor("p3_gp4", [128, 4 * CW], BF16))
    s_up = al(nc.sbuf_tensor("p3_up", [128, CW], F32))
    s_gub = al(nc.sbuf_tensor("p3_gub", [128, 32], F32))
    pe_use = {0: 0, 1: 0}
    pe_use_dn = {0: 0, 1: 0}
    gfree = {i: VE.n for i in range(4)}
    ufree = {i: VE.n for i in range(4)}
    dfree = {i: VE.n for i in range(4)}
    cb_free[0] = VE.n
    for e in range(8):
        lcbs = []
        for ncx in range(4):
            lcbs.append(b.dma("gpsimd", s_cbr[:, ncx * CW:(ncx + 1) * CW],
                              s_combT[e:e + 1, ncx * CW:(ncx + 1) * CW],
                              LG, waits=[(VE, v_cT), (PE, PE.n)]))
        lgub = b.dma("sync", s_gub[:], T["gub_c"][e], LD, waits=[(VE, VE.n)])
        v_cb = None
        for ic in range(16):
            wb = ic % 2
            lwg = b.dma("gpsimd", s_gu[wb][:].rearrange("p (c f) -> p c f", c=8),
                        T["guT"][e][:, :, ic * 128:(ic + 1) * 128].rearrange("c p f -> p c f"), LG,
                        waits=[(PE, pe_use[wb])])
            lwu = b.dma("gpsimd", s_gu2[wb][:].rearrange("p (c f) -> p c f", c=8),
                        T["guT"][e][:, :, 2048 + ic * 128: 2048 + ic * 128 + 128].rearrange("c p f -> p c f"), LG)
            for dc in range(8):
                for ncx in range(4):
                    b.mm(s_gu[wb][:, dc * 128:(dc + 1) * 128],
                         s_xn[:, dc * NOWN + ncx * CW: dc * NOWN + ncx * CW + CW],
                         p_gu[ncx][:], dc == 0, dc == 7,
                         waits=[(LG, lwg), (VE, max(gfree[ncx], dfree[ncx]))],
                         sem=(PE if (dc == 7 and ncx == 3) else None))
            pe_g = PE.n
            if ic == 0:
                for ncx in range(4):
                    b.mm(s_invb[0:1, :], s_cbr[0:1, ncx * CW:(ncx + 1) * CW],
                         p_cb2[:], True, True,
                         waits=[(LG, lcbs[ncx]), (VE, cb_free[0]), (VE, ufree[0])],
                         sem=PE)
                    b.copy("vector", s_cbb[:, ncx * CW:(ncx + 1) * CW], p_cb2[:],
                           waits=[(PE, PE.n)], sem=VE)
                    cb_free[0] = VE.n
                v_cb = VE.n
            for ncx in range(4):
                b.ts("vector", s_gp4[:, ncx * CW:(ncx + 1) * CW], p_gu[ncx][:],
                     s_gub[:, ic:ic + 1], 7.0, ALU.add, ALU.min,
                     waits=[(PE, pe_g), (LD, lgub)], sem=VE)
                gfree[ncx] = VE.n
            b.act("scalar", s_sil4[:], s_gp4[:], AF.Silu, scale=1.702,
                  waits=[(VE, VE.n)], sem=SE)
            for dc in range(8):
                for ncx in range(4):
                    uw = ufree[ncx] if not (ic == 0 and ncx == 0) else max(ufree[0], v_cb)
                    b.mm(s_gu2[wb][:, dc * 128:(dc + 1) * 128],
                         s_xn[:, dc * NOWN + ncx * CW: dc * NOWN + ncx * CW + CW],
                         p_gu[4 + ncx][:], dc == 0, dc == 7,
                         waits=[(LG, lwu), (VE, uw)],
                         sem=(PE if (dc == 7 and ncx == 3) else None))
            pe_u = PE.n
            pe_use[wb] = PE.n
            for ncx in range(4):
                b.ts("vector", s_up[:], p_gu[4 + ncx][:], s_gub[:, 16 + ic:17 + ic],
                     -7.0, ALU.add, ALU.max, waits=[(PE, pe_u), (LD, lgub)], sem=VE)
                ufree[ncx] = VE.n
                b.ts("vector", s_up[:], s_up[:], 7.0, 1.0, ALU.min, ALU.add, sem=None)
                b.tt("vector", s_up[:], s_up[:], s_cbb[:, ncx * CW:(ncx + 1) * CW],
                     ALU.mult, waits=[(VE, v_cb)], sem=None)
                b.tt("vector", s_act[:, ic * NOWN + ncx * CW: ic * NOWN + ncx * CW + CW],
                     s_up[:], s_sil4[:, ncx * CW:(ncx + 1) * CW], ALU.mult,
                     waits=[(SE, SE.n)], sem=VE)
        v_act = VE.n
        for dc in range(8):
            wb = dc % 2
            ldn = b.dma("gpsimd", s_dn[wb][:].rearrange("p (c f) -> p c f", c=16),
                        T["dnT"][e][:, :, dc * 128:(dc + 1) * 128].rearrange("c p f -> p c f"), LG,
                        waits=[(PE, pe_use_dn[wb])])
            for ic in range(16):
                for ncx in range(4):
                    b.mm(s_dn[wb][:, ic * 128:(ic + 1) * 128],
                         s_act[:, ic * NOWN + ncx * CW: ic * NOWN + ncx * CW + CW],
                         p_d4[ncx][:], ic == 0, ic == 15,
                         waits=[(LG, ldn), (VE, v_act), (VE, dfree[ncx])],
                         sem=(PE if (ic == 15 and ncx == 3) else None))
            pe_use_dn[wb] = PE.n
            for ncx in range(4):
                b.tt("vector", x_acc[:, dc * NOWN + ncx * CW: dc * NOWN + ncx * CW + CW],
                     x_acc[:, dc * NOWN + ncx * CW: dc * NOWN + ncx * CW + CW],
                     p_d4[ncx][:], ALU.add, waits=[(PE, PE.n)], sem=VE)
                dfree[ncx] = VE.n

    if store_x1:
        b.dma("sync", T["x1"][:, :, :].rearrange("c p w -> p c w"),
              x_acc[:].rearrange("p (c w) -> p c w", c=8), OUT,
              waits=[(VE, VE.n)])
    es.close()



def _phase_l1_ffn(nc, b, T, LD, LG, PE, VE, SE, OUT, s_xn1):
    es = ExitStack()
    al = es.enter_context
    if s_xn1 is None:
        s_xn1 = al(nc.sbuf_tensor("f4_xn1", [128, 8 * NOWN], BF16))
        b.dma("sync", s_xn1[:].rearrange("p (c w) -> p c w", c=8),
              T["xn1_d"][:, :, :].rearrange("c p w -> p c w"), LD,
              waits=[(OUT, OUT.n)])
    s_sc = al(nc.sbuf_tensor("b4_sc", [128, 32 * 4 + 8], F32))
    s_h0 = al(nc.sbuf_tensor("b4_h0", [128, 32 * 1024], BF16))
    s_h1 = al(nc.sbuf_tensor("b4_h1", [128, 32 * 1024], BF16))
    s_w = [al(nc.sbuf_tensor(f"b4_w{i}", [128, 32 * 128], BF16)) for i in range(2)]
    s_t = al(nc.sbuf_tensor("b4_t", [128, CW], F32))
    s_u = al(nc.sbuf_tensor("b4_u", [128, CW], F32))
    s_sil = al(nc.sbuf_tensor("b4_sil", [128, CW], F32))
    s_xa = al(nc.sbuf_tensor("b4_xa", [128, CW], F32))
    s_x2 = al(nc.sbuf_tensor("b4_x2", [128, CW], F32))
    p_a = [al(nc.psum_tensor(f"b4_a{i}", [128, CW], F32)) for i in range(2)]
    p_b2 = [al(nc.psum_tensor(f"b4_b{i}", [128, CW], F32)) for i in range(2)]

    l_sc = b.dma("sync", s_sc[:, 0:32], T["s1_c"], LD)
    b.dma("sync", s_sc[:, 32:64], T["s2_c"], LD)
    b.dma("sync", s_sc[:, 64:96], T["sh1_c"], LD)
    b.dma("sync", s_sc[:, 96:128], T["sh2_c"], LD)
    l_sc = b.dma("sync", s_sc[:, 128:136], T["s3_c"], LD)
    s1c = s_sc[:, 0:32]; s2c = s_sc[:, 32:64]
    sh1c = s_sc[:, 64:96]; sh2c = s_sc[:, 96:128]; s3c = s_sc[:, 128:136]

    pe_use = {0: 0, 1: 0}
    for nh in range(2):
        h0 = nh * 1024
        for jc in range(32):
            wb = jc % 2
            lw1 = b.dma("gpsimd", s_w[wb][:, 0:1024].rearrange("p (c f) -> p c f", c=8),
                        T["w1T"][:, :, jc * 128:(jc + 1) * 128].rearrange("c p f -> p c f"),
                        LG, waits=[(PE, pe_use[wb])])
            lw2 = b.dma("gpsimd", s_w[wb][:, 1024:2048].rearrange("p (c f) -> p c f", c=8),
                        T["w2T"][:, :, jc * 128:(jc + 1) * 128].rearrange("c p f -> p c f"),
                        LG)
            for dc in range(8):
                for nc2 in range(2):
                    b.mm(s_w[wb][:, dc * 128:(dc + 1) * 128],
                         s_xn1[:, dc * NOWN + h0 + nc2 * CW: dc * NOWN + h0 + nc2 * CW + CW],
                         p_a[nc2][:], dc == 0, dc == 7,
                         waits=[(LG, lw1), (LD, LD.n), (VE, VE.n)],
                         sem=(PE if (dc == 7 and nc2 == 1) else None))
            for dc in range(8):
                for nc2 in range(2):
                    b.mm(s_w[wb][:, 1024 + dc * 128: 1024 + dc * 128 + 128],
                         s_xn1[:, dc * NOWN + h0 + nc2 * CW: dc * NOWN + h0 + nc2 * CW + CW],
                         p_b2[nc2][:], dc == 0, dc == 7,
                         waits=[(LG, lw2), (VE, VE.n)],
                         sem=(PE if (dc == 7 and nc2 == 1) else None))
            pe_w = PE.n
            pe_use[wb] = PE.n
            for nc2 in range(2):
                b.ts("vector", s_t[:], p_a[nc2][:], s1c[:, jc:jc + 1], None, ALU.mult,
                     waits=[(PE, pe_w), (LD, l_sc)], sem=VE)
                b.act("scalar", s_sil[:], s_t[:], AF.Silu, waits=[(VE, VE.n)], sem=SE)
                b.ts("vector", s_u[:], p_b2[nc2][:], s2c[:, jc:jc + 1], None, ALU.mult,
                     sem=None)
                b.tt("vector", s_h0[:, jc * 1024 + nc2 * CW: jc * 1024 + nc2 * CW + CW],
                     s_u[:], s_sil[:], ALU.mult, waits=[(SE, SE.n)], sem=VE)
        def hidden_layer(wname, scl, src_t, dst, se_src=0):
            for oc in range(32):
                wb = oc % 2
                lw = b.dma("gpsimd", s_w[wb][:, 0:4096].rearrange("p (c f) -> p c f", c=32),
                           T[wname][:, :, oc * 128:(oc + 1) * 128].rearrange("c p f -> p c f"),
                           LG, waits=[(PE, pe_use[wb])])
                for ic in range(32):
                    for nc2 in range(2):
                        b.mm(s_w[wb][:, ic * 128:(ic + 1) * 128],
                             src_t[:, ic * 1024 + nc2 * CW: ic * 1024 + nc2 * CW + CW],
                             p_a[nc2][:], ic == 0, ic == 31,
                             waits=[(LG, lw), (VE, VE.n), (SE, se_src)],
                             sem=(PE if (ic == 31 and nc2 == 1) else None))
                pe_h = PE.n
                pe_use[wb] = PE.n
                for nc2 in range(2):
                    b.ts("vector", s_t[:], p_a[nc2][:], scl[:, oc:oc + 1], None, ALU.mult,
                         waits=[(PE, pe_h)], sem=VE)
                    b.act("scalar", dst[:, oc * 1024 + nc2 * CW: oc * 1024 + nc2 * CW + CW],
                          s_t[:], AF.Gelu, waits=[(VE, VE.n)], sem=SE)
        hidden_layer("h1T", sh1c, s_h0, s_h1)
        se_h1 = SE.n
        hidden_layer("h2T", sh2c, s_h1, s_h0, se_src=se_h1)
        se_h2 = SE.n
        for dc in range(8):
            wb = dc % 2
            lw = b.dma("gpsimd", s_w[wb][:, 0:4096].rearrange("p (c f) -> p c f", c=32),
                       T["w3T"][:, :, dc * 128:(dc + 1) * 128].rearrange("c p f -> p c f"),
                       LG, waits=[(PE, pe_use[wb])])
            for ic in range(32):
                for nc2 in range(2):
                    b.mm(s_w[wb][:, ic * 128:(ic + 1) * 128],
                         s_h0[:, ic * 1024 + nc2 * CW: ic * 1024 + nc2 * CW + CW],
                         p_a[nc2][:], ic == 0, ic == 31,
                         waits=[(LG, lw), (VE, VE.n), (SE, se_h2)],
                         sem=(PE if (ic == 31 and nc2 == 1) else None))
            pe_3 = PE.n
            pe_use[wb] = PE.n
            for nc2 in range(2):
                nn = h0 + nc2 * CW
                lx = b.dma("sync", s_xa[:], T["xacc_d"][dc][:, nn:nn + CW], LD,
                           waits=[(VE, VE.n)])
                b.stt("vector", s_x2[:], p_a[nc2][:], s3c[:, dc:dc + 1], s_xa[:],
                      ALU.mult, ALU.add,
                      waits=[(PE, pe_3), (LD, lx), (LD, l_sc), (OUT, OUT.n)], sem=VE)
                b.dma("sync", T["x2"][dc][:, nn:nn + CW], s_x2[:], OUT,
                      waits=[(VE, VE.n)])
        es.close()




def build_fused(dbg=False):
    nc = bass.Bass("TRN2", target_bir_lowering=False, debug=False)
    b = Builder(nc)
    T = {}

    def din(name, shape, dtype=F32):
        T[name] = nc.dram_tensor(name, shape, dtype, kind="ExternalInput").ap()

    def dscr(name, shape, dtype):
        kind = "ExternalOutput" if dbg else "Internal"
        T[name] = nc.dram_tensor(name, shape, dtype, kind=kind).ap()

    din("xT", [8, 128, L])
    for w in ("qwT", "kwT", "vwT", "owT"):
        din(w, [8, 128, D])
    din("qb_r", [1, D]); din("kb_r", [1, D]); din("vb_r", [1, D])
    din("ob_c", [128, 8]); din("qn_c", [128, 8]); din("kn_c", [128, 8])
    din("cosT", [L, HD]); din("sinT", [L, HD])
    din("ones", [128, 128]); din("iden", [128, 128]); din("consts", [128, 8])
    din("rwT", [8, 128, 8]); din("rb_r", [1, 8])
    din("guT", [8, 8, 128, 4096]); din("gub_c", [8, 128, 32])
    din("dnT", [8, 16, 128, 1024]); din("dnb", [8, 1024])
    din("inv1702", [1, 128])
    din("W1", [8, 128, 3 * D])
    din("w1T", [8, 128, 4096]); din("w2T", [8, 128, 4096])
    din("h1T", [32, 128, 4096]); din("h2T", [32, 128, 4096])
    din("w3T", [32, 128, 1024])
    din("s1_c", [128, 32]); din("s2_c", [128, 32])
    din("sh1_c", [128, 32]); din("sh2_c", [128, 32]); din("s3_c", [128, 8])
    T["x2"] = nc.dram_tensor("x2", [8, 128, NOWN], F32, kind="ExternalOutput").ap()
    dscr("q_fm", [8, 128, NOWN], BF16)
    dscr("k_fm", [8, 128, L], BF16)
    dscr("v_tm", [32, 128, H * 65], BF16)
    dscr("xn0", [8, 128, NOWN], BF16)
    dscr("k1_tm", [16, 128, D], BF16)
    dscr("v1_tm", [16, 128, D + 8], BF16)
    dscr("xacc_d", [8, 128, NOWN], F32)
    dscr("xn1_d", [8, 128, NOWN], BF16)
    if dbg:
        dscr("kvp_d", [8, 128, D + 8], F32)
        dscr("kvf_d", [8, 128, D + 8], BF16)
    cc_in_t = nc.dram_tensor("cc_in", [1024, 1032], F32, kind="Internal")
    cc_out_t = nc.dram_tensor("cc_out", [1024, 1032], F32, kind="Internal")
    T["cc_in"] = cc_in_t.ap()
    T["cc_out"] = cc_out_t.ap()
    T["cc_in_opt"] = cc_in_t.ap().opt()
    T["cc_out_opt"] = cc_out_t.ap().opt()

    LD = b.sem("LD"); LG = b.sem("LG"); PE = b.sem("PE")
    VE = b.sem("VE"); SE = b.sem("SE"); OUT = b.sem("OUT"); CC = b.sem("CC")

    al = b.es.enter_context
    s_ones_f = al(nc.sbuf_tensor("s_ones_f", [128, 128], F32))
    s_ones_b = al(nc.sbuf_tensor("s_ones_b", [128, 128], BF16))
    s_id_b = al(nc.sbuf_tensor("s_id_b", [128, 128], BF16))
    s_id_f = al(nc.sbuf_tensor("s_id_f", [128, 128], F32))
    s_consts = al(nc.sbuf_tensor("s_consts", [128, 8], F32))
    s_qn = al(nc.sbuf_tensor("s_qn", [128, 8], F32))
    s_kn = al(nc.sbuf_tensor("s_kn", [128, 8], F32))
    s_ob = al(nc.sbuf_tensor("s_ob", [128, 8], F32))
    s_inv1702 = al(nc.sbuf_tensor("s_inv1702", [1, 128], F32))

    b.dma("sync", s_ones_f[:], T["ones"], LD)
    b.dma("sync", s_id_f[:], T["iden"], LD)
    b.dma("sync", s_consts[:], T["consts"], LD)
    b.dma("sync", s_qn[:], T["qn_c"], LD)
    b.dma("sync", s_kn[:], T["kn_c"], LD)
    b.dma("sync", s_ob[:], T["ob_c"], LD)
    b.dma("sync", s_inv1702[:], T["inv1702"], LD)
    b.dma("gpsimd", s_ones_b[:], T["ones"], LG)
    b.dma("gpsimd", s_id_b[:], T["iden"], LG)

    eps_ln = s_consts[0:1, 0:1]
    eps_rms = s_consts[0:1, 1:2]
    b.barrier()

    _phase_l0_pre(nc, b, T, LD, LG, PE, VE, SE, OUT,
                  s_ones_f, s_ones_b, s_id_b, s_qn, s_kn, eps_ln, eps_rms)
    b.barrier()
    es_x = ExitStack()
    x_acc = es_x.enter_context(nc.sbuf_tensor("x_acc", [128, 8 * NOWN], F32))
    _phase_l0_attn(nc, b, T, LD, LG, PE, VE, SE, OUT, s_ones_f, s_ob, x_acc)
    b.barrier()
    _phase_l0_moe(nc, b, T, LD, LG, PE, VE, SE, OUT,
                  s_ones_f, s_id_f, s_inv1702, x_acc, store_x1=False)
    b.barrier()

    es_q = ExitStack()
    s_q1 = es_q.enter_context(nc.sbuf_tensor("b_q1", [128, 8 * NOWN], BF16))

    _phase_l1_pre_f(nc, b, T, LD, LG, PE, VE, SE, OUT, CC,
                    s_ones_f, s_ones_b, eps_ln, None, s_q1, x_acc)
    b.barrier()
    es_attn = ExitStack()
    s_kvb = es_attn.enter_context(nc.sbuf_tensor("b_kvb", [128, 8 * (D + 8)], BF16))
    s_ksb = None
    _phase_l1_kv_f(nc, b, T, LD, LG, PE, VE, SE, OUT, CC, s_kvb, s_ksb)
    b.barrier()
    _phase_l1_attnfin_f(nc, b, T, LD, LG, PE, VE, SE, OUT,
                        s_ones_f, s_q1, s_kvb, s_ksb, x_acc)
    b.barrier()
    es_attn.close()
    es_q.close()
    es_x.close()
    _phase_l1_ffn(nc, b, T, LD, LG, PE, VE, SE, OUT, None)
    b.wait("sync", [(OUT, OUT.n)])
    b.finalize()
    return nc


def _phase_l1_pre_f(nc, b, T, LD, LG, PE, VE, SE, OUT, CC,
                    s_ones_f, s_ones_b, eps_ln, s_xn1, s_q1, x_acc):
    es = ExitStack()
    al = es.enter_context
    if s_xn1 is None:
        s_xn1 = al(nc.sbuf_tensor("f1_xn1", [128, 8 * NOWN], BF16))
        spill_xn1 = True
    else:
        spill_xn1 = False
    s_W1 = al(nc.sbuf_tensor("f1_W1", [128, 8 * 3 * D], BF16))
    s_xb = al(nc.sbuf_tensor("f1_xb", [128, 8 * CW], BF16))
    s_sq = al(nc.sbuf_tensor("f1_sq", [128, 8 * CW], BF16))
    s_t = al(nc.sbuf_tensor("f1_t", [128, CW], F32))
    s_e = al(nc.sbuf_tensor("f1_e", [128, CW], F32))
    s_rows = al(nc.sbuf_tensor("f1_rows", [1, 2 * CW], F32))
    s_ktm = al(nc.sbuf_tensor("f1_ktm", [128, D], BF16))
    s_vtm = al(nc.sbuf_tensor("f1_vtm", [128, D + 8], BF16))
    p_sA = al(nc.psum_tensor("f1_sA", [1, CW], F32))
    p_sB = al(nc.psum_tensor("f1_sB", [1, CW], F32))
    p_m = al(nc.psum_tensor("f1_m", [128, CW], F32))
    p_r = al(nc.psum_tensor("f1_r", [128, CW], F32))
    p_k = al(nc.psum_tensor("f1_k", [128, 2 * CW], F32))

    mrow = s_rows[0:1, 0:CW]
    rstd = s_rows[0:1, CW:2 * CW]
    trow = s_e[0:1, 0:CW]
    erow = rstd
    varr = trow

    for dc in range(8):
        b.dma("gpsimd", s_W1[:, dc * 3 * D:(dc + 1) * 3 * D], T["W1"][dc], LG)
    lg_w = LG.n

    for ch in range(4):
        c0 = ch * CW
        xsl = x_acc[:].rearrange("p (c n) -> p c n", c=8)[:, :, c0:c0 + CW]
        b.copy("vector", s_xb[:].rearrange("p (c w) -> p c w", c=8), xsl,
               waits=[(PE, PE.n), (SE, SE.n)], sem=VE)
        lvb = VE.n
        sv = b.act("scalar", s_sq[:], s_xb[:], AF.Square, waits=[(VE, lvb)], sem=SE)
        for dc in range(8):
            b.mm(s_ones_b[:, 0:1], s_xb[:, dc * CW:(dc + 1) * CW], p_sA[:],
                 dc == 0, dc == 7, waits=[(VE, lvb), (VE, VE.n)],
                 sem=(PE if dc == 7 else None))
        for dc in range(8):
            b.mm(s_ones_b[:, 0:1], s_sq[:, dc * CW:(dc + 1) * CW], p_sB[:],
                 dc == 0, dc == 7, waits=[(SE, sv), (VE, VE.n)],
                 sem=(PE if dc == 7 else None))
        pe_stats = PE.n
        b.ts("vector", mrow, p_sA[:], 1.0 / D, None, ALU.mult, waits=[(PE, pe_stats)], sem=VE)
        b.ts("vector", erow, p_sB[:], 1.0 / D, None, ALU.mult, waits=[(PE, pe_stats)], sem=VE)
        b.act("scalar", trow, mrow, AF.Square, waits=[(VE, VE.n)], sem=SE)
        b.tt("vector", varr, erow, trow, ALU.subtract, waits=[(SE, SE.n)], sem=VE)
        b.act("scalar", trow, varr, AF.Ln, bias=eps_ln, waits=[(VE, VE.n)], sem=SE)
        b.act("scalar", rstd, trow, AF.Exp, scale=-0.5, waits=[(SE, SE.n)], sem=SE)
        b.mm(s_ones_f[0:1, :], mrow, p_m[:, 0:CW], True, True,
             waits=[(VE, VE.n)], sem=PE)
        b.mm(s_ones_f[0:1, :], rstd, p_r[:, 0:CW], True, True,
             waits=[(SE, SE.n), (VE, VE.n)], sem=PE)
        pe_bc = PE.n
        for dc in range(8):
            sl = slice(dc * CW, (dc + 1) * CW)
            b.tt("vector", s_t[:], xsl[:, dc, :], p_m[:, 0:CW], ALU.subtract,
                 waits=[(PE, pe_bc), (PE, pe_stats), (SE, sv)], sem=None)
            b.tt("vector", s_xb[:, sl], s_t[:], p_r[:, 0:CW], ALU.mult,
                 sem=(VE if dc == 7 else None))
        v_xn = VE.n
        b.copy("vector", s_xn1[:].rearrange("p (c w) -> p c w", c=8)[:, :, c0:c0 + CW],
               s_xb[:].rearrange("p (c w) -> p c w", c=8), sem=VE)
        for oc in range(8):
            for dc in range(8):
                b.mm(s_W1[:, dc * 3 * D + oc * 128: dc * 3 * D + oc * 128 + 128],
                     s_xb[:, dc * CW:(dc + 1) * CW],
                     p_k[:, 0:CW], dc == 0, dc == 7,
                     waits=[(LG, lg_w), (VE, VE.n)],
                     sem=(PE if dc == 7 else None))
            pe_q = PE.n
            b.ts("vector", s_t[:], p_k[:, 0:CW], 0.0, None, ALU.min,
                 waits=[(PE, pe_q)], sem=VE)
            b.act("scalar", s_e[:], s_t[:], AF.Exp, waits=[(VE, VE.n)], sem=SE)
            b.stt("vector",
                  s_q1[:, oc * NOWN + c0: oc * NOWN + c0 + CW],
                  p_k[:, 0:CW], 1.0, s_e[:], ALU.add, ALU.max,
                  waits=[(SE, SE.n)], sem=VE)
        for nt in range(4):
            for (col0, dst, do_elu) in ((D, s_ktm, True), (2 * D, s_vtm, False)):
                for dc in range(8):
                    for oc in range(2):
                        b.mm(s_xb[:, dc * CW + nt * 128: dc * CW + nt * 128 + 128],
                             s_W1[:, dc * 3 * D + col0 + oc * CW: dc * 3 * D + col0 + oc * CW + CW],
                             p_k[:, oc * CW:(oc + 1) * CW], dc == 0, dc == 7,
                             waits=[(VE, v_xn), (VE, VE.n), (LG, lg_w)],
                             sem=(PE if (dc == 7 and oc == 1) else None))
                pe_p = PE.n
                if do_elu:
                    for oc2 in range(2):
                        psl = p_k[:, oc2 * CW:(oc2 + 1) * CW]
                        b.ts("vector", s_t[:], psl, 0.0, None, ALU.min,
                             waits=[(PE, pe_p)], sem=VE)
                        b.act("scalar", s_e[:], s_t[:], AF.Exp, waits=[(VE, VE.n)], sem=SE)
                        b.stt("vector", dst[:, oc2 * CW:(oc2 + 1) * CW], psl,
                              1.0, s_e[:], ALU.add, ALU.max,
                              waits=[(SE, SE.n), (OUT, OUT.n)], sem=VE)
                else:
                    b.copy("vector", dst[:, 0:D], p_k[:, 0:D],
                           waits=[(PE, pe_p), (OUT, OUT.n)], sem=VE)
                    b.memset("vector", dst[:, D:D + 8], 0.0, sem=None)
                    b.memset("vector", dst[:, D:D + 1], 1.0, sem=VE)
                v_d = VE.n
                if do_elu:
                    b.dma("sync", T["k1_tm"][ch * 4 + nt], dst[:], OUT,
                          waits=[(VE, v_d)])
                else:
                    b.dma("sync", T["v1_tm"][ch * 4 + nt], dst[:], OUT,
                          waits=[(VE, v_d)])
    if spill_xn1:
        b.dma("sync", T["xn1_d"][:, :, :].rearrange("c p w -> p c w"),
              s_xn1[:].rearrange("p (c w) -> p c w", c=8), OUT, waits=[(VE, VE.n)])
    es.close()


def _phase_l1_kv_f(nc, b, T, LD, LG, PE, VE, SE, OUT, CC, s_kvb, s_ksb):
    es = ExitStack()
    al = es.enter_context
    CA = D + 8
    s_kt = [al(nc.sbuf_tensor(f"f2_kt{i}", [128, D], BF16)) for i in range(2)]
    s_vt = [al(nc.sbuf_tensor(f"f2_vt{i}", [128, CA], BF16)) for i in range(2)]
    s_kv = al(nc.sbuf_tensor("f2_kv", [128, 8 * CA], F32))
    p_kv = [al(nc.psum_tensor(f"f2_kv{i}", [128, CW], F32)) for i in range(8)]

    for vc in range(3):
        c0 = vc * CW
        cw = CW if vc < 2 else 8
        for lt in range(16):
            bu = lt % 2
            lk = b.dma("sync", s_kt[bu][:], T["k1_tm"][lt], LD,
                       waits=[(PE, PE.n)])
            lv = b.dma("sync", s_vt[bu][:], T["v1_tm"][lt], LD)
            for dc in range(8):
                b.mm(s_kt[bu][:, dc * 128:(dc + 1) * 128],
                     s_vt[bu][:, c0:c0 + cw],
                     p_kv[dc][:, 0:cw], lt == 0, lt == 15,
                     waits=[(LD, lv), (VE, VE.n)],
                     sem=(PE if dc == 7 else None))
        for dc in range(8):
            b.copy("vector", s_kv[:, dc * CA + c0: dc * CA + c0 + cw],
                   p_kv[dc][:, 0:cw], waits=[(PE, PE.n)], sem=VE)
    b.dma("sync", T["cc_in"][:, :].rearrange("(c p) h -> p c h", p=128),
          s_kv[:].rearrange("p (c h) -> p c h", c=8), OUT, waits=[(VE, VE.n)])
    if "kvp_d" in T:
        b.dma("sync", T["kvp_d"][:, :, :].rearrange("c p h -> p c h"),
              s_kv[:].rearrange("p (c h) -> p c h", c=8), OUT, waits=[(VE, VE.n)])
    def cc_fn(e):
        return e.collective_compute(
            "AllReduce", ALU.add,
            replica_groups=[[0, 1], [2, 3], [4, 5], [6, 7]],
            ins=[T["cc_in_opt"]], outs=[T["cc_out_opt"]],
        )
    b._emit("gpsimd", cc_fn, [(OUT, OUT.n)], CC, 1)
    b.seen["gpsimd"].pop(CC, None)
    b.dma("gpsimd", s_kvb[:].rearrange("p (c h) -> p c h", c=8),
          T["cc_out"][:, :].rearrange("(c p) h -> p c h", p=128), LG,
          waits=[(CC, CC.n)])
    if "kvf_d" in T:
        b.dma("sync", T["kvf_d"][:, :, :].rearrange("c p h -> p c h"),
              s_kvb[:].rearrange("p (c h) -> p c h", c=8), OUT, waits=[(LG, LG.n)])
    es.close()


def _phase_l1_attnfin_f(nc, b, T, LD, LG, PE, VE, SE, OUT,
                        s_ones_f, s_q1, s_kvb, s_ksb, x_acc):
    es = ExitStack()
    al = es.enter_context
    CA = D + 8
    s_zr = al(nc.sbuf_tensor("f3_zr", [1, 2 * CW], F32))
    s_zb4 = al(nc.sbuf_tensor("f3_zb4", [128, 4 * CW], F32))
    s_xa = al(nc.sbuf_tensor("f3_xa", [128, CW], F32))
    p_z = al(nc.psum_tensor("f3_z", [1, CW], F32))
    p_zb = al(nc.psum_tensor("f3_zbp", [128, CW], F32))
    p_V = [al(nc.psum_tensor(f"f3_V{i}", [128, CW], F32)) for i in range(4)]

    for ncx in range(4):
        n0 = ncx * CW
        for dc in range(8):
            b.mm(s_kvb[:, dc * CA + D: dc * CA + D + 1],
                 s_q1[:, dc * NOWN + n0: dc * NOWN + n0 + CW],
                 p_z[:], dc == 0, dc == 7, waits=[(LG, LG.n), (VE, VE.n)],
                 sem=(PE if dc == 7 else None))
        b.strict = True
        b.ts("vector", s_zr[0:1, 0:CW], p_z[:], 1e-6, None, ALU.add,
             waits=[(PE, PE.n), (VE, VE.n)], sem=VE)
        b.recip(s_zr[0:1, CW:2 * CW], s_zr[0:1, 0:CW], waits=[(VE, VE.n)], sem=VE)
        b.strict = False
        b.mm(s_ones_f[0:1, :], s_zr[0:1, CW:2 * CW], p_zb[:], True, True,
             waits=[(VE, VE.n)], sem=PE)
        b.copy("vector", s_zb4[:, n0:n0 + CW], p_zb[:], waits=[(PE, PE.n)], sem=VE)
    v_zb = VE.n
    for hc in range(8):
        for dc in range(8):
            for ncx in range(4):
                b.mm(s_kvb[:, dc * CA + hc * 128: dc * CA + hc * 128 + 128],
                     s_q1[:, dc * NOWN + ncx * CW: dc * NOWN + ncx * CW + CW],
                     p_V[ncx][:], dc == 0, dc == 7,
                     waits=[(VE, VE.n), (LG, LG.n)],
                     sem=(PE if (dc == 7 and ncx == 3) else None))
        pe_V = PE.n
        for ncx in range(4):
            n0 = ncx * CW
            b.tt("vector", s_xa[:], p_V[ncx][:], s_zb4[:, n0:n0 + CW], ALU.mult,
                 waits=[(PE, pe_V), (VE, v_zb), (OUT, OUT.n)], sem=None)
            acc = x_acc[:, hc * NOWN + n0: hc * NOWN + n0 + CW]
            b.tt("vector", acc, acc, s_xa[:], ALU.add, sem=(VE if ncx == 3 else None))
    b.dma("sync", T["xacc_d"][:, :, :].rearrange("c p w -> p c w"),
          x_acc[:].rearrange("p (c w) -> p c w", c=8), OUT, waits=[(VE, VE.n)])
    es.close()


def _in_map_A(hp, core):
    m = {"xT": hp["xT_core"][core],
         "cosT": hp["cosT_core"][core],
         "sinT": hp["sinT_core"][core]}
    for k in ("qwT", "kwT", "vwT", "owT", "qb_r", "kb_r", "vb_r", "ob_c", "qn_c",
              "kn_c", "ones", "iden", "consts", "rwT", "rb_r",
              "guT", "gub_c", "dnT", "dnb", "inv1702"):
        m[k] = hp[k]
    return m


_NC_F = None


def _get_fused():
    global _NC_F
    if _NC_F is None:
        _NC_F = build_fused(dbg=False)
    return _NC_F


def _in_map_F(hp, core):
    m = {"xT": hp["xT_core"][core],
         "cosT": hp["cosT_core"][core],
         "sinT": hp["sinT_core"][core]}
    for k in ("qwT", "kwT", "vwT", "owT", "qb_r", "kb_r", "vb_r", "ob_c", "qn_c",
              "kn_c", "ones", "iden", "consts", "rwT", "rb_r",
              "guT", "gub_c", "dnT", "dnb", "inv1702",
              "W1", "w1T", "w2T", "h1T", "h2T", "w3T", "s1_c", "s2_c",
              "sh1_c", "sh2_c", "s3_c"):
        m[k] = hp[k]
    return m


def kernel(**inputs):
    hp = host_prep(inputs)
    nc_f = _get_fused()
    in_maps = [_in_map_F(hp, c) for c in range(NCORE)]
    res = run_bass_kernel_spmd(nc_f, in_maps, core_ids=list(range(NCORE)))
    out = np.empty((B, L, D), dtype=np.float32)
    for c in range(NCORE):
        bi, hf = c // 2, c % 2
        x2 = res.results[c]["x2"].reshape(D, NOWN)
        out[bi, hf * NOWN:(hf + 1) * NOWN, :] = x2.T
    return out


